# version 21
# baseline (speedup 1.0000x reference)
import numpy as np
import concourse.bass as bass
import concourse.mybir as mybir
from concourse.bass_utils import run_bass_kernel_spmd

F32 = mybir.dt.float32
BF16 = mybir.dt.bfloat16
AL = mybir.AluOpType
AF = mybir.ActivationFunctionType
AX = mybir.AxisListType

D = 1024
C = 8
NT = 2048
NK = 4096
TG = 256
NGK = NK // TG
NGT = NT // TG
NQT = NT // 128
QB = 4
NB = NQT // QB
S = 16
NCH = 32
BIGB = 30000.0
BIG2 = 3.0e6
SCL = 1.0 / 32.0


def attn_steps():
    steps = []
    for bq in range(NB):
        nslot = 20 + 4 * bq
        for j in range(nslot):
            for ti in range(QB):
                i = bq * QB + ti
                if j <= 16 + i:
                    steps.append((bq, i, j))
    return steps


ATT_STEPS = attn_steps()
NSTEPS = len(ATT_STEPS)


class RCmd:
    __slots__ = ("name", "a", "k", "incs")

    def __init__(self, name, a, k):
        self.name = name
        self.a = a
        self.k = k
        self.incs = []

    def then_inc(self, sem_h, n):
        self.incs.append((sem_h, n))
        return self


class REng:

    def __init__(self):
        self.cmds = []

    def __getattr__(self, name):
        def call(*a, **k):
            c = RCmd(name, a, k)
            self.cmds.append(c)
            return c
        return call

    def replay(self, eng):
        for c in self.cmds:
            ins = getattr(eng, c.name)(*c.a, **c.k)
            for (sh, n) in c.incs:
                ins.then_inc(sh, n)


class Sem:
    def __init__(self, nc, name):
        self.h = nc.alloc_semaphore(name)
        self.v = 0

    def inc(self, ins, k):
        ins.then_inc(self.h, k)
        self.v += k
        return self.v

    def wait(self, eng):
        eng.wait_ge(self.h, self.v)


class Ctx:
    def __init__(self, nc):
        self.nc = nc
        self.e = {}
        self.ldc = Sem(nc, "s_ldc")
        self.lds = [Sem(nc, f"s_lds{i}") for i in range(2)]
        self.ldw = [Sem(nc, f"s_ldw{i}") for i in range(2)]
        self.ldk = [Sem(nc, f"s_ldk{i}") for i in range(2)]
        self.ldq = Sem(nc, "s_ldq")
        self.sps = [Sem(nc, f"s_sps{i}") for i in range(2)]
        self.spv = [Sem(nc, f"s_spv{i}") for i in range(2)]
        self.spo = Sem(nc, "s_spo")
        self.pin = Sem(nc, "s_pin")
        self.pot = Sem(nc, "s_pot")
        self.pe = Sem(nc, "s_pe")
        self.dv = Sem(nc, "s_dv")
        self.ac = Sem(nc, "s_ac")
        self.cc = Sem(nc, "s_cc")
        self.dq = Sem(nc, "s_dq")
        self.sem_of = {}
        for s in [self.ldc, self.ldq, self.spo, self.pin, self.pot, self.pe,
                  self.dv, self.ac,
                  self.cc, self.dq] + self.lds + self.ldw + self.ldk +                  self.sps + self.spv:
            self.sem_of[id(s.h)] = s
        self.AW = 53200
        self.arena = nc.alloc_sbuf_tensor("arena", [128, self.AW], F32)
        self.off = 0
        self.marks = []
        self.pb = [nc.alloc_psum_tensor(f"pb{i}", [128, 512], F32)
                   for i in range(8)]
        self.bank_free = [None] * 8

    def al(self, words):
        assert self.off + words <= self.AW, (self.off, words)
        ap = self.arena.ap()[:, self.off:self.off + words]
        self.off += words
        return ap

    def mark(self):
        self.marks.append(self.off)

    def release(self):
        self.off = self.marks.pop()

    def use_bank(self, i):
        bf = self.bank_free[i]
        if bf is not None:
            self.e["t"].wait_ge(bf[0].h, bf[1])
            self.bank_free[i] = None
        return self.pb[i]

    def bank_consumed(self, i, sem, val):
        self.bank_free[i] = (sem, val)

    def dma(self, eng, sem, out, in_, **kw):
        eng.wait_ge(sem.h, sem.v)
        ins = eng.dma_start(out=out, in_=in_, **kw)
        return sem.inc(ins, 16)

    def ve_edge(self, ins):
        if getattr(ins, "incs", None):
            sh, _ = ins.incs[0]
            sem = self.sem_of.get(id(sh))
            assert sem is not None, "ve_edge: unknown sem on instruction"
            self.e["v"].wait_ge(sem.h, sem.v)
        else:
            v = self.dq.inc(ins, 1)
            self.e["v"].wait_ge(self.dq.h, v)



def r3(ap, c=C):
    return ap.rearrange("p (c t) -> p c t", c=c)


def bf(ap):
    return ap.bitcast(BF16)


def emit_split(cx, x3f, xh3, xl3):
    ve = cx.e["v"]
    ins = ve.tensor_copy(xh3, x3f)
    cx.ve_edge(ins)
    ins = ve.tensor_tensor(xl3, x3f, xh3, AL.subtract)
    m = cx.dv.inc(ins, 1)
    return m


def emit_proj3(cx, T, wp, x_pair, out_ec, bi, nec=C, bias=None, act=AF.Copy,
               e_tot=D):
    te, se = cx.e["t"], cx.e["s"]
    xh3, xl3 = x_pair
    w4 = wp.rearrange("p (two dc e) -> p two dc e", two=2, dc=C)
    m_ev = None
    for ec in range(nec):
        bank = cx.use_bank(bi)
        first = True
        for dc in range(C):
            wh = w4[:, 0, dc, 128 * ec:128 * (ec + 1)]
            wl = w4[:, 1, dc, 128 * ec:128 * (ec + 1)]
            for (wt, xt) in ((wh, xh3), (wh, xl3), (wl, xh3)):
                last = (dc == C - 1) and (wt is wl)
                ins = te.matmul(bank[:, 0:T], wt, xt[:, dc, :],
                                start=first, stop=last)
                first = False
        m_pe = cx.pe.inc(ins, 1)
        b = bias[:, ec:ec + 1] if bias is not None else 0.0
        if act == AF.Copy:
            ve = cx.e["v"]
            ve.wait_ge(cx.pe.h, m_pe)
            if bias is not None:
                ins = ve.tensor_scalar_add(out_ec(ec), bank[:, 0:T], b)
            else:
                ins = ve.tensor_copy(out_ec(ec), bank[:, 0:T])
            m_ev = cx.dv.inc(ins, 1)
            cx.bank_consumed(bi, cx.dv, m_ev)
        else:
            se.wait_ge(cx.pe.h, m_pe)
            ins = se.activation(out_ec(ec), bank[:, 0:T], act, bias=b)
            m_ev = cx.ac.inc(ins, 1)
            cx.bank_consumed(bi, cx.ac, m_ev)
    return m_ev



def emit_ln(cx, T, x_src3, xn_out3, scr, gamma=None, beta=None):
    te, ve, se = cx.e["t"], cx.e["v"], cx.e["s"]
    xsq3 = r3(scr["xsq"][:, :C * T])
    ins = se.activation(xsq3, x_src3, AF.Square)
    m_sq = cx.ac.inc(ins, 1)

    pb = cx.use_bank(scr["bi_stats"])
    te.wait_ge(cx.ac.h, m_sq)
    for ci in range(C):
        te.matmul(pb[0:1, 0:T], scr["ones"], x_src3[:, ci, :],
                  start=(ci == 0), stop=False)
        ins = te.matmul(pb[0:1, T:2 * T], scr["ones"], xsq3[:, ci, :],
                        start=False, stop=(ci == C - 1))
    m_pe = cx.pe.inc(ins, 1)

    row = scr["row2"]
    mu = row[0:1, 0:T]
    var = row[0:1, T:2 * T]
    tmp = row[0:1, 2 * T:3 * T]
    ve.wait_ge(cx.pe.h, m_pe)
    ins = ve.tensor_scalar_mul(row[0:1, 0:2 * T], pb[0:1, 0:2 * T], 1.0 / D)
    m_dv = cx.dv.inc(ins, 1)
    cx.bank_consumed(scr["bi_stats"], cx.dv, m_dv)
    cx.ve_edge(ins)
    cx.ve_edge(ve.tensor_mul(tmp, mu, mu))
    cx.ve_edge(ve.tensor_sub(var, var, tmp))
    cx.ve_edge(ve.tensor_scalar_add(var, var, 1e-5))
    ins = ve.reciprocal(tmp, var)
    m_rc = cx.dv.inc(ins, 1)
    rstd = var
    se.wait_ge(cx.dv.h, m_rc)
    ins = se.sqrt(rstd, tmp)
    m_rs = cx.ac.inc(ins, 1)
    ve.wait_ge(cx.ac.h, m_rs)
    ins = ve.tensor_mul(mu, mu, rstd)
    m_mr = cx.dv.inc(ins, 1)
    cx.ve_edge(ins)

    pb2 = cx.use_bank(scr["bi_rep"])
    te.wait_ge(cx.dv.h, m_mr)
    te.matmul(pb2[:, 0:T], scr["onesr"], rstd, start=True, stop=False)
    ins = te.matmul(pb2[:, T:2 * T], scr["onesr"], mu, start=False, stop=True)
    m_rep = cx.pe.inc(ins, 1)
    ve.wait_ge(cx.pe.h, m_rep)
    rep = scr["rep"]
    ins = ve.tensor_copy(rep[:, 0:2 * T], pb2[:, 0:2 * T])
    m_repc = cx.dv.inc(ins, 1)
    cx.bank_consumed(scr["bi_rep"], cx.dv, m_repc)
    cx.ve_edge(ins)

    rrep = rep[:, 0:T].unsqueeze(1).broadcast_to([128, C, T])
    mrep = rep[:, T:2 * T].unsqueeze(1).broadcast_to([128, C, T])
    cx.ve_edge(ve.tensor_tensor(xn_out3, x_src3, rrep, AL.mult))
    ins = ve.tensor_tensor(xn_out3, xn_out3, mrep, AL.subtract)
    if gamma is not None:
        cx.ve_edge(ins)
        for ci in range(C):
            ins = ve.scalar_tensor_tensor(
                xn_out3[:, ci, :], xn_out3[:, ci, :], gamma[:, ci:ci + 1],
                beta[:, ci:ci + 1].broadcast_to([128, T]), AL.mult, AL.add)
    m_xn = cx.dv.inc(ins, 1)
    cx.ve_edge(ins)
    return m_xn


def make_ln_stages(cx, T, src3, out3, scr, gamma=None, beta=None):
    te, ve, se = cx.e["t"], cx.e["v"], cx.e["s"]
    xsq3 = r3(scr["xsq"][:, :C * T])
    row2 = scr["row2"]
    rep = scr["rep"]
    st = {}

    def sq(guard=None, dv_guard=None):
        if guard is not None:
            se.wait_ge(cx.pe.h, guard)
        if dv_guard is not None:
            se.wait_ge(cx.dv.h, dv_guard)
        ins = se.activation(xsq3, src3, AF.Square)
        st["sq"] = cx.ac.inc(ins, 1)
        return st["sq"]

    def stats():
        pb = cx.use_bank(scr["bi_stats"])
        te.wait_ge(cx.ac.h, st["sq"])
        for ci in range(C):
            te.matmul(pb[0:1, 0:T], scr["ones"], src3[:, ci, :],
                      start=(ci == 0), stop=False)
            ins = te.matmul(pb[0:1, T:2 * T], scr["ones"], xsq3[:, ci, :],
                            start=False, stop=(ci == C - 1))
        st["stats"] = cx.pe.inc(ins, 1)
        st["pb"] = pb
        return st["stats"]

    def row():
        mu = row2[0:1, 0:T]
        var = row2[0:1, T:2 * T]
        tmp = row2[0:1, 2 * T:3 * T]
        ve.wait_ge(cx.pe.h, st["stats"])
        ins = ve.tensor_scalar_mul(row2[0:1, 0:2 * T], st["pb"][0:1, 0:2 * T],
                                   1.0 / D)
        m_dv = cx.dv.inc(ins, 1)
        cx.bank_consumed(scr["bi_stats"], cx.dv, m_dv)
        cx.ve_edge(ins)
        cx.ve_edge(ve.tensor_mul(tmp, mu, mu))
        cx.ve_edge(ve.tensor_sub(var, var, tmp))
        cx.ve_edge(ve.tensor_scalar_add(var, var, 1e-5))
        ins = ve.reciprocal(tmp, var)
        m_rc = cx.dv.inc(ins, 1)
        rstd = var
        se.wait_ge(cx.dv.h, m_rc)
        ins = se.sqrt(rstd, tmp)
        m_rs = cx.ac.inc(ins, 1)
        ve.wait_ge(cx.ac.h, m_rs)
        ins = ve.tensor_mul(mu, mu, rstd)
        st["row"] = cx.dv.inc(ins, 1)
        cx.ve_edge(ins)

    def repmm():
        mu = row2[0:1, 0:T]
        rstd = row2[0:1, T:2 * T]
        pb2 = cx.use_bank(scr["bi_rep"])
        te.wait_ge(cx.dv.h, st["row"])
        te.matmul(pb2[:, 0:T], scr["onesr"], rstd, start=True, stop=False)
        ins = te.matmul(pb2[:, T:2 * T], scr["onesr"], mu, start=False,
                        stop=True)
        st["rep"] = cx.pe.inc(ins, 1)
        st["pb2"] = pb2

    def w_repcopy(guard=None):
        ve.wait_ge(cx.pe.h, st["rep"])
        if guard is not None:
            ve.wait_ge(cx.pe.h, guard)
        ins = ve.tensor_copy(rep[:, 0:2 * T], st["pb2"][:, 0:2 * T])
        m_rc = cx.dv.inc(ins, 1)
        cx.bank_consumed(scr["bi_rep"], cx.dv, m_rc)
        cx.ve_edge(ins)

    rrep = rep[:, 0:T].unsqueeze(1).broadcast_to([128, C, T])
    mrep = rep[:, T:2 * T].unsqueeze(1).broadcast_to([128, C, T])

    def w_mult1():
        cx.ve_edge(ve.tensor_tensor(out3, src3, rrep, AL.mult))

    def w_mult2():
        ins = ve.tensor_tensor(out3, out3, mrep, AL.subtract)
        st["xn"] = cx.dv.inc(ins, 1)
        cx.ve_edge(ins)

    def w_gb():
        ins = None
        for ci in range(C):
            ins = ve.scalar_tensor_tensor(
                out3[:, ci, :], out3[:, ci, :], gamma[:, ci:ci + 1],
                beta[:, ci:ci + 1].broadcast_to([128, T]), AL.mult, AL.add)
        st["xn"] = cx.dv.inc(ins, 1)
        cx.ve_edge(ins)

    E = [w_repcopy, w_mult1, w_mult2]
    if gamma is not None:
        E.append(w_gb)
    return dict(sq=sq, stats=stats, row=row, repmm=repmm, E=E, st=st)


def emit_proj(cx, T, w_sb, xn3, out_ec, bi, nec=C, bias=None, act=AF.Copy,
              pre_wait=None):
    te, se = cx.e["t"], cx.e["s"]
    w3 = w_sb.rearrange("p (dc e) -> p dc e", dc=C)
    m_ev = None
    for ec in range(nec):
        bank = cx.use_bank(bi)
        if pre_wait is not None:
            pre_wait()
            pre_wait = None
        for dc in range(C):
            ins = te.matmul(bank[:, 0:T], w3[:, dc, 128 * ec:128 * (ec + 1)],
                            xn3[:, dc, :], start=(dc == 0), stop=(dc == C - 1))
        m_pe = cx.pe.inc(ins, 1)
        b = bias[:, ec:ec + 1] if bias is not None else 0.0
        if act == AF.Copy:
            ve = cx.e["v"]
            ve.wait_ge(cx.pe.h, m_pe)
            if bias is not None:
                ins = ve.tensor_scalar_add(out_ec(ec), bank[:, 0:T], b)
            else:
                ins = ve.tensor_copy(out_ec(ec), bank[:, 0:T])
            m_ev = cx.dv.inc(ins, 1)
            cx.bank_consumed(bi, cx.dv, m_ev)
        else:
            se.wait_ge(cx.pe.h, m_pe)
            ins = se.activation(out_ec(ec), bank[:, 0:T], act, bias=b)
            m_ev = cx.ac.inc(ins, 1)
            cx.bank_consumed(bi, cx.ac, m_ev)
    return m_ev




def emit_row_ln(cx, xcol, out_col, scr2):
    te, ve, se = cx.e["t"], cx.e["v"], cx.e["s"]
    ins = se.activation(scr2["xsqc"], xcol, AF.Square)
    m_sq = cx.ac.inc(ins, 1)
    pb = cx.use_bank(scr2["bank"])
    te.wait_ge(cx.ac.h, m_sq)
    te.matmul(pb[0:1, 0:C], scr2["ones"], xcol, start=True, stop=False)
    ins = te.matmul(pb[0:1, C:2 * C], scr2["ones"], scr2["xsqc"],
                    start=False, stop=True)
    m_pe = cx.pe.inc(ins, 1)
    row = scr2["rowc"]
    ve.wait_ge(cx.pe.h, m_pe)
    cx.ve_edge(ve.reduce_sum(row[0:1, 0:1], pb[0:1, 0:C], axis=AX.X))
    ins = ve.reduce_sum(row[0:1, 1:2], pb[0:1, C:2 * C], axis=AX.X)
    m_dv = cx.dv.inc(ins, 1)
    cx.bank_consumed(scr2["bank"], cx.dv, m_dv)
    cx.ve_edge(ins)
    mu = row[0:1, 0:1]
    s2 = row[0:1, 1:2]
    tmp = row[0:1, 2:3]
    cx.ve_edge(ve.tensor_scalar_mul(mu, mu, 1.0 / D))
    cx.ve_edge(ve.tensor_scalar_mul(s2, s2, 1.0 / D))
    cx.ve_edge(ve.tensor_mul(tmp, mu, mu))
    cx.ve_edge(ve.tensor_sub(s2, s2, tmp))
    cx.ve_edge(ve.tensor_scalar_add(s2, s2, 1e-5))
    ins = ve.reciprocal(tmp, s2)
    m_rc = cx.dv.inc(ins, 1)
    se.wait_ge(cx.dv.h, m_rc)
    ins = se.sqrt(s2, tmp)
    m_rs = cx.ac.inc(ins, 1)
    pb2 = cx.use_bank(scr2["bank"])
    te.wait_ge(cx.ac.h, m_rs)
    ins = te.matmul(pb2[:, 0:2], scr2["onesr"], row[0:1, 0:2],
                    start=True, stop=True)
    m_rep = cx.pe.inc(ins, 1)
    ve.wait_ge(cx.pe.h, m_rep)
    ins = ve.tensor_copy(scr2["repc"], pb2[:, 0:2])
    m_rpc = cx.dv.inc(ins, 1)
    cx.bank_consumed(scr2["bank"], cx.dv, m_rpc)
    cx.ve_edge(ins)
    ins = ve.scalar_tensor_tensor(
        out_col, xcol, scr2["repc"][:, 0:1],
        scr2["repc"][:, 1:2].broadcast_to([128, C]), AL.subtract, AL.mult)
    m_o = cx.dv.inc(ins, 1)
    cx.ve_edge(ins)
    return m_o


def emit_rs_send(cx, payload, nslots_words, rs_in, rs_out, K, mskbuf):
    ve, gp = cx.e["v"], cx.e["g"]
    W = nslots_words
    mb3 = mskbuf.rearrange("p (s w) -> p s w", s=8)
    for k in range(8):
        ins = ve.tensor_scalar_mul(mb3[:, k, :], payload,
                                   K["slot8"][:, k:k + 1])
    m_msk = cx.dv.inc(ins, 1)
    cx.ve_edge(ins)
    gp.wait_ge(cx.dv.h, m_msk)
    ins = gp.dma_start(
        out=rs_in.rearrange("s (p w) -> p s w", p=128),
        in_=mb3)
    m_in = cx.spo.inc(ins, 16)
    gp.wait_ge(cx.spo.h, m_in)
    ins = gp.collective_compute(
        "ReduceScatter", AL.add, replica_groups=[list(range(8))],
        ins=[rs_in.opt()], outs=[rs_out.opt()])
    return cx.cc.inc(ins, 1)


def emit_rs_recv(cx, rs_out, recv_buf, K, m_cc):
    ve, sy = cx.e["v"], cx.e["y"]
    sy.wait_ge(cx.cc.h, m_cc)
    ins = sy.dma_start(out=recv_buf.unsqueeze(1),
                       in_=rs_out.rearrange("o (p w) -> p o w", p=128))
    m_rv = cx.ldc.inc(ins, 16)
    cx.ldc.wait(ve)
    ins = ve.tensor_scalar_mul(recv_buf, recv_buf, K["gate1"][:, 0:1])
    m_g = cx.dv.inc(ins, 1)
    cx.ve_edge(ins)
    return m_g


def emit_rs(cx, payload, nslots_words, rs_in, rs_out, recv_buf, K, mskbuf):
    m_cc = emit_rs_send(cx, payload, nslots_words, rs_in, rs_out, K, mskbuf)
    return emit_rs_recv(cx, rs_out, recv_buf, K, m_cc)


def emit_ffn(cx, K, wr_sb, wk2_sb, wv2_sb, xs3, bound_col, reg, T=128):
    te, ve, se, gp, sy = (cx.e["t"], cx.e["v"], cx.e["s"], cx.e["g"],
                          cx.e["y"])

    def rr(ap):
        return ap[:, 0:C * T].rearrange("p (c t) -> p c t", c=C)
    scr = reg["scr"]
    B_xn, B_sh, B_xr = reg["xn"], reg["xnsh"], reg["xr"]
    B_kp, B_r = reg["xk"], reg["diff"]
    xn3 = rr(B_xn)
    xnsh3 = rr(B_sh)
    xr3 = rr(B_xr)
    xsq3 = rr(B_xr)
    kb3 = rr(B_sh)
    rb3 = rr(B_r)
    xrh3 = r3(bf(B_xn[:, 0:C * T // 2]))
    xrl3 = r3(bf(B_xn[:, C * T // 2:C * T]))
    xkh3 = r3(bf(B_xr[:, 0:C * T // 2]))
    xkl3 = r3(bf(B_xr[:, C * T // 2:C * T]))
    kah3 = r3(bf(B_kp[:, 0:C * T // 2]))
    kal3 = r3(bf(B_kp[:, C * T // 2:C * T]))
    prevc = reg["prevc"]
    row2 = scr["row2"]
    rep = scr["rep"]
    onesv = scr["ones"]
    onesr_v = scr["onesr"]
    NG = NT // T
    st = {"kmm": None, "sq": None, "stats": None, "rep": None,
          "sxr": None, "sxk": None}

    def stage_sq(g):
        if st["kmm"] is not None:
            se.wait_ge(cx.pe.h, st["kmm"])
        ins = se.activation(xsq3, xs3[:, :, g * T:(g + 1) * T], AF.Square)
        st["sq"] = cx.ac.inc(ins, 1)

    def stage_stats(g):
        pb = cx.use_bank(scr["bi_stats"])
        te.wait_ge(cx.ac.h, st["sq"])
        xsrc3 = xs3[:, :, g * T:(g + 1) * T]
        for ci in range(C):
            te.matmul(pb[0:1, 0:T], onesv, xsrc3[:, ci, :],
                      start=(ci == 0), stop=False)
            ins = te.matmul(pb[0:1, T:2 * T], onesv, xsq3[:, ci, :],
                            start=False, stop=(ci == C - 1))
        st["stats"] = (cx.pe.inc(ins, 1), pb)

    def stage_row():
        m_pe, pb = st["stats"]
        mu = row2[0:1, 0:T]
        var = row2[0:1, T:2 * T]
        tmp = row2[0:1, 2 * T:3 * T]
        ve.wait_ge(cx.pe.h, m_pe)
        ins = ve.tensor_scalar_mul(row2[0:1, 0:2 * T], pb[0:1, 0:2 * T],
                                   1.0 / D)
        m_dv = cx.dv.inc(ins, 1)
        cx.bank_consumed(scr["bi_stats"], cx.dv, m_dv)
        cx.ve_edge(ins)
        cx.ve_edge(ve.tensor_mul(tmp, mu, mu))
        cx.ve_edge(ve.tensor_sub(var, var, tmp))
        cx.ve_edge(ve.tensor_scalar_add(var, var, 1e-5))
        ins = ve.reciprocal(tmp, var)
        m_rc = cx.dv.inc(ins, 1)
        rstd = var
        se.wait_ge(cx.dv.h, m_rc)
        ins = se.sqrt(rstd, tmp)
        m_rs = cx.ac.inc(ins, 1)
        ve.wait_ge(cx.ac.h, m_rs)
        ins = ve.tensor_mul(mu, mu, rstd)
        st["row"] = cx.dv.inc(ins, 1)
        cx.ve_edge(ins)

    def stage_repmm():
        mu = row2[0:1, 0:T]
        rstd = row2[0:1, T:2 * T]
        pb2 = cx.use_bank(scr["bi_rep"])
        te.wait_ge(cx.dv.h, st["row"])
        te.matmul(pb2[:, 0:T], onesr_v, rstd, start=True, stop=False)
        ins = te.matmul(pb2[:, T:2 * T], onesr_v, mu, start=False, stop=True)
        st["rep"] = (cx.pe.inc(ins, 1), pb2)

    def make_E(g):
        xsrc3 = xs3[:, :, g * T:(g + 1) * T]
        rrep = rep[:, 0:T].unsqueeze(1).broadcast_to([128, C, T])
        mrep = rep[:, T:2 * T].unsqueeze(1).broadcast_to([128, C, T])

        def w_repcopy():
            m_pe, pb2 = st["rep"]
            ve.wait_ge(cx.pe.h, m_pe)
            ins = ve.tensor_copy(rep[:, 0:2 * T], pb2[:, 0:2 * T])
            m_rc = cx.dv.inc(ins, 1)
            cx.bank_consumed(scr["bi_rep"], cx.dv, m_rc)
            cx.ve_edge(ins)

        def w_xn1():
            cx.ve_edge(ve.tensor_tensor(xn3, xsrc3, rrep, AL.mult))

        def w_xn2():
            cx.ve_edge(ve.tensor_tensor(xn3, xn3, mrep, AL.subtract))

        def w_shift():
            cx.ve_edge(ve.tensor_copy(xnsh3[:, :, 1:T], xn3[:, :, 0:T - 1]))
            if g == 0:
                ins = ve.tensor_copy(xnsh3[:, :, 0:1],
                                     bound_col.unsqueeze(2))
            else:
                ins = ve.tensor_copy(xnsh3[:, :, 0:1], prevc.unsqueeze(2))
            cx.ve_edge(ins)
            cx.ve_edge(ve.tensor_copy(prevc, xn3[:, :, T - 1]))

        def w_diff():
            cx.ve_edge(ve.tensor_tensor(xn3, xn3, xnsh3, AL.subtract))

        def w_mixr():
            ins = None
            for c in range(C):
                ins = ve.scalar_tensor_tensor(
                    xr3[:, c, :], xn3[:, c, :], K["mu_r"][:, c:c + 1],
                    xnsh3[:, c, :], AL.mult, AL.add)
            cx.ve_edge(ins)

        def w_mixk():
            ins = None
            for c in range(C):
                ins = ve.scalar_tensor_tensor(
                    xnsh3[:, c, :], xn3[:, c, :], K["mu_k"][:, c:c + 1],
                    xnsh3[:, c, :], AL.mult, AL.add)
            cx.ve_edge(ins)

        return [w_repcopy, w_xn1, w_xn2, w_shift, w_diff, w_mixr, w_mixk]

    def post_splits(g):
        st["sxr"] = emit_split(cx, xr3, xrh3, xrl3)
        st["sxk"] = emit_split(cx, rr(B_sh), xkh3, xkl3)

    stage_sq(0)
    stage_stats(0)
    stage_row()
    stage_repmm()
    for w in make_E(0):
        w()
    post_splits(0)

    for g in range(NG):
        xsrc3 = xs3[:, :, g * T:(g + 1) * T]
        te.wait_ge(cx.dv.h, st["sxr"])
        emit_proj3(cx, T, bf(wr_sb), (xrh3, xrl3), lambda ec: rb3[:, ec, :],
                   4, act=AF.Sigmoid)
        te.wait_ge(cx.dv.h, st["sxk"])
        emit_proj3(cx, T, bf(wk2_sb), (xkh3, xkl3),
                   lambda ec: kb3[:, ec, :], 5, act=AF.Relu)
        st["kmm"] = cx.pe.v
        ins = se.activation(kb3, kb3, AF.Square)
        m_kact = cx.ac.inc(ins, 1)
        if g + 1 < NG:
            stage_sq(g + 1)
        ve.wait_ge(cx.ac.h, m_kact)
        m_ska = emit_split(cx, kb3, kah3, kal3)
        te.wait_ge(cx.dv.h, m_ska)
        E = make_E(g + 1) if g + 1 < NG else []
        wv4 = bf(wv2_sb).rearrange("p (two dc e) -> p two dc e", two=2, dc=C)
        for ec in range(C):
            bank = cx.use_bank(6)
            first = True
            for dc in range(C):
                wh = wv4[:, 0, dc, 128 * ec:128 * (ec + 1)]
                wl = wv4[:, 1, dc, 128 * ec:128 * (ec + 1)]
                for pi, (wt, xt3) in enumerate(((wh, kah3), (wh, kal3),
                                               (wl, kah3))):
                    last = (dc == C - 1) and pi == 2
                    ins = te.matmul(bank[:, 0:T], wt, xt3[:, dc, :],
                                    start=first, stop=last)
                    first = False
            m_pe = cx.pe.inc(ins, 1)
            if g + 1 < NG and ec == 1:
                stage_stats(g + 1)
            if g + 1 < NG and ec == 3:
                stage_repmm()
            ve.wait_ge(cx.pe.h, m_pe)
            ins = ve.tensor_mul(bank[:, 0:T], bank[:, 0:T], rb3[:, ec, :])
            cx.ve_edge(ins)
            ins = ve.tensor_tensor(xsrc3[:, ec, :], xsrc3[:, ec, :],
                                   bank[:, 0:T], AL.add)
            m_ev = cx.dv.inc(ins, 1)
            cx.ve_edge(ins)
            cx.bank_consumed(6, cx.dv, m_ev)
            if g + 1 < NG:
                if ec == 2:
                    stage_row()
                elif ec >= 3 and ec - 3 < len(E):
                    E[ec - 3]()
        if g + 1 < NG:
            for w in E[5:]:
                w()
            st["sxr"] = emit_split(cx, xr3, xrh3, xrl3)
            st["sxk"] = emit_split(cx, rr(B_sh), xkh3, xkl3)


def emit_attnb(cx, K, di, sp, xs3, onesc, onesr, ln4row, cc):
    te, ve, se, gp, sy = (cx.e["t"], cx.e["v"], cx.e["s"], cx.e["g"],
                          cx.e["y"])
    T = TG
    cx.mark()
    bproj = cx.al(NCH * S)

    def load_w2(dst, name, necol):
        ins = sy.dma_start(
            out=dst.rearrange("p (dc e) -> p dc e", dc=C),
            in_=di[name].rearrange("(dc p) e -> p dc e", p=128))
        return cx.ldw[0].inc(ins, 16)

    cx.mark()
    win_sb = cx.al(C * 2 * D)
    xn = cx.al(C * T)
    xsq = cx.al(C * T)
    row2 = cx.al(3 * T)
    rep = cx.al(2 * T)
    xp_g = cx.al(C * T)
    z_g = cx.al(C * T)
    wb_sb = cx.al(C * S)
    dtv = cx.al(C)
    dtm = cx.al(1)
    mskbuf2 = cx.al(8 * NCH * S // 2)
    rowc = cx.al(4)
    xsqc = cx.al(C)
    repc = cx.al(2)
    scr = dict(xsq=xsq, row2=row2[0:1, :], rep=rep, ones=onesc, onesr=onesr,
               bi_stats=2, bi_rep=3)
    scr2 = dict(bank=7, xsqc=xsqc, rowc=rowc[0:1, :], repc=repc,
                ones=onesc, onesr=onesr)

    sy.wait_ge(cx.pe.h, cx.pe.v)
    sy.wait_ge(cx.dv.h, cx.dv.v)
    sy.wait_ge(cx.ac.h, cx.ac.v)
    se.wait_ge(cx.pe.h, cx.pe.v)
    se.wait_ge(cx.dv.h, cx.dv.v)
    ve.wait_ge(cx.pe.h, cx.pe.v)
    ve.wait_ge(cx.ac.h, cx.ac.v)
    te.wait_ge(cx.dv.h, cx.dv.v)
    te.wait_ge(cx.ac.h, cx.ac.v)
    ins = sy.dma_start(
        out=bf(win_sb[:, 0:C * D]).rearrange(
            "p (two dc e) -> p two dc e", two=2, dc=C),
        in_=sp["hl_win0T"].rearrange("(dc p) two e -> p two dc e", p=128))
    cx.ldw[0].inc(ins, 16)
    ins = sy.dma_start(
        out=bf(win_sb[:, C * D:2 * C * D]).rearrange(
            "p (two dc e) -> p two dc e", two=2, dc=C),
        in_=sp["hl_win1T"].rearrange("(dc p) two e -> p two dc e", p=128))
    m_win = cx.ldw[0].inc(ins, 16)
    ins = sy.dma_start(out=wb_sb.rearrange("p (dc s) -> p dc s", dc=C),
                       in_=di["wbT"].rearrange("(dc p) s -> p dc s", p=128))
    m_wb = cx.ldc.inc(ins, 16)
    ins = se.activation(dtv, K["logdt"], AF.Exp)
    m_dt = cx.ac.inc(ins, 1)
    ve.wait_ge(cx.ac.h, m_dt)
    cx.ve_edge(ve.tensor_scalar_max(dtv, dtv, 1e-4))
    ins = ve.tensor_scalar_min(dtv, dtv, 1.0)
    m_dtc = cx.dv.inc(ins, 1)
    cx.ve_edge(ins)
    bank = cx.use_bank(7)
    te.wait_ge(cx.dv.h, m_dtc)
    ins = te.matmul(bank[0:1, 0:C], onesc, dtv, start=True, stop=True)
    m_pe = cx.pe.inc(ins, 1)
    ve.wait_ge(cx.pe.h, m_pe)
    ins = ve.reduce_sum(dtm[0:1, 0:1], bank[0:1, 0:C], axis=AX.X)
    m_d1 = cx.dv.inc(ins, 1)
    cx.bank_consumed(7, cx.dv, m_d1)
    cx.ve_edge(ins)
    ins = ve.tensor_scalar_mul(dtm[0:1, 0:1], dtm[0:1, 0:1], 1.0 / D)
    m_d2 = cx.dv.inc(ins, 1)
    cx.ve_edge(ins)
    bank = cx.use_bank(7)
    te.wait_ge(cx.dv.h, m_d2)
    ins = te.matmul(bank[:, 0:1], onesr, dtm[0:1, 0:1], start=True, stop=True)
    m_pe = cx.pe.inc(ins, 1)
    dtm_rep = cx.al(1)
    ve.wait_ge(cx.pe.h, m_pe)
    ins = ve.tensor_copy(dtm_rep, bank[:, 0:1])
    m_d3 = cx.dv.inc(ins, 1)
    cx.bank_consumed(7, cx.dv, m_d3)
    cx.ve_edge(ins)
    cx.ldc.wait(ve)
    ins = ve.tensor_scalar_mul(wb_sb, wb_sb, dtm_rep[:, 0:1])
    m_wbs = cx.dv.inc(ins, 1)
    cx.ve_edge(ins)

    xp3 = r3(xp_g)
    z3 = r3(z_g)
    wb3 = wb_sb.rearrange("p (dc s) -> p dc s", dc=C)
    bp3 = bproj.rearrange("p (k s) -> p k s", k=NCH)
    xnp = cx.al(C * T)
    pairb = [cx.al(C * T), cx.al(C * T)]
    win0p = bf(win_sb[:, 0:C * D]).rearrange(
        "p (two dc e) -> p two dc e", two=2, dc=C)
    win1p = bf(win_sb[:, C * D:2 * C * D]).rearrange(
        "p (two dc e) -> p two dc e", two=2, dc=C)
    p1st = {"stats2": None, "sq2": None, "split": [None, None]}

    def p1_chain_steps(g):
        xsrc3 = xs3[:, :, g * T:(g + 1) * T]
        ln1 = make_ln_stages(cx, T, xsrc3, r3(xnp), scr)
        ln2 = make_ln_stages(cx, T, r3(xnp), r3(xn), scr,
                             gamma=K["lng"], beta=K["lnb"])
        sl = g % 2
        xh = r3(bf(pairb[sl][:, 0:C * T // 2]))
        xl = r3(bf(pairb[sl][:, C * T // 2:C * T]))

        def s_sq1():
            ln1["sq"](p1st["stats2"])

        def s_mult1():
            if p1st["stats2"] is not None:
                ve.wait_ge(cx.pe.h, p1st["stats2"])
            if p1st["sq2"] is not None:
                ve.wait_ge(cx.ac.h, p1st["sq2"])
            ln1["E"][1]()

        def s_mult2_sq2():
            ln1["E"][2]()
            p1st["sq2"] = ln2["sq"](ln1["st"]["stats"],
                                    dv_guard=ln1["st"]["xn"])

        def s_stats2():
            ln2["stats"]()
            p1st["stats2"] = ln2["st"]["stats"]

        def s_split():
            p1st["split"][sl] = emit_split(cx, r3(xn), xh, xl)

        te_steps = {1: ln1["stats"], 3: ln1["repmm"], 7: s_stats2,
                    9: ln2["repmm"]}
        ve_steps = {0: s_sq1, 2: ln1["row"], 4: ln1["E"][0],
                    5: s_mult1, 6: s_mult2_sq2, 8: ln2["row"],
                    10: ln2["E"][0], 11: ln2["E"][1], 12: ln2["E"][2],
                    13: ln2["E"][3], 14: s_split}
        return te_steps, ve_steps

    te0, ve0 = p1_chain_steps(0)
    for i in range(16):
        if i in ve0 and i != 0:
            pass
    ve0[0]()
    te0[1]()
    ve0[2]()
    te0[3]()
    ve0[4]()
    ve0[5]()
    ve0[6]()
    te0[7]()
    ve0[8]()
    te0[9]()
    for i in (10, 11, 12, 13, 14):
        ve0[i]()

    for g in range(NGT):
        sl = g % 2
        xnh3 = r3(bf(pairb[sl][:, 0:C * T // 2]))
        xnl3 = r3(bf(pairb[sl][:, C * T // 2:C * T]))
        xsrc3 = xs3[:, :, g * T:(g + 1) * T]
        if cc in ("p1stop", "p1nobp"):
            gp.wait_ge(cx.dv.h, cx.dv.v)
            ins = gp.dma_start(
                out=sp["dbg_sp"].rearrange("(ec p) t -> p ec t", p=128)[
                    :, :, g * T:(g + 1) * T], in_=xsrc3)
            cx.spo.inc(ins, 16)
            ins = gp.dma_start(
                out=sp["q_sp"].rearrange("(ec p) t -> p ec t", p=128)[
                    :, :, g * T:(g + 1) * T], in_=r3(xnp))
            cx.spo.inc(ins, 16)
            cx.spo.wait(ve)
            cx.spo.wait(se)
            cx.spo.wait(te)
        nte, nve = (p1_chain_steps(g + 1) if g + 1 < NGT else ({}, {}))
        te.wait_ge(cx.dv.h, p1st["split"][sl])
        if g == 0:
            te.wait_ge(cx.ldw[0].h, m_win)
        m_xp_last = None
        for ec in range(16):
            wp4 = win0p if ec < 8 else win1p
            el = (ec % 8) * 128
            bank = cx.use_bank(4)
            first = True
            for dc in range(C):
                wh = wp4[:, 0, dc, el:el + 128]
                wl = wp4[:, 1, dc, el:el + 128]
                for pi, (wt, xt3) in enumerate(((wh, xnh3), (wh, xnl3),
                                               (wl, xnh3))):
                    ins = te.matmul(bank[:, 0:T], wt, xt3[:, dc, :],
                                    start=first,
                                    stop=(dc == C - 1 and pi == 2))
                    first = False
            m_pe = cx.pe.inc(ins, 1)
            if ec in nte:
                nte[ec]()
            if ec < 8:
                ve.wait_ge(cx.pe.h, m_pe)
                ins = ve.tensor_scalar_add(xp3[:, ec, :], bank[:, 0:T],
                                           K["b_in"][:, ec:ec + 1])
                m_ev = cx.dv.inc(ins, 1)
                m_xp_last = m_ev
                cx.bank_consumed(4, cx.dv, m_ev)
            else:
                se.wait_ge(cx.pe.h, m_pe)
                ins = se.activation(z3[:, ec - 8, :], bank[:, 0:T], AF.Silu,
                                    bias=K["b_in"][:, ec:ec + 1])
                m_ev = cx.ac.inc(ins, 1)
                cx.bank_consumed(4, cx.ac, m_ev)
            if ec in nve:
                nve[ec]()
        gp.wait_ge(cx.dv.h, cx.dv.v)
        gp.wait_ge(cx.ac.h, cx.ac.v)
        ins = gp.dma_start(
            out=sp["xp_sp"].rearrange("(ec p) t -> p ec t", p=128)[
                :, :, g * T:(g + 1) * T], in_=xp3)
        cx.sps[0].inc(ins, 16)
        ins = gp.dma_start(
            out=sp["z_sp"].rearrange("(ec p) t -> p ec t", p=128)[
                :, :, g * T:(g + 1) * T], in_=z3)
        cx.sps[1].inc(ins, 16)
        if cc == "p1nobp":
            continue
        if g == 0:
            ve.wait_ge(cx.dv.h, m_wbs)
            te.wait_ge(cx.dv.h, m_wbs)
        te.wait_ge(cx.dv.h, m_xp_last)
        for kt in range(T // 128):
            bank = cx.use_bank(5)
            for dc in range(C):
                ins = te.matmul(bank[:, 0:S],
                                xp3[:, dc, kt * 128:(kt + 1) * 128],
                                wb3[:, dc, :], start=(dc == 0),
                                stop=(dc == C - 1))
            m_pe = cx.pe.inc(ins, 1)
            ve.wait_ge(cx.pe.h, m_pe)
            ins = ve.tensor_copy(bp3[:, 16 + g * (T // 128) + kt, :],
                                 bank[:, 0:S])
            m_bp = cx.dv.inc(ins, 1)
            cx.bank_consumed(5, cx.dv, m_bp)
        cx.sps[0].wait(ve)
        cx.sps[1].wait(ve)
        cx.sps[0].wait(se)
        cx.sps[1].wait(se)
    cx.release()

    if cc in ("p1stop", "p1nobp"):
        return

    if cc:
        cx.mark()
        mskbuf2 = cx.al(8 * 256)
        emit_rs(cx, bproj[:, 256:512], 256, sp["rs2_in"], sp["rs2_out"],
                bproj[:, 0:256], K, mskbuf2)
        cx.release()
    else:
        ins = ve.memset(bproj[:, 0:256], 0.0)
        cx.dv.inc(ins, 1)

    cx.mark()
    xmt_sb = cx.al(S * 128)
    xmean_sb = cx.al(NCH // 2 * S)
    bprev = cx.al(NCH * S)
    cx.mark()
    abar = cx.al(C * S)
    labar = cx.al(C * S)
    a128 = cx.al(C * S)
    lnD = cx.al(1)
    ins = cx.e["v"].memset(lnD, -6.931471805599453)
    cx.dv.inc(ins, 1)
    p2buf = cx.al(S * 128)
    k1buf = cx.al(S * 128)
    tmp128b = [cx.al(128), cx.al(128)]
    hstate = cx.al(C * S)
    hbuf = cx.al(C * S * 16)
    cbuf = cx.al(C * S * NCH)
    ttbuf = cx.al(S * 128)
    am_sb = cx.al(S * 128)
    zrow = cx.al(128)

    ve.wait_ge(cx.dv.h, cx.dv.v)
    ve.wait_ge(cx.pe.h, cx.pe.v)
    ve.wait_ge(cx.ac.h, cx.ac.v)
    se.wait_ge(cx.pe.h, cx.pe.v)
    se.wait_ge(cx.dv.h, cx.dv.v)
    te.wait_ge(cx.dv.h, cx.dv.v)
    for half in range(1):
        bank = cx.use_bank(4)
        ins = te.matmul(bank[:, 0:512], K["arev"], bproj, start=True,
                        stop=True)
        m_rv = cx.pe.inc(ins, 1)
        ve.wait_ge(cx.pe.h, m_rv)
        ins = ve.tensor_copy(bprev, bank[:, 0:512])
        m_rvc = cx.dv.inc(ins, 1)
        cx.bank_consumed(4, cx.dv, m_rvc)
        cx.ve_edge(ins)
    bpr3 = bprev.rearrange("p (k s) -> p k s", k=NCH)
    al3 = K["alog"].rearrange("p (dc s) -> p dc s", dc=C)
    ab3 = abar.rearrange("p (dc s) -> p dc s", dc=C)
    ins = se.activation(abar, K["alog"], AF.Exp)
    m1 = cx.ac.inc(ins, 1)
    ve.wait_ge(cx.ac.h, m1)
    for dc in range(C):
        ins = ve.tensor_scalar_mul(ab3[:, dc, :], ab3[:, dc, :],
                                   dtv[:, dc:dc + 1])
    cx.ve_edge(ins)
    ins = ve.tensor_scalar_mul(abar, abar, -1.0)
    m2 = cx.dv.inc(ins, 1)
    se.wait_ge(cx.dv.h, m2)
    ins = se.activation(abar, abar, AF.Exp)
    m3 = cx.ac.inc(ins, 1)
    ve.wait_ge(cx.ac.h, m3)
    cx.ve_edge(ve.tensor_scalar_max(abar, abar, 1e-8))
    ins = ve.tensor_scalar_min(abar, abar, 1.0 - 1e-8)
    m4 = cx.dv.inc(ins, 1)
    se.wait_ge(cx.dv.h, m4)
    ins = se.activation(labar, abar, AF.Ln)
    m5 = cx.ac.inc(ins, 1)
    ins = se.activation(a128, labar, AF.Exp, scale=128.0)
    m6 = cx.ac.inc(ins, 1)
    ve.wait_ge(cx.ac.h, m6)

    lb3 = labar.rearrange("p (dc s) -> p dc s", dc=C)
    p23 = p2buf.rearrange("p (s t) -> p s t", s=S)
    k13 = k1buf.rearrange("p (s t) -> p s t", s=S)
    cb3 = cbuf.rearrange("p (dc s k) -> p dc s k", dc=C, s=S)
    hb3 = hbuf.rearrange("p (dc s k) -> p dc s k", dc=C, s=S)

    exp_hist = [None, None]
    for dc in range(C):
        ve.wait_ge(cx.pe.h, cx.pe.v)
        se.wait_ge(cx.pe.h, cx.pe.v)
        for s in range(S):
            tmp = tmp128b[s % 2]
            if exp_hist[s % 2] is not None:
                ve.wait_ge(cx.ac.h, exp_hist[s % 2])
            ins = ve.tensor_scalar_mul(tmp, K["iota_p"],
                                       lb3[:, dc, s:s + 1])
            mm1 = cx.dv.inc(ins, 1)
            se.wait_ge(cx.dv.h, mm1)
            ins = se.activation(p23[:, s, :], tmp, AF.Exp,
                                bias=lnD[:, 0:1])
            mm2 = cx.ac.inc(ins, 1)
            exp_hist[s % 2] = mm2
            bank_id = 6 if s % 2 == 0 else 4
            bank = cx.use_bank(bank_id)
            te.wait_ge(cx.ac.h, mm2)
            ins = te.transpose(bank[:, 0:128], p23[:, s, :], K["identc"])
            mm3 = cx.pe.inc(ins, 1)
            ve.wait_ge(cx.pe.h, mm3)
            ins = ve.tensor_copy(k13[:, s, :], bank[:, 0:128])
            mm4 = cx.dv.inc(ins, 1)
            cx.bank_consumed(bank_id, cx.dv, mm4)
        te.wait_ge(cx.dv.h, cx.dv.v)
        for s in range(S):
            bank_id = 7 if s % 2 == 0 else 5
            bank = cx.use_bank(bank_id)
            ins = te.matmul(bank[:, 0:NCH], k13[:, s, :], bpr3[:, :, s],
                            start=True, stop=True)
            mm5 = cx.pe.inc(ins, 1)
            ve.wait_ge(cx.pe.h, mm5)
            ins = ve.tensor_copy(cb3[:, dc, s, :], bank[:, 0:NCH])
            mm6 = cx.dv.inc(ins, 1)
            cx.bank_consumed(bank_id, cx.dv, mm6)

    ins = ve.memset(hstate, 0.0)
    m_h = cx.dv.inc(ins, 1)
    cx.ve_edge(ins)
    for k in range(1, NCH):
        cx.ve_edge(ve.tensor_mul(hstate, hstate, a128))
        ins = ve.tensor_tensor(
            hstate, hstate,
            cbuf.rearrange("p (g k) -> p g k", g=C * S)[:, :, k - 1], AL.add)
        cx.ve_edge(ins)
        if k >= 16:
            ins = ve.tensor_copy(
                hbuf.rearrange("p (g k) -> p g k", g=C * S)[:, :, k - 16],
                hstate)
    m_hc = cx.dv.inc(ins, 1)

    ins = ve.memset(am_sb, 0.0)
    cx.dv.inc(ins, 1)
    cx.ve_edge(ins)
    bank_i = cx.use_bank(3)
    exp_hist = [None, None]
    for dc in range(C):
        ve.wait_ge(cx.pe.h, cx.pe.v)
        se.wait_ge(cx.pe.h, cx.pe.v)
        for s in range(S):
            tmp = tmp128b[s % 2]
            if exp_hist[s % 2] is not None:
                ve.wait_ge(cx.ac.h, exp_hist[s % 2])
            ins = ve.tensor_scalar_mul(tmp, K["iota_t"],
                                       lb3[:, dc, s:s + 1])
            mm1 = cx.dv.inc(ins, 1)
            se.wait_ge(cx.dv.h, mm1)
            ins = se.activation(p23[:, s, :], tmp, AF.Exp)
            mm2 = cx.ac.inc(ins, 1)
            exp_hist[s % 2] = mm2
        te.wait_ge(cx.ac.h, cx.ac.v)
        if dc == 0:
            te.wait_ge(cx.dv.h, m_hc)
        for s in range(S):
            ins = te.matmul(bank_i[:, s * S:(s + 1) * S], p23[:, s, :],
                            hb3[:, dc, s, :],
                            start=(dc == 0 and s == 0), stop=False)
            cx.pe.inc(ins, 1)
        for s in range(S):
            bank_id = 6 if s % 2 == 0 else 7
            bank = cx.use_bank(bank_id)
            ins = te.matmul(bank[0:1, 0:128], onesc, p23[:, s, :],
                            start=True, stop=True)
            mm7 = cx.pe.inc(ins, 1)
            ve.wait_ge(cx.pe.h, mm7)
            ins = ve.tensor_tensor(am_sb[0:1, s * 128:(s + 1) * 128],
                                   am_sb[0:1, s * 128:(s + 1) * 128],
                                   bank[0:1, 0:128], AL.add)
            mm8 = cx.dv.inc(ins, 1)
            cx.bank_consumed(bank_id, cx.dv, mm8)

    cx.ve_edge(ve.tensor_scalar_mul(am_sb, am_sb, 1.0 / D))
    ins = ve.memset(zrow, 0.0)
    m_u = cx.dv.inc(ins, 1)
    cx.ve_edge(ins)
    gp.wait_ge(cx.dv.h, m_u)
    ins = gp.dma_start(out=sp["u_sp"][:, 0:127], in_=zrow[0:S, 0:127])
    cx.spo.inc(ins, 16)
    ins = gp.dma_start(out=sp["u_sp"][:, 255:256], in_=zrow[0:S, 0:1],
                       allow_slow_non_contiguous=True)
    cx.spv[0].inc(ins, 16)
    ins = gp.dma_start(
        out=sp["u_sp"][:, 127:255].unsqueeze(0),
        in_=am_sb[0:1, 0:S * 128].rearrange("o (s t) -> o s t", s=S))
    cx.spv[1].inc(ins, 16)
    cx.spo.wait(sy)
    cx.spv[0].wait(sy)
    cx.spv[1].wait(sy)
    tt_sems = [cx.ldc, cx.ldq, cx.ldw[0], cx.ldw[1]]
    for s in range(S):
        src_ap = bass.AP(tensor=sp["u_sp"].tensor, offset=s * 256,
                         ap=[[1, 128], [1, 128]])
        ins = sy.dma_start(out=ttbuf[:, s * 128:(s + 1) * 128], in_=src_ap)
        tt_sems[s % 4].inc(ins, 16)
    for sem in tt_sems:
        sem.wait(te)
    for s in range(S):
        ins = te.matmul(bank_i[:, s * S:(s + 1) * S],
                        ttbuf[:, s * 128:(s + 1) * 128],
                        bpr3[:, 16:32, s],
                        start=False, stop=(s == S - 1))
        m_ii = cx.pe.inc(ins, 1)
    ve.wait_ge(cx.pe.h, m_ii)
    ins = ve.tensor_copy(xmean_sb, bank_i[:, 0:256])
    m_xm = cx.dv.inc(ins, 1)
    cx.bank_consumed(3, cx.dv, m_xm)
    te.wait_ge(cx.dv.h, m_xm)
    xm2 = xmean_sb.rearrange("p (s k) -> p s k", s=S)
    for k in range(16):
        bank = cx.use_bank(6)
        ins = te.transpose(bank[0:S, 0:128], xm2[:, :, k], K["identc"])
        m_tk = cx.pe.inc(ins, 1)
        ve.wait_ge(cx.pe.h, m_tk)
        ins = ve.tensor_copy(xmt_sb[0:S, k * 128:(k + 1) * 128],
                             bank[0:S, 0:128])
        m_tkc = cx.dv.inc(ins, 1)
        cx.bank_consumed(6, cx.dv, m_tkc)
    cx.release()

    cx.mark()
    wout_sb = cx.al(C * D)
    wc_sb = cx.al(C * 128)
    xp_g2 = [cx.al(C * T), cx.al(C * T)]
    z_g2 = [cx.al(C * T), cx.al(C * T)]
    y_g = cx.al(C * T)
    xnp2 = [cx.al(C * T), cx.al(C * T)]
    ypair = [cx.al(C * T), cx.al(C * T)]
    xsq = cx.al(C * T)
    row2 = cx.al(3 * T)
    rep = cx.al(2 * T)
    scr = dict(xsq=xsq, row2=row2[0:1, :], rep=rep, ones=onesc, onesr=onesr,
               bi_stats=2, bi_rep=3)
    sy.wait_ge(cx.pe.h, cx.pe.v)
    sy.wait_ge(cx.dv.h, cx.dv.v)
    sy.wait_ge(cx.ac.h, cx.ac.v)
    ins = sy.dma_start(
        out=bf(wout_sb).rearrange("p (two dc e) -> p two dc e", two=2, dc=C),
        in_=sp["hl_woutT"].rearrange("(dc p) two e -> p two dc e", p=128))
    m_wo = cx.ldw[0].inc(ins, 16)
    ins = sy.dma_start(out=wc_sb[0:S, :],
                       in_=di["wcT"])
    m_wc = cx.ldc.inc(ins, 16)
    y3 = r3(y_g)
    p3st = {"stats": None, "yz": [None, None], "split": [None, None]}

    def p3_chain_steps(g, sl):
        xp3 = r3(xp_g2[sl])
        z3g = r3(z_g2[sl])
        xsrc3 = xs3[:, :, g * T:(g + 1) * T]
        ln = make_ln_stages(cx, T, xsrc3, r3(xnp2[sl]), scr)
        yh = r3(bf(ypair[sl][:, 0:C * T // 2]))
        yl = r3(bf(ypair[sl][:, C * T // 2:C * T]))
        h = {}

        def s_loads_sq():
            if p3st["yz"][sl] is not None:
                sy.wait_ge(cx.dv.h, p3st["yz"][sl])
            ins = sy.dma_start(
                out=xp3,
                in_=sp["xp_sp"].rearrange("(ec p) t -> p ec t", p=128)[
                    :, :, g * T:(g + 1) * T])
            h["mxp"] = cx.lds[0].inc(ins, 16)
            ins = sy.dma_start(
                out=z3g,
                in_=sp["z_sp"].rearrange("(ec p) t -> p ec t", p=128)[
                    :, :, g * T:(g + 1) * T])
            h["mz"] = cx.lds[1].inc(ins, 16)
            ln["sq"](p3st["stats"])

        def s_stats():
            ln["stats"]()
            p3st["stats"] = ln["st"]["stats"]

        def s_ymm(i):
            def go():
                for ec in (2 * i, 2 * i + 1):
                    bi = 4 if ec % 2 == 0 else 6
                    bank = cx.use_bank(bi)
                    for kt in range(T // 128):
                        k = g * (T // 128) + kt
                        ins = te.matmul(bank[:, kt * 128:(kt + 1) * 128],
                                        wc_sb[0:S, ec * 128:(ec + 1) * 128],
                                        xmt_sb[0:S, k * 128:(k + 1) * 128],
                                        start=(kt == 0),
                                        stop=(kt == T // 128 - 1))
                    h[("ymm", ec)] = cx.pe.inc(ins, 1)
            return go

        def s_yevac(i):
            def go():
                if i == 0:
                    ve.wait_ge(cx.lds[0].h, h["mxp"])
                ins = None
                for ec in (2 * i, 2 * i + 1):
                    bi = 4 if ec % 2 == 0 else 6
                    ve.wait_ge(cx.pe.h, h[("ymm", ec)])
                    ins = ve.scalar_tensor_tensor(
                        y3[:, ec, :], r3(xp_g2[sl])[:, ec, :],
                        K["dskip"][:, ec:ec + 1],
                        cx.pb[bi][:, 0:T], AL.mult, AL.add)
                    m_y = cx.dv.inc(ins, 1)
                    cx.bank_consumed(bi, cx.dv, m_y)
                if i == 3:
                    cx.ve_edge(ins)
            return go

        def s_yz():
            ve.wait_ge(cx.lds[1].h, h["mz"])
            ins = ve.tensor_tensor(y3, y3, z3g, AL.mult)
            p3st["yz"][sl] = cx.dv.inc(ins, 1)
            cx.ve_edge(ins)

        def s_split():
            p3st["split"][sl] = emit_split(cx, y3, yh, yl)

        def w2():
            ln["row"]()
            s_yevac(0)()

        def w4():
            ln["E"][0]()
            s_yevac(1)()

        def w5():
            ln["E"][1]()
            s_yevac(2)()

        def w6():
            ln["E"][2]()
            s_yevac(3)()
            s_yz()

        te_steps = {1: s_stats, 2: s_ymm(0), 3: ln["repmm"],
                    4: s_ymm(1), 5: s_ymm(2), 6: s_ymm(3)}
        ve_steps = {0: s_loads_sq, 2: w2, 4: w4, 5: w5, 6: w6,
                    7: s_split}
        return te_steps, ve_steps, s_split

    ve.wait_ge(cx.pe.h, cx.pe.v)
    ve.wait_ge(cx.ac.h, cx.ac.v)
    se.wait_ge(cx.pe.h, cx.pe.v)
    se.wait_ge(cx.dv.h, cx.dv.v)
    te.wait_ge(cx.dv.h, cx.dv.v)
    cx.ldc.wait(te)
    p3_order = [7] + list(range(7))
    rs3own = cx.al(C)
    rs3msk = cx.al(8 * C)
    rs3scr = dict(bank=7, xsqc=cx.al(C), rowc=cx.al(4)[0:1, :],
                  repc=cx.al(2), ones=onesc, onesr=onesr)
    m_rs3cc = None
    te0, ve0, split0 = p3_chain_steps(p3_order[0], 0)
    ve0[0]()
    te0[1]()
    te0[2]()
    ve0[2]()
    te0[3]()
    te0[4]()
    ve0[4]()
    te0[5]()
    ve0[5]()
    te0[6]()
    ve0[6]()
    ve0[7]()

    wo4 = bf(wout_sb).rearrange("p (two dc e) -> p two dc e", two=2, dc=C)
    for i, g in enumerate(p3_order):
        sl = i % 2
        xsrc3 = xs3[:, :, g * T:(g + 1) * T]
        yh3 = r3(bf(ypair[sl][:, 0:C * T // 2]))
        yl3 = r3(bf(ypair[sl][:, C * T // 2:C * T]))
        xnp3 = r3(xnp2[sl])
        if i + 1 < NGT:
            nte, nve, _ = p3_chain_steps(p3_order[i + 1], (i + 1) % 2)
        else:
            nte, nve = {}, {}
        te.wait_ge(cx.dv.h, p3st["split"][sl])
        if i == 0:
            te.wait_ge(cx.ldw[0].h, m_wo)
        for ec in range(C):
            bank = cx.use_bank(5)
            first = True
            for dc in range(C):
                wh = wo4[:, 0, dc, 128 * ec:128 * (ec + 1)]
                wl = wo4[:, 1, dc, 128 * ec:128 * (ec + 1)]
                for pi, (wt, xt3) in enumerate(((wh, yh3), (wh, yl3),
                                               (wl, yh3))):
                    ins = te.matmul(bank[:, 0:T], wt, xt3[:, dc, :],
                                    start=first,
                                    stop=(dc == C - 1 and pi == 2))
                    first = False
            m_pe = cx.pe.inc(ins, 1)
            if ec in nte:
                nte[ec]()
            ve.wait_ge(cx.pe.h, m_pe)
            ins = ve.scalar_tensor_tensor(
                xsrc3[:, ec, :], bank[:, 0:T], K["bout"][:, ec:ec + 1],
                xsrc3[:, ec, :], AL.add, AL.add)
            m_x4 = cx.dv.inc(ins, 1)
            cx.bank_consumed(5, cx.dv, m_x4)
            cx.ve_edge(ins)
            ins = ve.tensor_tensor(xsrc3[:, ec, :], xsrc3[:, ec, :],
                                   xnp3[:, ec, :], AL.add)
            cx.dv.inc(ins, 1)
            if ec in nve:
                nve[ec]()
        if i == 0 and cc is True:
            se.wait_ge(cx.dv.h, cx.dv.v)
            te.wait_ge(cx.dv.h, cx.dv.v)
            emit_row_ln(cx, xs3[:, :, NT - 1], rs3own, rs3scr)
            m_rs3cc = emit_rs_send(cx, rs3own, C, sp["rs3_in"],
                                   sp["rs3_out"], K, rs3msk)
    cx.release()
    cx.release()

    if cc is True:
        emit_rs_recv(cx, sp["rs3_out"], ln4row, K, m_rs3cc)
    else:
        ve.wait_ge(cx.ac.h, cx.ac.v)
        ve.wait_ge(cx.pe.h, cx.pe.v)
        ins = ve.memset(ln4row, 0.0)
        cx.dv.inc(ins, 1)
    cx.release()


def build_program(stop_after=None, cc=True, detect=False):
    nc = bass.Bass("TRN2", target_bir_lowering=False)
    nc.detect_race_conditions = detect
    cx = Ctx(nc)

    di = {}

    def inp(name, shape):
        di[name] = nc.dram_tensor(name, shape, F32, kind="ExternalInput").ap()
        return di[name]

    inp("x_res", [D, NT])
    inp("x_kv", [D, NK])
    for w in ["wqT", "wkT", "wvT", "wrT", "wk2T", "wv2T", "woutT"]:
        inp(w, [D, D])
    inp("winT", [D, 2 * D])
    inp("wbT", [D, S])
    inp("wcT", [S, D])
    for b in ["bq", "bk", "bv", "bout", "mu_r", "mu_k", "lng", "lnb",
              "dskip", "logdt"]:
        inp(b, [128, C])
    inp("b_in", [128, 2 * C])
    inp("alog", [128, C * S])
    inp("bias_cols", [128, NSTEPS])
    inp("m2d", [128, 256])
    inp("gate1", [128, 1])
    inp("slot8", [128, 8])
    inp("bv_row", [1, D])
    inp("iota_t", [128, 128])
    inp("iota_r", [128, 128])
    inp("iota_p", [128, 128])
    inp("arev", [128, 128])
    inp("identc", [128, 128])
    out_d = nc.dram_tensor("out_fm", [C, 128, NT], F32,
                           kind="ExternalOutput").ap()

    sp = {}
    for wname in ["wqT", "wkT", "wvT", "wrT", "wk2T", "wv2T", "woutT",
                  "win0T", "win1T"]:
        sp["hl_" + wname] = nc.dram_tensor("hl_" + wname, [D, 2, D],
                                           BF16).ap()
    sp["k_sp"] = nc.dram_tensor("k_sp", [D, NK], F32).ap()
    sp["v_sp"] = nc.dram_tensor("v_sp", [NK, D], F32).ap()
    sp["q_sp"] = nc.dram_tensor("q_sp", [D, NT], F32).ap()
    sp["xp_sp"] = nc.dram_tensor("xp_sp", [D, NT], F32).ap()
    sp["z_sp"] = nc.dram_tensor("z_sp", [D, NT], F32).ap()
    sp["u_sp"] = nc.dram_tensor("u_sp", [S, 256], F32).ap()
    sp["dbg_sp"] = nc.dram_tensor("dbg_sp", [D, NT], F32).ap()
    sp["rs1_in"] = nc.dram_tensor("rs1_in", [8, D], F32).ap()
    sp["rs1_out"] = nc.dram_tensor("rs1_out", [1, D], F32).ap()
    sp["rs2_in"] = nc.dram_tensor("rs2_in", [8, NT * S], F32).ap()
    sp["rs2_out"] = nc.dram_tensor("rs2_out", [1, NT * S], F32).ap()
    sp["rs3_in"] = nc.dram_tensor("rs3_in", [8, D], F32).ap()
    sp["rs3_out"] = nc.dram_tensor("rs3_out", [1, D], F32).ap()

    for nm in ["y", "g", "s", "v", "t"]:
        cx.e[nm] = REng()
    emit_all(cx, di, out_d, sp, stop_after, cc)

    for eng in cx.e.values():
        totals = {}
        out = []
        for c in eng.cmds:
            if c.name == "dma_start" and c.incs:
                sh, n = c.incs[0]
                prev = totals.get(id(sh), 0)
                if prev > 0:
                    w = RCmd("wait_ge", (sh, prev), {})
                    out.append(w)
                totals[id(sh)] = prev + n
            out.append(c)
        eng.cmds = out

    with nc.Block() as block:
        @block.sync
        def _(sy):
            cx.e["y"].replay(sy)

        @block.gpsimd
        def _(g):
            cx.e["g"].replay(g)

        @block.scalar
        def _(s):
            cx.e["s"].replay(s)

        @block.vector
        def _(v):
            cx.e["v"].replay(v)

        @block.tensor
        def _(t):
            cx.e["t"].replay(t)

    return nc


def emit_all(cx, di, out_d, sp, stop_after=None, cc=True):
    te, ve, se, gp, sy = (cx.e["t"], cx.e["v"], cx.e["s"], cx.e["g"],
                          cx.e["y"])

    xs = cx.al(C * NT)
    xs3 = r3(xs)
    K = {}
    for name in ["bq", "bk", "bv", "bout", "mu_r", "mu_k", "lng", "lnb",
                 "dskip", "logdt"]:
        K[name] = cx.al(C)
    K["b_in"] = cx.al(2 * C)
    K["alog"] = cx.al(C * S)
    K["gate1"] = cx.al(1)
    K["slot8"] = cx.al(8)
    K["iota_t"] = cx.al(128)
    K["iota_r"] = cx.al(128)
    K["iota_p"] = cx.al(128)
    K["arev"] = cx.al(128)
    K["identc"] = cx.al(128)
    onesc = cx.al(1)
    onesr_t = cx.al(128)
    ln2row = cx.al(C)
    ln4row = cx.al(C)

    for name in ["bq", "bk", "bv", "bout", "mu_r", "mu_k", "lng", "lnb",
                 "dskip", "logdt", "b_in", "alog", "gate1", "slot8",
                 "iota_t", "iota_r", "iota_p", "arev", "identc"]:
        ins = sy.dma_start(out=K[name], in_=di[name])
        cx.ldc.inc(ins, 16)
    cx.ldc.wait(ve)
    cx.ldc.wait(te)
    cx.ldc.wait(se)
    ve.memset(onesc, 1.0)
    ins = ve.memset(onesr_t[0:1, :], 1.0)
    m_ones = cx.dv.inc(ins, 1)
    onesr = onesr_t[0:1, :]
    te.wait_ge(cx.dv.h, m_ones)

    cx.mark()
    bvrep = cx.al(C * 128)
    wbuf = cx.al(C * D)
    wbuf2 = cx.al(C * D)
    stg = [cx.al(C * TG), cx.al(C * TG)]
    xn = cx.al(C * TG)
    xnh = cx.al(C * TG // 2)
    xnl = cx.al(C * TG // 2)
    xsq = cx.al(C * TG)
    row2 = cx.al(3 * TG)
    rep = cx.al(2 * TG)
    pout = [cx.al(C * TG), cx.al(C * TG)]
    vout = [cx.al(C * TG)]

    scr = dict(xsq=xsq, row2=row2[0:1, :], rep=rep, ones=onesc, onesr=onesr,
               bi_stats=2, bi_rep=3)

    def load_w(dst, name, eng, sem):
        ins = eng.dma_start(
            out=dst.rearrange("p (dc e) -> p dc e", dc=C),
            in_=di[name].rearrange("(dc p) e -> p dc e", p=128))
        return sem.inc(ins, 16)

    def load_w_hl(dst_words, name, eng, sem):
        ins = eng.dma_start(
            out=bf(dst_words).rearrange("p (two dc e) -> p two dc e", two=2,
                                        dc=C),
            in_=sp["hl_" + name].rearrange("(dc p) two e -> p two dc e",
                                           p=128))
        return sem.inc(ins, 16)

    def prep_src_ap(wname, ec):
        if wname == "win0T":
            base = di["winT"].rearrange("(dc p) e -> p dc e", p=128)[
                :, :, 0:D]
        elif wname == "win1T":
            base = di["winT"].rearrange("(dc p) e -> p dc e", p=128)[
                :, :, D:2 * D]
        else:
            base = di[wname].rearrange("(dc p) e -> p dc e", p=128)
        return base[:, :, 128 * ec:128 * (ec + 1)]

    def pair_view(words):
        return bf(words).rearrange("p (two dc e) -> p two dc e", two=2, dc=C)

    prep = {"war": {}, "m_out": {}, "m_dir": {}, "m_in": {},
            "pair_out": None}

    def prep_load(wname, ec, srcb, war=None):
        if war is not None:
            sy.wait_ge(cx.dv.h, war)
        prep["m_in"][(wname, ec)] = cx.dma(sy, cx.pin, r3(srcb),
                                           prep_src_ap(wname, ec))

    def prep_split(wname, ec, srcb, direct=None, pairb=None):
        ve.wait_ge(cx.pin.h, prep["m_in"].pop((wname, ec)))
        if direct is not None:
            pv = pair_view(direct)[:, :, :, 128 * ec:128 * (ec + 1)]
        else:
            if prep["pair_out"] is not None:
                ve.wait_ge(cx.pot.h, prep["pair_out"])
            pv = pair_view(pairb)
        ins = ve.tensor_copy(pv[:, 0], r3(srcb))
        cx.ve_edge(ins)
        ins = ve.tensor_tensor(pv[:, 1], r3(srcb), pv[:, 0], AL.subtract)
        m_spl = cx.dv.inc(ins, 1)
        if direct is None:
            gp.wait_ge(cx.dv.h, m_spl)
            m_out = cx.dma(
                gp, cx.pot,
                sp["hl_" + wname].rearrange(
                    "(dc p) two e -> p two dc e",
                    p=128)[:, :, :, 128 * ec:128 * (ec + 1)],
                pv)
            prep["pair_out"] = m_out
            prep["m_out"][wname] = m_out
        else:
            prep["m_dir"][wname] = m_spl
        return m_spl

    war_q = [None, None]
    for ec in range(C):
        srcb = pout[ec % 2][:, 0:1024]
        prep_load("wqT", ec, srcb, war=war_q[ec % 2])
        war_q[ec % 2] = prep_split("wqT", ec, srcb, direct=wbuf)
    m_wq = prep["m_dir"]["wqT"]

    stage_guard = [None, None]

    def ln_group(g, src_name, T=TG):
        sl = g % 2
        st = stg[sl]
        if stage_guard[sl] is not None:
            sy.wait_ge(cx.dv.h, stage_guard[sl][0])
            sy.wait_ge(cx.ac.h, stage_guard[sl][1])
        ins = sy.dma_start(
            out=r3(st),
            in_=di[src_name].rearrange("(dc p) t -> p dc t", p=128)[
                :, :, g * T:(g + 1) * T])
        cx.lds[sl].inc(ins, 16)
        cx.lds[sl].wait(ve)
        cx.lds[sl].wait(se)
        cx.lds[sl].wait(te)
        ve.wait_ge(cx.pe.h, cx.pe.v)
        ve.wait_ge(cx.ac.h, cx.ac.v)
        se.wait_ge(cx.pe.h, cx.pe.v)
        se.wait_ge(cx.dv.h, cx.dv.v)
        te.wait_ge(cx.dv.h, cx.dv.v)
        m_xn = emit_ln(cx, T, r3(st), r3(xn), scr)
        m_xn = emit_split(cx, r3(xn), r3(bf(xnh)), r3(bf(xnl)))
        stage_guard[sl] = (m_xn, cx.ac.v)
        te.wait_ge(cx.dv.h, m_xn)
        return m_xn

    def s1_load(t, war):
        wname = "wkT" if t % 2 == 0 else "wvT"
        prep_load(wname, t // 2, vout[0][:, (t % 2) * 1024:(t % 2 + 1) * 1024],
                  war=war)

    def s1_split(t):
        wname = "wkT" if t % 2 == 0 else "wvT"
        srcb = vout[0][:, (t % 2) * 1024:(t % 2 + 1) * 1024]
        if wname == "wkT":
            return prep_split(wname, t // 2, srcb, pairb=bvrep)
        return prep_split(wname, t // 2, srcb, direct=wbuf2)

    s1_war = [None, None]
    for g in range(NGT):
        s1_load(2 * g, s1_war[0])
        s1_load(2 * g + 1, s1_war[1])
        ln_group(g, "x_res")
        sl = g % 2
        po = pout[sl]
        cx.sps[sl].wait(ve)
        te.wait_ge(cx.dv.h, m_wq)
        emit_proj3(cx, TG, bf(wbuf), (r3(bf(xnh)), r3(bf(xnl))),
                   lambda ec: r3(po)[:, ec, :], 4, bias=K["bq"])
        gp.wait_ge(cx.dv.h, cx.dv.v)
        ins = gp.dma_start(
            out=sp["q_sp"].rearrange("(ec p) t -> p ec t", p=128)[
                :, :, g * TG:(g + 1) * TG],
            in_=r3(po))
        cx.sps[sl].inc(ins, 16)
        s1_war[0] = s1_split(2 * g)
        s1_war[1] = s1_split(2 * g + 1)

    sy.wait_ge(cx.pe.h, cx.pe.v)
    sy.wait_ge(cx.pot.h, prep["m_out"]["wkT"])
    load_w_hl(wbuf, "wkT", sy, cx.ldw[0])
    ins = sy.dma_start(out=bvrep,
                       in_=di["bv_row"][0:1, :].broadcast_to([128, D]))
    cx.ldc.inc(ins, 16)
    cx.ldc.wait(ve)
    for g in range(NGK):
        ln_group(g, "x_kv")
        sl = g % 2
        po = pout[sl]
        vo = vout[0]
        cx.sps[sl].wait(ve)
        cx.ldw[0].wait(te)
        emit_proj3(cx, TG, bf(wbuf), (r3(bf(xnh)), r3(bf(xnl))),
                   lambda ec: r3(po)[:, ec, :], 4, bias=K["bk"])
        gp.wait_ge(cx.dv.h, cx.dv.v)
        ins = gp.dma_start(
            out=sp["k_sp"].rearrange("(ec p) t -> p ec t", p=128)[
                :, :, g * TG:(g + 1) * TG],
            in_=r3(po))
        cx.sps[sl].inc(ins, 16)
        te.wait_ge(cx.dv.h, prep["m_dir"]["wvT"])
        cx.spv[0].wait(ve)
        wv4 = bf(wbuf2).rearrange("p (two dc e) -> p two dc e", two=2, dc=C)
        xh3b = r3(bf(xnh))
        xl3b = r3(bf(xnl))
        vo3 = vo.rearrange("p (kt e) -> p kt e", kt=TG // 128)
        m_vv = None
        for kt in range(TG // 128):
            for half in range(2):
                bank = cx.use_bank(5)
                first = True
                for dc in range(C):
                    xh_c = xh3b[:, dc, kt * 128:(kt + 1) * 128]
                    xl_c = xl3b[:, dc, kt * 128:(kt + 1) * 128]
                    wh_c = wv4[:, 0, dc, half * 512:(half + 1) * 512]
                    wl_c = wv4[:, 1, dc, half * 512:(half + 1) * 512]
                    for (xt, wt) in ((xh_c, wh_c), (xh_c, wl_c),
                                     (xl_c, wh_c)):
                        last = (dc == C - 1) and (xt is xl_c)
                        ins = te.matmul(bank[:, 0:512], xt, wt,
                                        start=first, stop=last)
                        first = False
                m_pe = cx.pe.inc(ins, 1)
                ve.wait_ge(cx.pe.h, m_pe)
                ins = ve.tensor_tensor(
                    vo3[:, kt, half * 512:(half + 1) * 512], bank[:, 0:512],
                    bvrep[:, half * 512:(half + 1) * 512], AL.add)
                m_vv = cx.dv.inc(ins, 1)
                cx.bank_consumed(5, cx.dv, m_vv)
        gp.wait_ge(cx.dv.h, m_vv)
        ins = gp.dma_start(
            out=sp["v_sp"][g * TG:(g + 1) * TG, :].rearrange(
                "(kt p) e -> p kt e", p=128),
            in_=vo3)
        cx.spv[0].inc(ins, 16)
    cx.release()

    if stop_after in ("dump_q", "dump_k", "dump_v"):
        finish_stub(cx, out_d,
                    dump=sp[{"dump_q": "q_sp", "dump_k": "k_sp",
                             "dump_v": "v_sp"}[stop_after]])
        return
    if stop_after == "qkv":
        finish_stub(cx, out_d, zero=True)
        return

    for eng in (sy, ve, se, gp):
        for s in cx.sps + cx.spv:
            s.wait(eng)
    cx.mark()
    qblk = cx.al(C * 512)
    oblk = cx.al(QB * C * 128)
    ktb = [cx.al(C * 128), cx.al(C * 128)]
    vtb = [cx.al(C * 128), cx.al(C * 128)]
    p_sb = cx.al(128)
    pt_sb = cx.al(128)
    mstat = cx.al(QB * 2)
    lstat = cx.al(QB)
    colA = cx.al(1)
    colB = cx.al(1)
    colC = cx.al(1)
    colD = cx.al(1)
    bias_sb = cx.al(NSTEPS)
    m2d_sb = cx.al(256)
    p_src = cx.al(2 * 1024)
    p_pair = cx.al(2 * 1024)

    ins = sy.dma_start(out=bias_sb, in_=di["bias_cols"])
    cx.ldc.inc(ins, 16)
    ins = sy.dma_start(out=m2d_sb, in_=di["m2d"])
    cx.ldc.inc(ins, 16)

    s2_list = [(w, ec) for w in ["wrT", "wk2T", "wv2T", "win0T", "win1T",
                                 "woutT"] for ec in range(C)]
    s2 = {"load": 0, "split": 0, "war": [None, None]}

    def s2_pump():
        if s2["load"] < len(s2_list) and s2["load"] - s2["split"] < 2:
            t = s2["load"]
            w, ec = s2_list[t]
            prep_load(w, ec, p_src[:, (t % 2) * 1024:(t % 2 + 1) * 1024],
                      war=s2["war"][t % 2])
            s2["load"] += 1
        elif s2["split"] < s2["load"]:
            t = s2["split"]
            w, ec = s2_list[t]
            s2["war"][t % 2] = prep_split(
                w, ec, p_src[:, (t % 2) * 1024:(t % 2 + 1) * 1024],
                pairb=p_pair[:, (t % 2) * 1024:(t % 2 + 1) * 1024])
            s2["split"] += 1
    oblk3 = oblk.rearrange("p (i e) -> p i e", i=QB)
    cx.ldc.wait(ve)
    cx.ldc.wait(se)

    step_idx = 0
    pipe_state = {"mpp": None}
    nb_run = 1 if stop_after in ("attn1b", "attn1b_noret") else NB
    for bq in range(nb_run):
        nslot = 20 + 4 * bq
        sy.wait_ge(cx.pe.h, cx.pe.v)
        ins = sy.dma_start(
            out=r3(qblk),
            in_=sp["q_sp"].rearrange("(ec p) t -> p ec t", p=128)[
                :, :, bq * 512:(bq + 1) * 512])
        cx.ldq.inc(ins, 16)
        if bq == 0:
            ve.wait_ge(cx.pe.h, cx.pe.v)
            ve.wait_ge(cx.ac.h, cx.ac.v)
        ve.memset(oblk, 0.0)
        ve.memset(mstat, -1e30)
        ins = ve.memset(lstat, 0.0)
        m_init = cx.dv.inc(ins, 1)
        cx.ldq.wait(te)
        te.wait_ge(cx.dv.h, m_init)
        q3 = r3(qblk)
        kv_guard = {}
        tile_par = [0] * QB
        pending = []

        def emit_rest(st):
            (j2, ti2, special, bi_s, vta2) = st[:5]
            bank_s = cx.pb[bi_s]
            par = tile_par[ti2]
            mc_prev = mstat[:, ti2 * 2 + par:ti2 * 2 + par + 1]
            mc_new = mstat[:, ti2 * 2 + 1 - par:ti2 * 2 + 2 - par]
            tile_par[ti2] = 1 - par
            sidx = st[5]
            ve.wait_ge(cx.pe.h, st[6])
            if special:
                if pipe_state["mpp"] is not None:
                    ve.wait_ge(cx.pe.h, pipe_state["mpp"])
                msl = (m2d_sb[:, 0:128] if special == 1
                       else m2d_sb[:, 128:256])
                cx.ve_edge(ve.tensor_tensor(p_sb, bank_s[:, 0:128], msl,
                                            AL.add))
                srcb = p_sb
            else:
                srcb = bank_s[:, 0:128]
            cx.ve_edge(ve.reduce_max(colA, srcb, axis=AX.X))
            cx.ve_edge(ve.scalar_tensor_tensor(
                colA, colA, SCL, bias_sb[:, sidx:sidx + 1],
                AL.mult, AL.add))
            cx.ve_edge(ve.tensor_tensor(mc_new, mc_prev, colA, AL.max))
            ins = ve.tensor_sub(colB, bias_sb[:, sidx:sidx + 1], mc_new)
            m_mn = cx.dv.inc(ins, 1)
            cx.ve_edge(ins)
            se.wait_ge(cx.dv.h, m_mn)
            ins = se.activation(colC, mc_new, AF.Exp, bias=mc_prev,
                                scale=-1.0)
            if not special and pipe_state["mpp"] is not None:
                se.wait_ge(cx.pe.h, pipe_state["mpp"])
            ins = se.activation(p_sb, srcb, AF.Exp, bias=colB, scale=SCL)
            m_p = cx.ac.inc(ins, 1)
            cx.bank_consumed(bi_s, cx.ac, m_p)
            ve.wait_ge(cx.ac.h, m_p)
            cx.ve_edge(ve.reduce_sum(colD, p_sb, axis=AX.X))
            ins = ve.scalar_tensor_tensor(
                lstat[:, ti2:ti2 + 1], lstat[:, ti2:ti2 + 1], colC, colD,
                AL.mult, AL.add)
            m_l = cx.dv.inc(ins, 1)
            cx.ve_edge(ins)
            bank_t = cx.use_bank(2)
            te.wait_ge(cx.ac.h, m_p)
            ins = te.transpose(bank_t[:, 0:128], p_sb, K["identc"])
            m_t = cx.pe.inc(ins, 1)
            pipe_state["mpp"] = m_t
            ve.wait_ge(cx.pe.h, m_t)
            ins = ve.tensor_copy(pt_sb, bank_t[:, 0:128])
            m_ptc = cx.dv.inc(ins, 1)
            cx.bank_consumed(2, cx.dv, m_ptc)
            bank_a = cx.use_bank(3)
            bank_b = cx.use_bank(4)
            te.wait_ge(cx.dv.h, m_ptc)
            te.matmul(bank_a[:, 0:512], pt_sb, vta2[:, 0:512],
                      start=True, stop=True)
            ins = te.matmul(bank_b[:, 0:512], pt_sb, vta2[:, 512:1024],
                            start=True, stop=True)
            m_av = cx.pe.inc(ins, 1)
            kv_guard[j2] = m_av
            ve.wait_ge(cx.pe.h, m_av)
            ve.scalar_tensor_tensor(
                oblk3[:, ti2, 0:512], oblk3[:, ti2, 0:512], colC,
                bank_a[:, 0:512], AL.mult, AL.add)
            ins = ve.scalar_tensor_tensor(
                oblk3[:, ti2, 512:1024], oblk3[:, ti2, 512:1024], colC,
                bank_b[:, 0:512], AL.mult, AL.add)
            m_o = cx.dv.inc(ins, 1)
            cx.bank_consumed(3, cx.dv, m_o)
            cx.bank_consumed(4, cx.dv, m_o)

        for j in range(nslot):
            sl = j % 2
            kta, vta = ktb[sl], vtb[sl]
            while pending and pending[0][0] <= j - 2:
                emit_rest(pending.pop(0))
            if (j - 2) in kv_guard:
                sy.wait_ge(cx.pe.h, kv_guard[j - 2])
            cx.dma(sy, cx.ldk[sl], r3(kta),
                   sp["k_sp"].rearrange("(ec p) t -> p ec t", p=128)[
                       :, :, j * 128:(j + 1) * 128])
            m_kv = cx.dma(sy, cx.ldk[sl], vta,
                          sp["v_sp"][j * 128:(j + 1) * 128, :])
            s2_pump()
            cx.ldk[sl].wait(te)
            k3 = r3(kta)
            for ti in range(QB):
                i = bq * QB + ti
                if j > 16 + i:
                    continue
                special = 1 if j == i else (2 if j == 16 + i else 0)
                bi_s = step_idx % 2
                bank_s = cx.use_bank(bi_s)
                for dc in range(C):
                    ins = te.matmul(
                        bank_s[:, 0:128],
                        q3[:, dc, ti * 128:(ti + 1) * 128], k3[:, dc, :],
                        start=(dc == 0), stop=(dc == C - 1))
                m_s = cx.pe.inc(ins, 1)
                pending.append((j, ti, special, bi_s, vta, step_idx, m_s))
                step_idx += 1
                if len(pending) > 1:
                    emit_rest(pending.pop(0))
        while pending:
            emit_rest(pending.pop(0))
        if bq == nb_run - 1:
            while s2["split"] < len(s2_list):
                s2_pump()
        for ti in range(QB):
            i = bq * QB + ti
            cx.ve_edge(ve.reciprocal(colA, lstat[:, ti:ti + 1]))
            ins = ve.tensor_scalar_mul(oblk3[:, ti, :], oblk3[:, ti, :], colA)
            m_on = cx.dv.inc(ins, 1)
            cx.ve_edge(ins)
            te.wait_ge(cx.dv.h, m_on)
            for dc in range(C):
                bank = cx.use_bank(6)
                ins = te.transpose(bank[:, 0:128],
                                   oblk3[:, ti, dc * 128:(dc + 1) * 128],
                                   K["identc"])
                m_tt = cx.pe.inc(ins, 1)
                ve.wait_ge(cx.pe.h, m_tt)
                ins = ve.tensor_copy(xs3[:, dc, i * 128:(i + 1) * 128],
                                     bank[:, 0:128])
                m_xc = cx.dv.inc(ins, 1)
                cx.bank_consumed(6, cx.dv, m_xc)
    assert stop_after in ("attn1b", "attn1b_noret", "attn4b_noadd") or step_idx == NSTEPS

    if stop_after in ("attn1b", "attn1b_noret", "attn4b_noadd"):
        finish_stub(cx, out_d, zero=False)
        return

    m_rs1cc = None
    if False and cc is True:
        rs1x = cx.al(C)
        rs1t = cx.al(C)
        rs1own = cx.al(C)
        rs1msk = cx.al(8 * C)
        rs1scr = dict(bank=7, xsqc=cx.al(C), rowc=cx.al(4)[0:1, :],
                      repc=cx.al(2), ones=onesc, onesr=onesr)
        ins = sy.dma_start(
            out=rs1x.unsqueeze(2),
            in_=di["x_res"].rearrange("(dc p) t -> p dc t", p=128)[
                :, :, NT - 1:NT],
            allow_slow_non_contiguous=True)
        m_r1x = cx.ldc.inc(ins, 16)
        ve.wait_ge(cx.ldc.h, m_r1x)
        ins = ve.tensor_tensor(rs1t, xs3[:, :, NT - 1], rs1x, AL.add)
        m_r1t = cx.dv.inc(ins, 1)
        cx.ve_edge(ins)
        se.wait_ge(cx.dv.h, m_r1t)
        te.wait_ge(cx.dv.h, m_r1t)
        emit_row_ln(cx, rs1t, rs1own, rs1scr)
        m_rs1cc = emit_rs_send(cx, rs1own, C, sp["rs1_in"], sp["rs1_out"],
                               K, rs1msk)

    res_war = [None, None]
    for g in range(NGT):
        sl = g % 2
        if res_war[sl] is not None:
            sy.wait_ge(cx.dv.h, res_war[sl])
        elif g == 0:
            sy.wait_ge(cx.dv.h, cx.dv.v)
        ins = sy.dma_start(
            out=r3(qblk)[:, :, sl * TG:(sl + 1) * TG],
            in_=di["x_res"].rearrange("(dc p) t -> p dc t", p=128)[
                :, :, g * TG:(g + 1) * TG])
        m_ld = cx.ldq.inc(ins, 16)
        ve.wait_ge(cx.ldq.h, m_ld)
        ins = ve.tensor_tensor(xs3[:, :, g * TG:(g + 1) * TG],
                               xs3[:, :, g * TG:(g + 1) * TG],
                               r3(qblk)[:, :, sl * TG:(sl + 1) * TG], AL.add)
        res_war[sl] = cx.dv.inc(ins, 1)
    cx.release()

    if stop_after == "attn":
        finish_stub(cx, out_d)
        return

    TF = 256
    cx.mark()
    wr_sb = cx.al(C * D)
    wk2_sb = cx.al(C * D)
    wv2_sb = cx.al(C * D)
    reg = {}
    for nm in ["xn", "xnsh", "diff", "xr", "xk"]:
        reg[nm] = cx.al(C * TF)
    reg["rb"] = reg["xn"]
    reg["kb"] = reg["xnsh"]
    reg["xsq"] = reg["xr"]
    f_row2 = cx.al(3 * TF)
    f_rep = reg["xnsh"][:, 0:2 * TF]
    reg["prevc"] = cx.al(C)
    mskbuf = cx.al(8 * C)
    rowc = cx.al(4)
    xsqc = cx.al(C)
    repc = cx.al(2)
    reg["scr"] = dict(xsq=reg["xsq"], row2=f_row2[0:1, :], rep=f_rep,
                      ones=onesc, onesr=onesr, bi_stats=2, bi_rep=3)
    scr2 = dict(bank=7, xsqc=xsqc, rowc=rowc[0:1, :], repc=repc,
                ones=onesc, onesr=onesr)
    sy.wait_ge(cx.pe.h, cx.pe.v)
    sy.wait_ge(cx.dv.h, cx.dv.v)
    sy.wait_ge(cx.ac.h, cx.ac.v)
    sy.wait_ge(cx.pot.h, cx.pot.v)
    m_fw0 = load_w_hl(wr_sb, "wrT", sy, cx.ldw[0])
    m_fw1 = load_w_hl(wk2_sb, "wk2T", sy, cx.ldw[1])
    load_w_hl(wv2_sb, "wv2T", sy, cx.ldw[0])
    ve.wait_ge(cx.ac.h, cx.ac.v)
    ve.wait_ge(cx.pe.h, cx.pe.v)
    se.wait_ge(cx.dv.h, cx.dv.v)
    se.wait_ge(cx.pe.h, cx.pe.v)
    if cc is True:
        ln2own = cx.al(C)
        se.wait_ge(cx.dv.h, cx.dv.v)
        te.wait_ge(cx.dv.h, cx.dv.v)
        emit_row_ln(cx, xs3[:, :, NT - 1], ln2own, scr2)
        emit_rs(cx, ln2own, C, sp["rs1_in"], sp["rs1_out"], ln2row, K,
                mskbuf)
    elif True:
        ins = ve.memset(ln2row, 0.0)
        cx.dv.inc(ins, 1)
    cx.ldw[0].wait(te)
    cx.ldw[1].wait(te)
    emit_ffn(cx, K, wr_sb, wk2_sb, wv2_sb, xs3, ln2row, reg, T=TF)
    cx.release()

    if stop_after == "ffn1":
        finish_stub(cx, out_d)
        return

    emit_attnb(cx, K, di, sp, xs3, onesc, onesr, ln4row,
               ("p1stop" if stop_after == "ab_p1" else
                ("p1nobp" if stop_after == "ab_p1nobp" else cc)))

    if stop_after in ("ab_p1", "ab_p1nobp"):
        finish_stub(cx, out_d, zero=True)
        return

    if stop_after == "attnb":
        finish_stub(cx, out_d)
        return

    TF = 256
    cx.mark()
    wrb = cx.al(C * D)
    wkb = cx.al(C * D)
    wvb = cx.al(C * D)
    reg = {}
    for nm in ["xn", "xnsh", "diff", "xr", "xk"]:
        reg[nm] = cx.al(C * TF)
    reg["rb"] = reg["xn"]
    reg["kb"] = reg["xnsh"]
    reg["xsq"] = reg["xr"]
    f_row2 = cx.al(3 * TF)
    f_rep = reg["xnsh"][:, 0:2 * TF]
    reg["prevc"] = cx.al(C)
    reg["scr"] = dict(xsq=reg["xsq"], row2=f_row2[0:1, :], rep=f_rep,
                      ones=onesc, onesr=onesr, bi_stats=2, bi_rep=3)
    sy.wait_ge(cx.pe.h, cx.pe.v)
    sy.wait_ge(cx.dv.h, cx.dv.v)
    sy.wait_ge(cx.ac.h, cx.ac.v)
    ve.wait_ge(cx.pe.h, cx.pe.v)
    ve.wait_ge(cx.ac.h, cx.ac.v)
    se.wait_ge(cx.pe.h, cx.pe.v)
    se.wait_ge(cx.dv.h, cx.dv.v)
    load_w_hl(wrb, "wrT", sy, cx.ldw[0])
    load_w_hl(wkb, "wk2T", sy, cx.ldw[1])
    load_w_hl(wvb, "wv2T", sy, cx.ldw[0])
    cx.ldw[0].wait(te)
    cx.ldw[1].wait(te)
    emit_ffn(cx, K, wrb, wkb, wvb, xs3, ln4row, reg, T=TF)
    cx.release()

    finish_stub(cx, out_d)


def finish_stub(cx, out_d, zero=False, dump=None):
    gp = cx.e["g"]
    ve = cx.e["v"]
    xs = cx.arena.ap()[:, 0:C * NT]
    if dump is not None:
        gp.wait_ge(cx.dv.h, cx.dv.v)
        gp.wait_ge(cx.ac.h, cx.ac.v)
        for s in (cx.sps + cx.spv + [cx.spo]):
            s.wait(gp)
        flat_out = out_d.rearrange("c p t -> (c p t)")
        ins = gp.dma_start(out=flat_out,
                           in_=dump.rearrange("a b -> (a b)")[0:C * 128 * NT])
        cx.spo.inc(ins, 16)
        cx.spo.wait(gp)
        return
    if zero:
        ins = ve.memset(xs, 0.0)
        cx.dv.inc(ins, 1)
    gp.wait_ge(cx.dv.h, cx.dv.v)
    gp.wait_ge(cx.ac.h, cx.ac.v)
    ins = gp.dma_start(out=out_d.rearrange("c p t -> p c t"), in_=r3(xs))
    cx.spo.inc(ins, 16)
    for s in ([cx.ldc, cx.ldq, cx.spo] + cx.lds + cx.ldw + cx.ldk +
              cx.sps + cx.spv):
        s.wait(gp)



def host_inputs(full, core):
    b, h = core // 2, core % 2
    X = np.asarray(full["L0_attn_a_X"])
    XF = np.ascontiguousarray(X[b].T)
    d = {}
    d["x_kv"] = XF
    d["x_res"] = np.ascontiguousarray(XF[:, 2048 * h:2048 * h + 2048])
    d["wqT"] = np.ascontiguousarray(np.asarray(full["L0_attn_a_Wq"]).T)
    d["wkT"] = np.ascontiguousarray(np.asarray(full["L0_attn_a_Wk"]).T)
    d["wvT"] = np.ascontiguousarray(np.asarray(full["L0_attn_a_Wv"]).T)
    d["wrT"] = np.ascontiguousarray(np.asarray(full["L0_ffn_W_r"]).T)
    d["wk2T"] = np.ascontiguousarray(np.asarray(full["L0_ffn_W_k"]).T)
    d["wv2T"] = np.ascontiguousarray(np.asarray(full["L0_ffn_W_v"]).T)
    d["woutT"] = np.ascontiguousarray(np.asarray(full["L0_attn_b_W_out"]).T)
    d["winT"] = np.ascontiguousarray(np.asarray(full["L0_attn_b_W_in"]).T)
    d["wbT"] = np.ascontiguousarray(np.asarray(full["L0_attn_b_W_B"]).T)
    d["wcT"] = np.ascontiguousarray(np.asarray(full["L0_attn_b_W_C"]).T)

    def fm(v):
        return np.ascontiguousarray(np.asarray(v).reshape(C, 128).T)

    d["bv_row"] = np.asarray(full["L0_attn_a_bv"]).reshape(1, D)
    d["bq"] = fm(full["L0_attn_a_bq"])
    d["bk"] = fm(full["L0_attn_a_bk"])
    d["bv"] = fm(full["L0_attn_a_bv"])
    d["bout"] = fm(full["L0_attn_b_b_out"])
    d["mu_r"] = fm(full["L0_ffn_mu_r"])
    d["mu_k"] = fm(full["L0_ffn_mu_k"])
    d["lng"] = fm(full["L0_attn_b_ln_g"])
    d["lnb"] = fm(full["L0_attn_b_ln_b"])
    d["dskip"] = fm(full["L0_attn_b_D_skip"])
    d["logdt"] = fm(full["L0_attn_b_log_dt"])
    d["b_in"] = np.ascontiguousarray(
        np.asarray(full["L0_attn_b_b_in"]).reshape(2 * C, 128).T)
    al = np.asarray(full["L0_attn_b_A_log"]).reshape(C, 128, S)
    d["alog"] = np.ascontiguousarray(al.transpose(1, 0, 2).reshape(128, C * S))

    bias_cols = np.zeros((128, NSTEPS), np.float32)
    for sidx, (bq_, i, j) in enumerate(ATT_STEPS):
        g = 16 * h + i
        if j == i or j == 16 + i:
            val = 0.0
        elif j > g:
            val = -BIGB
        else:
            val = 0.0
        bias_cols[:, sidx] = val
    d["bias_cols"] = bias_cols
    diag = np.where(np.arange(128)[:, None] >= np.arange(128)[None, :],
                    0.0, -BIG2).astype(np.float32)
    m2d = np.zeros((128, 256), np.float32)
    if h == 0:
        m2d[:, 0:128] = diag
        m2d[:, 128:256] = -BIG2
    else:
        m2d[:, 128:256] = diag
    d["m2d"] = m2d
    d["gate1"] = np.full((128, 1), float(h), np.float32)
    slot8 = np.zeros((128, 8), np.float32)
    slot8[:, core ^ 1] = 1.0
    d["slot8"] = slot8
    d["iota_t"] = np.broadcast_to(
        np.arange(128, dtype=np.float32)[None, :], (128, 128)).copy()
    d["iota_r"] = np.broadcast_to(
        (128.0 - np.arange(128, dtype=np.float32))[None, :], (128, 128)).copy()
    d["iota_p"] = np.broadcast_to(
        (1.0 + np.arange(128, dtype=np.float32))[None, :], (128, 128)).copy()
    d["arev"] = np.eye(128, dtype=np.float32)[::-1].copy()
    d["identc"] = np.eye(128, dtype=np.float32)
    return {k: np.ascontiguousarray(v, dtype=np.float32) for k, v in d.items()}


def kernel(**inputs):
    nc = build_program()
    in_maps = [host_inputs(inputs, c) for c in range(8)]
    res = run_bass_kernel_spmd(nc, in_maps, core_ids=list(range(8)))
    out = np.empty((4, 4096, 1024), np.float32)
    for c in range(8):
        b, h = c // 2, c % 2
        o = np.asarray(res.results[c]["out_fm"])
        out[b, 2048 * h:2048 * h + 2048, :] = o.reshape(1024, 2048).T
    return out


# version 22
# speedup vs baseline: 1.0118x; 1.0118x over previous
import numpy as np
import concourse.bass as bass
import concourse.mybir as mybir
from concourse.bass_utils import run_bass_kernel_spmd

F32 = mybir.dt.float32
BF16 = mybir.dt.bfloat16
AL = mybir.AluOpType
AF = mybir.ActivationFunctionType
AX = mybir.AxisListType

D = 1024
C = 8
NT = 2048
NK = 4096
TG = 256
NGK = NK // TG
NGT = NT // TG
NQT = NT // 128
QB = 4
NB = NQT // QB
S = 16
NCH = 32
BIGB = 30000.0
BIG2 = 3.0e6
SCL = 1.0 / 32.0


def attn_steps():
    steps = []
    for bq in range(NB):
        nslot = 20 + 4 * bq
        for j in range(nslot):
            for ti in range(QB):
                i = bq * QB + ti
                if j <= 16 + i:
                    steps.append((bq, i, j))
    return steps


ATT_STEPS = attn_steps()
NSTEPS = len(ATT_STEPS)


class RCmd:
    __slots__ = ("name", "a", "k", "incs")

    def __init__(self, name, a, k):
        self.name = name
        self.a = a
        self.k = k
        self.incs = []

    def then_inc(self, sem_h, n):
        self.incs.append((sem_h, n))
        return self


class REng:

    def __init__(self):
        self.cmds = []

    def __getattr__(self, name):
        def call(*a, **k):
            c = RCmd(name, a, k)
            self.cmds.append(c)
            return c
        return call

    def replay(self, eng):
        for c in self.cmds:
            ins = getattr(eng, c.name)(*c.a, **c.k)
            for (sh, n) in c.incs:
                ins.then_inc(sh, n)


class Sem:
    def __init__(self, nc, name):
        self.h = nc.alloc_semaphore(name)
        self.v = 0

    def inc(self, ins, k):
        ins.then_inc(self.h, k)
        self.v += k
        return self.v

    def wait(self, eng):
        eng.wait_ge(self.h, self.v)


class Ctx:
    def __init__(self, nc):
        self.nc = nc
        self.e = {}
        self.ldc = Sem(nc, "s_ldc")
        self.lds = [Sem(nc, f"s_lds{i}") for i in range(2)]
        self.ldw = [Sem(nc, f"s_ldw{i}") for i in range(2)]
        self.ldk = [Sem(nc, f"s_ldk{i}") for i in range(2)]
        self.ldq = Sem(nc, "s_ldq")
        self.sps = [Sem(nc, f"s_sps{i}") for i in range(2)]
        self.spv = [Sem(nc, f"s_spv{i}") for i in range(2)]
        self.spo = Sem(nc, "s_spo")
        self.pin = Sem(nc, "s_pin")
        self.pot = Sem(nc, "s_pot")
        self.pe = Sem(nc, "s_pe")
        self.dv = Sem(nc, "s_dv")
        self.ac = Sem(nc, "s_ac")
        self.cc = Sem(nc, "s_cc")
        self.dq = Sem(nc, "s_dq")
        self.sem_of = {}
        for s in [self.ldc, self.ldq, self.spo, self.pin, self.pot, self.pe,
                  self.dv, self.ac,
                  self.cc, self.dq] + self.lds + self.ldw + self.ldk +                  self.sps + self.spv:
            self.sem_of[id(s.h)] = s
        self.AW = 53200
        self.arena = nc.alloc_sbuf_tensor("arena", [128, self.AW], F32)
        self.off = 0
        self.marks = []
        self.pb = [nc.alloc_psum_tensor(f"pb{i}", [128, 512], F32)
                   for i in range(8)]
        self.bank_free = [None] * 8

    def al(self, words):
        assert self.off + words <= self.AW, (self.off, words)
        ap = self.arena.ap()[:, self.off:self.off + words]
        self.off += words
        return ap

    def mark(self):
        self.marks.append(self.off)

    def release(self):
        self.off = self.marks.pop()

    def use_bank(self, i):
        bf = self.bank_free[i]
        if bf is not None:
            self.e["t"].wait_ge(bf[0].h, bf[1])
            self.bank_free[i] = None
        return self.pb[i]

    def bank_consumed(self, i, sem, val):
        self.bank_free[i] = (sem, val)

    def dma(self, eng, sem, out, in_, **kw):
        eng.wait_ge(sem.h, sem.v)
        ins = eng.dma_start(out=out, in_=in_, **kw)
        return sem.inc(ins, 16)

    def ve_edge(self, ins):
        if getattr(ins, "incs", None):
            sh, _ = ins.incs[0]
            sem = self.sem_of.get(id(sh))
            assert sem is not None, "ve_edge: unknown sem on instruction"
            self.e["v"].wait_ge(sem.h, sem.v)
        else:
            v = self.dq.inc(ins, 1)
            self.e["v"].wait_ge(self.dq.h, v)



def r3(ap, c=C):
    return ap.rearrange("p (c t) -> p c t", c=c)


def bf(ap):
    return ap.bitcast(BF16)


def emit_split(cx, x3f, xh3, xl3):
    ve = cx.e["v"]
    ins = ve.tensor_copy(xh3, x3f)
    cx.ve_edge(ins)
    ins = ve.tensor_tensor(xl3, x3f, xh3, AL.subtract)
    m = cx.dv.inc(ins, 1)
    return m


def emit_proj3(cx, T, wp, x_pair, out_ec, bi, nec=C, bias=None, act=AF.Copy,
               e_tot=D):
    te, se = cx.e["t"], cx.e["s"]
    xh3, xl3 = x_pair
    w4 = wp.rearrange("p (two dc e) -> p two dc e", two=2, dc=C)
    m_ev = None
    for ec in range(nec):
        bank = cx.use_bank(bi)
        first = True
        for dc in range(C):
            wh = w4[:, 0, dc, 128 * ec:128 * (ec + 1)]
            wl = w4[:, 1, dc, 128 * ec:128 * (ec + 1)]
            for (wt, xt) in ((wh, xh3), (wh, xl3), (wl, xh3)):
                last = (dc == C - 1) and (wt is wl)
                ins = te.matmul(bank[:, 0:T], wt, xt[:, dc, :],
                                start=first, stop=last)
                first = False
        m_pe = cx.pe.inc(ins, 1)
        b = bias[:, ec:ec + 1] if bias is not None else 0.0
        if act == AF.Copy:
            ve = cx.e["v"]
            ve.wait_ge(cx.pe.h, m_pe)
            if bias is not None:
                ins = ve.tensor_scalar_add(out_ec(ec), bank[:, 0:T], b)
            else:
                ins = ve.tensor_copy(out_ec(ec), bank[:, 0:T])
            m_ev = cx.dv.inc(ins, 1)
            cx.bank_consumed(bi, cx.dv, m_ev)
        else:
            se.wait_ge(cx.pe.h, m_pe)
            ins = se.activation(out_ec(ec), bank[:, 0:T], act, bias=b)
            m_ev = cx.ac.inc(ins, 1)
            cx.bank_consumed(bi, cx.ac, m_ev)
    return m_ev



def emit_ln(cx, T, x_src3, xn_out3, scr, gamma=None, beta=None):
    te, ve, se = cx.e["t"], cx.e["v"], cx.e["s"]
    xsq3 = r3(scr["xsq"][:, :C * T])
    ins = se.activation(xsq3, x_src3, AF.Square)
    m_sq = cx.ac.inc(ins, 1)

    pb = cx.use_bank(scr["bi_stats"])
    te.wait_ge(cx.ac.h, m_sq)
    for ci in range(C):
        te.matmul(pb[0:1, 0:T], scr["ones"], x_src3[:, ci, :],
                  start=(ci == 0), stop=False)
        ins = te.matmul(pb[0:1, T:2 * T], scr["ones"], xsq3[:, ci, :],
                        start=False, stop=(ci == C - 1))
    m_pe = cx.pe.inc(ins, 1)

    row = scr["row2"]
    mu = row[0:1, 0:T]
    var = row[0:1, T:2 * T]
    tmp = row[0:1, 2 * T:3 * T]
    ve.wait_ge(cx.pe.h, m_pe)
    ins = ve.tensor_scalar_mul(row[0:1, 0:2 * T], pb[0:1, 0:2 * T], 1.0 / D)
    m_dv = cx.dv.inc(ins, 1)
    cx.bank_consumed(scr["bi_stats"], cx.dv, m_dv)
    cx.ve_edge(ins)
    cx.ve_edge(ve.tensor_mul(tmp, mu, mu))
    cx.ve_edge(ve.tensor_sub(var, var, tmp))
    cx.ve_edge(ve.tensor_scalar_add(var, var, 1e-5))
    ins = ve.reciprocal(tmp, var)
    m_rc = cx.dv.inc(ins, 1)
    rstd = var
    se.wait_ge(cx.dv.h, m_rc)
    ins = se.sqrt(rstd, tmp)
    m_rs = cx.ac.inc(ins, 1)
    ve.wait_ge(cx.ac.h, m_rs)
    ins = ve.tensor_mul(mu, mu, rstd)
    m_mr = cx.dv.inc(ins, 1)
    cx.ve_edge(ins)

    pb2 = cx.use_bank(scr["bi_rep"])
    te.wait_ge(cx.dv.h, m_mr)
    te.matmul(pb2[:, 0:T], scr["onesr"], rstd, start=True, stop=False)
    ins = te.matmul(pb2[:, T:2 * T], scr["onesr"], mu, start=False, stop=True)
    m_rep = cx.pe.inc(ins, 1)
    ve.wait_ge(cx.pe.h, m_rep)
    rep = scr["rep"]
    ins = ve.tensor_copy(rep[:, 0:2 * T], pb2[:, 0:2 * T])
    m_repc = cx.dv.inc(ins, 1)
    cx.bank_consumed(scr["bi_rep"], cx.dv, m_repc)
    cx.ve_edge(ins)

    rrep = rep[:, 0:T].unsqueeze(1).broadcast_to([128, C, T])
    mrep = rep[:, T:2 * T].unsqueeze(1).broadcast_to([128, C, T])
    cx.ve_edge(ve.tensor_tensor(xn_out3, x_src3, rrep, AL.mult))
    ins = ve.tensor_tensor(xn_out3, xn_out3, mrep, AL.subtract)
    if gamma is not None:
        cx.ve_edge(ins)
        for ci in range(C):
            ins = ve.scalar_tensor_tensor(
                xn_out3[:, ci, :], xn_out3[:, ci, :], gamma[:, ci:ci + 1],
                beta[:, ci:ci + 1].broadcast_to([128, T]), AL.mult, AL.add)
    m_xn = cx.dv.inc(ins, 1)
    cx.ve_edge(ins)
    return m_xn


def make_ln_stages(cx, T, src3, out3, scr, gamma=None, beta=None):
    te, ve, se = cx.e["t"], cx.e["v"], cx.e["s"]
    xsq3 = r3(scr["xsq"][:, :C * T])
    row2 = scr["row2"]
    rep = scr["rep"]
    st = {}

    def sq(guard=None, dv_guard=None):
        if guard is not None:
            se.wait_ge(cx.pe.h, guard)
        if dv_guard is not None:
            se.wait_ge(cx.dv.h, dv_guard)
        ins = se.activation(xsq3, src3, AF.Square)
        st["sq"] = cx.ac.inc(ins, 1)
        return st["sq"]

    def stats():
        pb = cx.use_bank(scr["bi_stats"])
        te.wait_ge(cx.ac.h, st["sq"])
        for ci in range(C):
            te.matmul(pb[0:1, 0:T], scr["ones"], src3[:, ci, :],
                      start=(ci == 0), stop=False)
            ins = te.matmul(pb[0:1, T:2 * T], scr["ones"], xsq3[:, ci, :],
                            start=False, stop=(ci == C - 1))
        st["stats"] = cx.pe.inc(ins, 1)
        st["pb"] = pb
        return st["stats"]

    def row():
        mu = row2[0:1, 0:T]
        var = row2[0:1, T:2 * T]
        tmp = row2[0:1, 2 * T:3 * T]
        ve.wait_ge(cx.pe.h, st["stats"])
        ins = ve.tensor_scalar_mul(row2[0:1, 0:2 * T], st["pb"][0:1, 0:2 * T],
                                   1.0 / D)
        m_dv = cx.dv.inc(ins, 1)
        cx.bank_consumed(scr["bi_stats"], cx.dv, m_dv)
        cx.ve_edge(ins)
        cx.ve_edge(ve.tensor_mul(tmp, mu, mu))
        cx.ve_edge(ve.tensor_sub(var, var, tmp))
        cx.ve_edge(ve.tensor_scalar_add(var, var, 1e-5))
        ins = ve.reciprocal(tmp, var)
        m_rc = cx.dv.inc(ins, 1)
        rstd = var
        se.wait_ge(cx.dv.h, m_rc)
        ins = se.sqrt(rstd, tmp)
        m_rs = cx.ac.inc(ins, 1)
        ve.wait_ge(cx.ac.h, m_rs)
        ins = ve.tensor_mul(mu, mu, rstd)
        st["row"] = cx.dv.inc(ins, 1)
        cx.ve_edge(ins)

    def repmm():
        mu = row2[0:1, 0:T]
        rstd = row2[0:1, T:2 * T]
        pb2 = cx.use_bank(scr["bi_rep"])
        te.wait_ge(cx.dv.h, st["row"])
        te.matmul(pb2[:, 0:T], scr["onesr"], rstd, start=True, stop=False)
        ins = te.matmul(pb2[:, T:2 * T], scr["onesr"], mu, start=False,
                        stop=True)
        st["rep"] = cx.pe.inc(ins, 1)
        st["pb2"] = pb2

    def w_repcopy(guard=None):
        ve.wait_ge(cx.pe.h, st["rep"])
        if guard is not None:
            ve.wait_ge(cx.pe.h, guard)
        ins = ve.tensor_copy(rep[:, 0:2 * T], st["pb2"][:, 0:2 * T])
        m_rc = cx.dv.inc(ins, 1)
        cx.bank_consumed(scr["bi_rep"], cx.dv, m_rc)
        cx.ve_edge(ins)

    rrep = rep[:, 0:T].unsqueeze(1).broadcast_to([128, C, T])
    mrep = rep[:, T:2 * T].unsqueeze(1).broadcast_to([128, C, T])

    def w_mult1():
        cx.ve_edge(ve.tensor_tensor(out3, src3, rrep, AL.mult))

    def w_mult2():
        ins = ve.tensor_tensor(out3, out3, mrep, AL.subtract)
        st["xn"] = cx.dv.inc(ins, 1)
        cx.ve_edge(ins)

    def w_gb():
        ins = None
        for ci in range(C):
            ins = ve.scalar_tensor_tensor(
                out3[:, ci, :], out3[:, ci, :], gamma[:, ci:ci + 1],
                beta[:, ci:ci + 1].broadcast_to([128, T]), AL.mult, AL.add)
        st["xn"] = cx.dv.inc(ins, 1)
        cx.ve_edge(ins)

    E = [w_repcopy, w_mult1, w_mult2]
    if gamma is not None:
        E.append(w_gb)
    return dict(sq=sq, stats=stats, row=row, repmm=repmm, E=E, st=st)


def emit_proj(cx, T, w_sb, xn3, out_ec, bi, nec=C, bias=None, act=AF.Copy,
              pre_wait=None):
    te, se = cx.e["t"], cx.e["s"]
    w3 = w_sb.rearrange("p (dc e) -> p dc e", dc=C)
    m_ev = None
    for ec in range(nec):
        bank = cx.use_bank(bi)
        if pre_wait is not None:
            pre_wait()
            pre_wait = None
        for dc in range(C):
            ins = te.matmul(bank[:, 0:T], w3[:, dc, 128 * ec:128 * (ec + 1)],
                            xn3[:, dc, :], start=(dc == 0), stop=(dc == C - 1))
        m_pe = cx.pe.inc(ins, 1)
        b = bias[:, ec:ec + 1] if bias is not None else 0.0
        if act == AF.Copy:
            ve = cx.e["v"]
            ve.wait_ge(cx.pe.h, m_pe)
            if bias is not None:
                ins = ve.tensor_scalar_add(out_ec(ec), bank[:, 0:T], b)
            else:
                ins = ve.tensor_copy(out_ec(ec), bank[:, 0:T])
            m_ev = cx.dv.inc(ins, 1)
            cx.bank_consumed(bi, cx.dv, m_ev)
        else:
            se.wait_ge(cx.pe.h, m_pe)
            ins = se.activation(out_ec(ec), bank[:, 0:T], act, bias=b)
            m_ev = cx.ac.inc(ins, 1)
            cx.bank_consumed(bi, cx.ac, m_ev)
    return m_ev




def emit_row_ln(cx, xcol, out_col, scr2):
    te, ve, se = cx.e["t"], cx.e["v"], cx.e["s"]
    ins = se.activation(scr2["xsqc"], xcol, AF.Square)
    m_sq = cx.ac.inc(ins, 1)
    pb = cx.use_bank(scr2["bank"])
    te.wait_ge(cx.ac.h, m_sq)
    te.matmul(pb[0:1, 0:C], scr2["ones"], xcol, start=True, stop=False)
    ins = te.matmul(pb[0:1, C:2 * C], scr2["ones"], scr2["xsqc"],
                    start=False, stop=True)
    m_pe = cx.pe.inc(ins, 1)
    row = scr2["rowc"]
    ve.wait_ge(cx.pe.h, m_pe)
    cx.ve_edge(ve.reduce_sum(row[0:1, 0:1], pb[0:1, 0:C], axis=AX.X))
    ins = ve.reduce_sum(row[0:1, 1:2], pb[0:1, C:2 * C], axis=AX.X)
    m_dv = cx.dv.inc(ins, 1)
    cx.bank_consumed(scr2["bank"], cx.dv, m_dv)
    cx.ve_edge(ins)
    mu = row[0:1, 0:1]
    s2 = row[0:1, 1:2]
    tmp = row[0:1, 2:3]
    cx.ve_edge(ve.tensor_scalar_mul(mu, mu, 1.0 / D))
    cx.ve_edge(ve.tensor_scalar_mul(s2, s2, 1.0 / D))
    cx.ve_edge(ve.tensor_mul(tmp, mu, mu))
    cx.ve_edge(ve.tensor_sub(s2, s2, tmp))
    cx.ve_edge(ve.tensor_scalar_add(s2, s2, 1e-5))
    ins = ve.reciprocal(tmp, s2)
    m_rc = cx.dv.inc(ins, 1)
    se.wait_ge(cx.dv.h, m_rc)
    ins = se.sqrt(s2, tmp)
    m_rs = cx.ac.inc(ins, 1)
    pb2 = cx.use_bank(scr2["bank"])
    te.wait_ge(cx.ac.h, m_rs)
    ins = te.matmul(pb2[:, 0:2], scr2["onesr"], row[0:1, 0:2],
                    start=True, stop=True)
    m_rep = cx.pe.inc(ins, 1)
    ve.wait_ge(cx.pe.h, m_rep)
    ins = ve.tensor_copy(scr2["repc"], pb2[:, 0:2])
    m_rpc = cx.dv.inc(ins, 1)
    cx.bank_consumed(scr2["bank"], cx.dv, m_rpc)
    cx.ve_edge(ins)
    ins = ve.scalar_tensor_tensor(
        out_col, xcol, scr2["repc"][:, 0:1],
        scr2["repc"][:, 1:2].broadcast_to([128, C]), AL.subtract, AL.mult)
    m_o = cx.dv.inc(ins, 1)
    cx.ve_edge(ins)
    return m_o


def emit_rs_send(cx, payload, nslots_words, rs_in, rs_out, K, mskbuf):
    ve, gp = cx.e["v"], cx.e["g"]
    W = nslots_words
    mb3 = mskbuf.rearrange("p (s w) -> p s w", s=8)
    for k in range(8):
        ins = ve.tensor_scalar_mul(mb3[:, k, :], payload,
                                   K["slot8"][:, k:k + 1])
    m_msk = cx.dv.inc(ins, 1)
    cx.ve_edge(ins)
    gp.wait_ge(cx.dv.h, m_msk)
    ins = gp.dma_start(
        out=rs_in.rearrange("s (p w) -> p s w", p=128),
        in_=mb3)
    m_in = cx.spo.inc(ins, 16)
    gp.wait_ge(cx.spo.h, m_in)
    ins = gp.collective_compute(
        "ReduceScatter", AL.add, replica_groups=[list(range(8))],
        ins=[rs_in.opt()], outs=[rs_out.opt()])
    return cx.cc.inc(ins, 1)


def emit_rs_recv(cx, rs_out, recv_buf, K, m_cc):
    ve, sy = cx.e["v"], cx.e["y"]
    sy.wait_ge(cx.cc.h, m_cc)
    ins = sy.dma_start(out=recv_buf.unsqueeze(1),
                       in_=rs_out.rearrange("o (p w) -> p o w", p=128))
    m_rv = cx.ldc.inc(ins, 16)
    cx.ldc.wait(ve)
    ins = ve.tensor_scalar_mul(recv_buf, recv_buf, K["gate1"][:, 0:1])
    m_g = cx.dv.inc(ins, 1)
    cx.ve_edge(ins)
    return m_g


def emit_rs(cx, payload, nslots_words, rs_in, rs_out, recv_buf, K, mskbuf):
    m_cc = emit_rs_send(cx, payload, nslots_words, rs_in, rs_out, K, mskbuf)
    return emit_rs_recv(cx, rs_out, recv_buf, K, m_cc)


def emit_ffn(cx, K, wr_sb, wk2_sb, wv2_sb, xs3, bound_col, reg, T=128):
    te, ve, se, gp, sy = (cx.e["t"], cx.e["v"], cx.e["s"], cx.e["g"],
                          cx.e["y"])

    def rr(ap):
        return ap[:, 0:C * T].rearrange("p (c t) -> p c t", c=C)
    scr = reg["scr"]
    B_xn, B_sh, B_xr = reg["xn"], reg["xnsh"], reg["xr"]
    B_kp, B_r = reg["xk"], reg["diff"]
    xn3 = rr(B_xn)
    xnsh3 = rr(B_sh)
    xr3 = rr(B_xr)
    xsq3 = rr(B_xr)
    kb3 = rr(B_sh)
    rb3 = rr(B_r)
    xrh3 = r3(bf(B_xn[:, 0:C * T // 2]))
    xrl3 = r3(bf(B_xn[:, C * T // 2:C * T]))
    xkh3 = r3(bf(B_xr[:, 0:C * T // 2]))
    xkl3 = r3(bf(B_xr[:, C * T // 2:C * T]))
    kah3 = r3(bf(B_kp[:, 0:C * T // 2]))
    kal3 = r3(bf(B_kp[:, C * T // 2:C * T]))
    prevc = reg["prevc"]
    row2 = scr["row2"]
    rep = scr["rep"]
    onesv = scr["ones"]
    onesr_v = scr["onesr"]
    NG = NT // T
    st = {"kmm": None, "sq": None, "stats": None, "rep": None,
          "sxr": None, "sxk": None}

    def stage_sq(g):
        if st["kmm"] is not None:
            se.wait_ge(cx.pe.h, st["kmm"])
        ins = se.activation(xsq3, xs3[:, :, g * T:(g + 1) * T], AF.Square)
        st["sq"] = cx.ac.inc(ins, 1)

    def stage_stats(g):
        pb = cx.use_bank(scr["bi_stats"])
        te.wait_ge(cx.ac.h, st["sq"])
        xsrc3 = xs3[:, :, g * T:(g + 1) * T]
        for ci in range(C):
            te.matmul(pb[0:1, 0:T], onesv, xsrc3[:, ci, :],
                      start=(ci == 0), stop=False)
            ins = te.matmul(pb[0:1, T:2 * T], onesv, xsq3[:, ci, :],
                            start=False, stop=(ci == C - 1))
        st["stats"] = (cx.pe.inc(ins, 1), pb)

    def stage_row():
        m_pe, pb = st["stats"]
        mu = row2[0:1, 0:T]
        var = row2[0:1, T:2 * T]
        tmp = row2[0:1, 2 * T:3 * T]
        ve.wait_ge(cx.pe.h, m_pe)
        ins = ve.tensor_scalar_mul(row2[0:1, 0:2 * T], pb[0:1, 0:2 * T],
                                   1.0 / D)
        m_dv = cx.dv.inc(ins, 1)
        cx.bank_consumed(scr["bi_stats"], cx.dv, m_dv)
        cx.ve_edge(ins)
        cx.ve_edge(ve.tensor_mul(tmp, mu, mu))
        cx.ve_edge(ve.tensor_sub(var, var, tmp))
        cx.ve_edge(ve.tensor_scalar_add(var, var, 1e-5))
        ins = ve.reciprocal(tmp, var)
        m_rc = cx.dv.inc(ins, 1)
        rstd = var
        se.wait_ge(cx.dv.h, m_rc)
        ins = se.sqrt(rstd, tmp)
        m_rs = cx.ac.inc(ins, 1)
        ve.wait_ge(cx.ac.h, m_rs)
        ins = ve.tensor_mul(mu, mu, rstd)
        st["row"] = cx.dv.inc(ins, 1)
        cx.ve_edge(ins)

    def stage_repmm():
        mu = row2[0:1, 0:T]
        rstd = row2[0:1, T:2 * T]
        pb2 = cx.use_bank(scr["bi_rep"])
        te.wait_ge(cx.dv.h, st["row"])
        te.matmul(pb2[:, 0:T], onesr_v, rstd, start=True, stop=False)
        ins = te.matmul(pb2[:, T:2 * T], onesr_v, mu, start=False, stop=True)
        st["rep"] = (cx.pe.inc(ins, 1), pb2)

    def make_E(g):
        xsrc3 = xs3[:, :, g * T:(g + 1) * T]
        rrep = rep[:, 0:T].unsqueeze(1).broadcast_to([128, C, T])
        mrep = rep[:, T:2 * T].unsqueeze(1).broadcast_to([128, C, T])

        def w_repcopy():
            m_pe, pb2 = st["rep"]
            ve.wait_ge(cx.pe.h, m_pe)
            ins = ve.tensor_copy(rep[:, 0:2 * T], pb2[:, 0:2 * T])
            m_rc = cx.dv.inc(ins, 1)
            cx.bank_consumed(scr["bi_rep"], cx.dv, m_rc)
            cx.ve_edge(ins)

        def w_xn1():
            cx.ve_edge(ve.tensor_tensor(xn3, xsrc3, rrep, AL.mult))

        def w_xn2():
            cx.ve_edge(ve.tensor_tensor(xn3, xn3, mrep, AL.subtract))

        def w_shift():
            cx.ve_edge(ve.tensor_copy(xnsh3[:, :, 1:T], xn3[:, :, 0:T - 1]))
            if g == 0:
                ins = ve.tensor_copy(xnsh3[:, :, 0:1],
                                     bound_col.unsqueeze(2))
            else:
                ins = ve.tensor_copy(xnsh3[:, :, 0:1], prevc.unsqueeze(2))
            cx.ve_edge(ins)
            cx.ve_edge(ve.tensor_copy(prevc, xn3[:, :, T - 1]))

        def w_diff():
            cx.ve_edge(ve.tensor_tensor(xn3, xn3, xnsh3, AL.subtract))

        def w_mixr():
            ins = None
            for c in range(C):
                ins = ve.scalar_tensor_tensor(
                    xr3[:, c, :], xn3[:, c, :], K["mu_r"][:, c:c + 1],
                    xnsh3[:, c, :], AL.mult, AL.add)
            cx.ve_edge(ins)

        def w_mixk():
            ins = None
            for c in range(C):
                ins = ve.scalar_tensor_tensor(
                    xnsh3[:, c, :], xn3[:, c, :], K["mu_k"][:, c:c + 1],
                    xnsh3[:, c, :], AL.mult, AL.add)
            cx.ve_edge(ins)

        return [w_repcopy, w_xn1, w_xn2, w_shift, w_diff, w_mixr, w_mixk]

    def post_splits(g):
        st["sxr"] = emit_split(cx, xr3, xrh3, xrl3)
        st["sxk"] = emit_split(cx, rr(B_sh), xkh3, xkl3)

    stage_sq(0)
    stage_stats(0)
    stage_row()
    stage_repmm()
    for w in make_E(0):
        w()
    post_splits(0)

    for g in range(NG):
        xsrc3 = xs3[:, :, g * T:(g + 1) * T]
        te.wait_ge(cx.dv.h, st["sxr"])
        emit_proj3(cx, T, bf(wr_sb), (xrh3, xrl3), lambda ec: rb3[:, ec, :],
                   4, act=AF.Sigmoid)
        te.wait_ge(cx.dv.h, st["sxk"])
        emit_proj3(cx, T, bf(wk2_sb), (xkh3, xkl3),
                   lambda ec: kb3[:, ec, :], 5, act=AF.Relu)
        st["kmm"] = cx.pe.v
        ins = se.activation(kb3, kb3, AF.Square)
        m_kact = cx.ac.inc(ins, 1)
        if g + 1 < NG:
            stage_sq(g + 1)
        ve.wait_ge(cx.ac.h, m_kact)
        m_ska = emit_split(cx, kb3, kah3, kal3)
        te.wait_ge(cx.dv.h, m_ska)
        E = make_E(g + 1) if g + 1 < NG else []
        wv4 = bf(wv2_sb).rearrange("p (two dc e) -> p two dc e", two=2, dc=C)
        for ec in range(C):
            bank = cx.use_bank(6)
            first = True
            for dc in range(C):
                wh = wv4[:, 0, dc, 128 * ec:128 * (ec + 1)]
                wl = wv4[:, 1, dc, 128 * ec:128 * (ec + 1)]
                for pi, (wt, xt3) in enumerate(((wh, kah3), (wh, kal3),
                                               (wl, kah3))):
                    last = (dc == C - 1) and pi == 2
                    ins = te.matmul(bank[:, 0:T], wt, xt3[:, dc, :],
                                    start=first, stop=last)
                    first = False
            m_pe = cx.pe.inc(ins, 1)
            if g + 1 < NG and ec == 1:
                stage_stats(g + 1)
            if g + 1 < NG and ec == 3:
                stage_repmm()
            ve.wait_ge(cx.pe.h, m_pe)
            ins = ve.tensor_mul(bank[:, 0:T], bank[:, 0:T], rb3[:, ec, :])
            cx.ve_edge(ins)
            ins = ve.tensor_tensor(xsrc3[:, ec, :], xsrc3[:, ec, :],
                                   bank[:, 0:T], AL.add)
            m_ev = cx.dv.inc(ins, 1)
            cx.ve_edge(ins)
            cx.bank_consumed(6, cx.dv, m_ev)
            if g + 1 < NG:
                if ec == 2:
                    stage_row()
                elif ec >= 3 and ec - 3 < len(E):
                    E[ec - 3]()
        if g + 1 < NG:
            for w in E[5:]:
                w()
            st["sxr"] = emit_split(cx, xr3, xrh3, xrl3)
            st["sxk"] = emit_split(cx, rr(B_sh), xkh3, xkl3)


def emit_attnb(cx, K, di, sp, xs3, onesc, onesr, ln4row, cc):
    te, ve, se, gp, sy = (cx.e["t"], cx.e["v"], cx.e["s"], cx.e["g"],
                          cx.e["y"])
    T = TG
    cx.mark()
    bproj = cx.al(NCH * S)

    def load_w2(dst, name, necol):
        ins = sy.dma_start(
            out=dst.rearrange("p (dc e) -> p dc e", dc=C),
            in_=di[name].rearrange("(dc p) e -> p dc e", p=128))
        return cx.ldw[0].inc(ins, 16)

    cx.mark()
    win_sb = cx.al(C * 2 * D)
    xn = cx.al(C * T)
    xsq = cx.al(C * T)
    row2 = cx.al(3 * T)
    rep = cx.al(2 * T)
    xp_g = cx.al(C * T)
    z_g = cx.al(C * T)
    wb_sb = cx.al(C * S)
    dtv = cx.al(C)
    dtm = cx.al(1)
    mskbuf2 = cx.al(8 * NCH * S // 2)
    rowc = cx.al(4)
    xsqc = cx.al(C)
    repc = cx.al(2)
    scr = dict(xsq=xsq, row2=row2[0:1, :], rep=rep, ones=onesc, onesr=onesr,
               bi_stats=2, bi_rep=3)
    scr2 = dict(bank=7, xsqc=xsqc, rowc=rowc[0:1, :], repc=repc,
                ones=onesc, onesr=onesr)

    sy.wait_ge(cx.pe.h, cx.pe.v)
    sy.wait_ge(cx.dv.h, cx.dv.v)
    sy.wait_ge(cx.ac.h, cx.ac.v)
    se.wait_ge(cx.pe.h, cx.pe.v)
    se.wait_ge(cx.dv.h, cx.dv.v)
    ve.wait_ge(cx.pe.h, cx.pe.v)
    ve.wait_ge(cx.ac.h, cx.ac.v)
    te.wait_ge(cx.dv.h, cx.dv.v)
    te.wait_ge(cx.ac.h, cx.ac.v)
    ins = sy.dma_start(
        out=bf(win_sb[:, 0:C * D]).rearrange(
            "p (two dc e) -> p two dc e", two=2, dc=C),
        in_=sp["hl_win0T"].rearrange("(dc p) two e -> p two dc e", p=128))
    cx.ldw[0].inc(ins, 16)
    ins = sy.dma_start(
        out=bf(win_sb[:, C * D:2 * C * D]).rearrange(
            "p (two dc e) -> p two dc e", two=2, dc=C),
        in_=sp["hl_win1T"].rearrange("(dc p) two e -> p two dc e", p=128))
    m_win = cx.ldw[0].inc(ins, 16)
    ins = sy.dma_start(out=wb_sb.rearrange("p (dc s) -> p dc s", dc=C),
                       in_=di["wbT"].rearrange("(dc p) s -> p dc s", p=128))
    m_wb = cx.ldc.inc(ins, 16)
    ins = se.activation(dtv, K["logdt"], AF.Exp)
    m_dt = cx.ac.inc(ins, 1)
    ve.wait_ge(cx.ac.h, m_dt)
    cx.ve_edge(ve.tensor_scalar_max(dtv, dtv, 1e-4))
    ins = ve.tensor_scalar_min(dtv, dtv, 1.0)
    m_dtc = cx.dv.inc(ins, 1)
    cx.ve_edge(ins)
    bank = cx.use_bank(7)
    te.wait_ge(cx.dv.h, m_dtc)
    ins = te.matmul(bank[0:1, 0:C], onesc, dtv, start=True, stop=True)
    m_pe = cx.pe.inc(ins, 1)
    ve.wait_ge(cx.pe.h, m_pe)
    ins = ve.reduce_sum(dtm[0:1, 0:1], bank[0:1, 0:C], axis=AX.X)
    m_d1 = cx.dv.inc(ins, 1)
    cx.bank_consumed(7, cx.dv, m_d1)
    cx.ve_edge(ins)
    ins = ve.tensor_scalar_mul(dtm[0:1, 0:1], dtm[0:1, 0:1], 1.0 / D)
    m_d2 = cx.dv.inc(ins, 1)
    cx.ve_edge(ins)
    bank = cx.use_bank(7)
    te.wait_ge(cx.dv.h, m_d2)
    ins = te.matmul(bank[:, 0:1], onesr, dtm[0:1, 0:1], start=True, stop=True)
    m_pe = cx.pe.inc(ins, 1)
    dtm_rep = cx.al(1)
    ve.wait_ge(cx.pe.h, m_pe)
    ins = ve.tensor_copy(dtm_rep, bank[:, 0:1])
    m_d3 = cx.dv.inc(ins, 1)
    cx.bank_consumed(7, cx.dv, m_d3)
    cx.ve_edge(ins)
    cx.ldc.wait(ve)
    ins = ve.tensor_scalar_mul(wb_sb, wb_sb, dtm_rep[:, 0:1])
    m_wbs = cx.dv.inc(ins, 1)
    cx.ve_edge(ins)

    xp3 = r3(xp_g)
    z3 = r3(z_g)
    wb3 = wb_sb.rearrange("p (dc s) -> p dc s", dc=C)
    bp3 = bproj.rearrange("p (k s) -> p k s", k=NCH)
    xnp = cx.al(C * T)
    pairb = [cx.al(C * T), cx.al(C * T)]
    win0p = bf(win_sb[:, 0:C * D]).rearrange(
        "p (two dc e) -> p two dc e", two=2, dc=C)
    win1p = bf(win_sb[:, C * D:2 * C * D]).rearrange(
        "p (two dc e) -> p two dc e", two=2, dc=C)
    p1st = {"stats2": None, "sq2": None, "split": [None, None]}

    def p1_chain_steps(g):
        xsrc3 = xs3[:, :, g * T:(g + 1) * T]
        ln1 = make_ln_stages(cx, T, xsrc3, r3(xnp), scr)
        ln2 = make_ln_stages(cx, T, r3(xnp), r3(xn), scr,
                             gamma=K["lng"], beta=K["lnb"])
        sl = g % 2
        xh = r3(bf(pairb[sl][:, 0:C * T // 2]))
        xl = r3(bf(pairb[sl][:, C * T // 2:C * T]))

        def s_sq1():
            ln1["sq"](p1st["stats2"])

        def s_mult1():
            if p1st["stats2"] is not None:
                ve.wait_ge(cx.pe.h, p1st["stats2"])
            if p1st["sq2"] is not None:
                ve.wait_ge(cx.ac.h, p1st["sq2"])
            ln1["E"][1]()

        def s_mult2_sq2():
            ln1["E"][2]()
            p1st["sq2"] = ln2["sq"](ln1["st"]["stats"],
                                    dv_guard=ln1["st"]["xn"])

        def s_stats2():
            ln2["stats"]()
            p1st["stats2"] = ln2["st"]["stats"]

        def s_split():
            p1st["split"][sl] = emit_split(cx, r3(xn), xh, xl)

        te_steps = {1: ln1["stats"], 3: ln1["repmm"], 7: s_stats2,
                    9: ln2["repmm"]}
        ve_steps = {0: s_sq1, 2: ln1["row"], 4: ln1["E"][0],
                    5: s_mult1, 6: s_mult2_sq2, 8: ln2["row"],
                    10: ln2["E"][0], 11: ln2["E"][1], 12: ln2["E"][2],
                    13: ln2["E"][3], 14: s_split}
        return te_steps, ve_steps

    te0, ve0 = p1_chain_steps(0)
    for i in range(16):
        if i in ve0 and i != 0:
            pass
    ve0[0]()
    te0[1]()
    ve0[2]()
    te0[3]()
    ve0[4]()
    ve0[5]()
    ve0[6]()
    te0[7]()
    ve0[8]()
    te0[9]()
    for i in (10, 11, 12, 13, 14):
        ve0[i]()

    for g in range(NGT):
        sl = g % 2
        xnh3 = r3(bf(pairb[sl][:, 0:C * T // 2]))
        xnl3 = r3(bf(pairb[sl][:, C * T // 2:C * T]))
        xsrc3 = xs3[:, :, g * T:(g + 1) * T]
        if cc in ("p1stop", "p1nobp"):
            gp.wait_ge(cx.dv.h, cx.dv.v)
            ins = gp.dma_start(
                out=sp["dbg_sp"].rearrange("(ec p) t -> p ec t", p=128)[
                    :, :, g * T:(g + 1) * T], in_=xsrc3)
            cx.spo.inc(ins, 16)
            ins = gp.dma_start(
                out=sp["q_sp"].rearrange("(ec p) t -> p ec t", p=128)[
                    :, :, g * T:(g + 1) * T], in_=r3(xnp))
            cx.spo.inc(ins, 16)
            cx.spo.wait(ve)
            cx.spo.wait(se)
            cx.spo.wait(te)
        nte, nve = (p1_chain_steps(g + 1) if g + 1 < NGT else ({}, {}))
        te.wait_ge(cx.dv.h, p1st["split"][sl])
        if g == 0:
            te.wait_ge(cx.ldw[0].h, m_win)
        m_xp_last = None
        for ec in range(16):
            wp4 = win0p if ec < 8 else win1p
            el = (ec % 8) * 128
            bank = cx.use_bank(4)
            first = True
            for dc in range(C):
                wh = wp4[:, 0, dc, el:el + 128]
                wl = wp4[:, 1, dc, el:el + 128]
                for pi, (wt, xt3) in enumerate(((wh, xnh3), (wh, xnl3),
                                               (wl, xnh3))):
                    ins = te.matmul(bank[:, 0:T], wt, xt3[:, dc, :],
                                    start=first,
                                    stop=(dc == C - 1 and pi == 2))
                    first = False
            m_pe = cx.pe.inc(ins, 1)
            if ec in nte:
                nte[ec]()
            if ec < 8:
                ve.wait_ge(cx.pe.h, m_pe)
                ins = ve.tensor_scalar_add(xp3[:, ec, :], bank[:, 0:T],
                                           K["b_in"][:, ec:ec + 1])
                m_ev = cx.dv.inc(ins, 1)
                m_xp_last = m_ev
                cx.bank_consumed(4, cx.dv, m_ev)
            else:
                se.wait_ge(cx.pe.h, m_pe)
                ins = se.activation(z3[:, ec - 8, :], bank[:, 0:T], AF.Silu,
                                    bias=K["b_in"][:, ec:ec + 1])
                m_ev = cx.ac.inc(ins, 1)
                cx.bank_consumed(4, cx.ac, m_ev)
            if ec in nve:
                nve[ec]()
        gp.wait_ge(cx.dv.h, cx.dv.v)
        gp.wait_ge(cx.ac.h, cx.ac.v)
        ins = gp.dma_start(
            out=sp["xp_sp"].rearrange("(ec p) t -> p ec t", p=128)[
                :, :, g * T:(g + 1) * T], in_=xp3)
        cx.sps[0].inc(ins, 16)
        ins = gp.dma_start(
            out=sp["z_sp"].rearrange("(ec p) t -> p ec t", p=128)[
                :, :, g * T:(g + 1) * T], in_=z3)
        cx.sps[1].inc(ins, 16)
        if cc == "p1nobp":
            continue
        if g == 0:
            ve.wait_ge(cx.dv.h, m_wbs)
            te.wait_ge(cx.dv.h, m_wbs)
        te.wait_ge(cx.dv.h, m_xp_last)
        for kt in range(T // 128):
            bank = cx.use_bank(5)
            for dc in range(C):
                ins = te.matmul(bank[:, 0:S],
                                xp3[:, dc, kt * 128:(kt + 1) * 128],
                                wb3[:, dc, :], start=(dc == 0),
                                stop=(dc == C - 1))
            m_pe = cx.pe.inc(ins, 1)
            ve.wait_ge(cx.pe.h, m_pe)
            ins = ve.tensor_copy(bp3[:, 16 + g * (T // 128) + kt, :],
                                 bank[:, 0:S])
            m_bp = cx.dv.inc(ins, 1)
            cx.bank_consumed(5, cx.dv, m_bp)
        cx.sps[0].wait(ve)
        cx.sps[1].wait(ve)
        cx.sps[0].wait(se)
        cx.sps[1].wait(se)
    cx.release()

    if cc in ("p1stop", "p1nobp"):
        return

    if cc:
        cx.mark()
        mskbuf2 = cx.al(8 * 256)
        emit_rs(cx, bproj[:, 256:512], 256, sp["rs2_in"], sp["rs2_out"],
                bproj[:, 0:256], K, mskbuf2)
        cx.release()
    else:
        ins = ve.memset(bproj[:, 0:256], 0.0)
        cx.dv.inc(ins, 1)

    cx.mark()
    xmt_sb = cx.al(S * 128)
    xmean_sb = cx.al(NCH // 2 * S)
    bprev = cx.al(NCH * S)
    cx.mark()
    abar = cx.al(C * S)
    labar = cx.al(C * S)
    a128 = cx.al(C * S)
    lnD = cx.al(1)
    ins = cx.e["v"].memset(lnD, -6.931471805599453)
    cx.dv.inc(ins, 1)
    p2buf = cx.al(S * 128)
    k1buf = cx.al(S * 128)
    tmp128b = [cx.al(128), cx.al(128)]
    hstate = cx.al(C * S)
    hbuf = cx.al(C * S * 16)
    cbuf = cx.al(C * S * NCH)
    ttbuf = cx.al(S * 128)
    am_sb = cx.al(S * 128)
    zrow = cx.al(128)

    ve.wait_ge(cx.dv.h, cx.dv.v)
    ve.wait_ge(cx.pe.h, cx.pe.v)
    ve.wait_ge(cx.ac.h, cx.ac.v)
    se.wait_ge(cx.pe.h, cx.pe.v)
    se.wait_ge(cx.dv.h, cx.dv.v)
    te.wait_ge(cx.dv.h, cx.dv.v)
    for half in range(1):
        bank = cx.use_bank(4)
        ins = te.matmul(bank[:, 0:512], K["arev"], bproj, start=True,
                        stop=True)
        m_rv = cx.pe.inc(ins, 1)
        ve.wait_ge(cx.pe.h, m_rv)
        ins = ve.tensor_copy(bprev, bank[:, 0:512])
        m_rvc = cx.dv.inc(ins, 1)
        cx.bank_consumed(4, cx.dv, m_rvc)
        cx.ve_edge(ins)
    bpr3 = bprev.rearrange("p (k s) -> p k s", k=NCH)
    al3 = K["alog"].rearrange("p (dc s) -> p dc s", dc=C)
    ab3 = abar.rearrange("p (dc s) -> p dc s", dc=C)
    ins = se.activation(abar, K["alog"], AF.Exp)
    m1 = cx.ac.inc(ins, 1)
    ve.wait_ge(cx.ac.h, m1)
    for dc in range(C):
        ins = ve.tensor_scalar_mul(ab3[:, dc, :], ab3[:, dc, :],
                                   dtv[:, dc:dc + 1])
    cx.ve_edge(ins)
    ins = ve.tensor_scalar_mul(abar, abar, -1.0)
    m2 = cx.dv.inc(ins, 1)
    se.wait_ge(cx.dv.h, m2)
    ins = se.activation(abar, abar, AF.Exp)
    m3 = cx.ac.inc(ins, 1)
    ve.wait_ge(cx.ac.h, m3)
    cx.ve_edge(ve.tensor_scalar_max(abar, abar, 1e-8))
    ins = ve.tensor_scalar_min(abar, abar, 1.0 - 1e-8)
    m4 = cx.dv.inc(ins, 1)
    se.wait_ge(cx.dv.h, m4)
    ins = se.activation(labar, abar, AF.Ln)
    m5 = cx.ac.inc(ins, 1)
    ins = se.activation(a128, labar, AF.Exp, scale=128.0)
    m6 = cx.ac.inc(ins, 1)
    ve.wait_ge(cx.ac.h, m6)

    lb3 = labar.rearrange("p (dc s) -> p dc s", dc=C)
    p23 = p2buf.rearrange("p (s t) -> p s t", s=S)
    k13 = k1buf.rearrange("p (s t) -> p s t", s=S)
    cb3 = cbuf.rearrange("p (dc s k) -> p dc s k", dc=C, s=S)
    hb3 = hbuf.rearrange("p (dc s k) -> p dc s k", dc=C, s=S)

    exp_hist = [None, None]
    for dc in range(C):
        ve.wait_ge(cx.pe.h, cx.pe.v)
        se.wait_ge(cx.pe.h, cx.pe.v)
        for s in range(S):
            tmp = tmp128b[s % 2]
            if exp_hist[s % 2] is not None:
                ve.wait_ge(cx.ac.h, exp_hist[s % 2])
            ins = ve.tensor_scalar_mul(tmp, K["iota_p"],
                                       lb3[:, dc, s:s + 1])
            mm1 = cx.dv.inc(ins, 1)
            se.wait_ge(cx.dv.h, mm1)
            ins = se.activation(p23[:, s, :], tmp, AF.Exp,
                                bias=lnD[:, 0:1])
            mm2 = cx.ac.inc(ins, 1)
            exp_hist[s % 2] = mm2
            bank_id = 6 if s % 2 == 0 else 4
            bank = cx.use_bank(bank_id)
            te.wait_ge(cx.ac.h, mm2)
            ins = te.transpose(bank[:, 0:128], p23[:, s, :], K["identc"])
            mm3 = cx.pe.inc(ins, 1)
            ve.wait_ge(cx.pe.h, mm3)
            ins = ve.tensor_copy(k13[:, s, :], bank[:, 0:128])
            mm4 = cx.dv.inc(ins, 1)
            cx.bank_consumed(bank_id, cx.dv, mm4)
        te.wait_ge(cx.dv.h, cx.dv.v)
        for s in range(S):
            bank_id = 7 if s % 2 == 0 else 5
            bank = cx.use_bank(bank_id)
            ins = te.matmul(bank[:, 0:NCH], k13[:, s, :], bpr3[:, :, s],
                            start=True, stop=True)
            mm5 = cx.pe.inc(ins, 1)
            ve.wait_ge(cx.pe.h, mm5)
            ins = ve.tensor_copy(cb3[:, dc, s, :], bank[:, 0:NCH])
            mm6 = cx.dv.inc(ins, 1)
            cx.bank_consumed(bank_id, cx.dv, mm6)

    ins = ve.memset(hstate, 0.0)
    m_h = cx.dv.inc(ins, 1)
    cx.ve_edge(ins)
    for k in range(1, NCH):
        cx.ve_edge(ve.tensor_mul(hstate, hstate, a128))
        ins = ve.tensor_tensor(
            hstate, hstate,
            cbuf.rearrange("p (g k) -> p g k", g=C * S)[:, :, k - 1], AL.add)
        cx.ve_edge(ins)
        if k >= 16:
            ins = ve.tensor_copy(
                hbuf.rearrange("p (g k) -> p g k", g=C * S)[:, :, k - 16],
                hstate)
    m_hc = cx.dv.inc(ins, 1)

    ins = ve.memset(am_sb, 0.0)
    cx.dv.inc(ins, 1)
    cx.ve_edge(ins)
    bank_i = cx.use_bank(3)
    exp_hist = [None, None]
    for dc in range(C):
        ve.wait_ge(cx.pe.h, cx.pe.v)
        se.wait_ge(cx.pe.h, cx.pe.v)
        for s in range(S):
            tmp = tmp128b[s % 2]
            if exp_hist[s % 2] is not None:
                ve.wait_ge(cx.ac.h, exp_hist[s % 2])
            ins = ve.tensor_scalar_mul(tmp, K["iota_t"],
                                       lb3[:, dc, s:s + 1])
            mm1 = cx.dv.inc(ins, 1)
            se.wait_ge(cx.dv.h, mm1)
            ins = se.activation(p23[:, s, :], tmp, AF.Exp)
            mm2 = cx.ac.inc(ins, 1)
            exp_hist[s % 2] = mm2
        te.wait_ge(cx.ac.h, cx.ac.v)
        if dc == 0:
            te.wait_ge(cx.dv.h, m_hc)
        for s in range(S):
            ins = te.matmul(bank_i[:, s * S:(s + 1) * S], p23[:, s, :],
                            hb3[:, dc, s, :],
                            start=(dc == 0 and s == 0), stop=False)
            cx.pe.inc(ins, 1)
        for s in range(S):
            bank_id = 6 if s % 2 == 0 else 7
            bank = cx.use_bank(bank_id)
            ins = te.matmul(bank[0:1, 0:128], onesc, p23[:, s, :],
                            start=True, stop=True)
            mm7 = cx.pe.inc(ins, 1)
            ve.wait_ge(cx.pe.h, mm7)
            ins = ve.tensor_tensor(am_sb[0:1, s * 128:(s + 1) * 128],
                                   am_sb[0:1, s * 128:(s + 1) * 128],
                                   bank[0:1, 0:128], AL.add)
            mm8 = cx.dv.inc(ins, 1)
            cx.bank_consumed(bank_id, cx.dv, mm8)

    cx.ve_edge(ve.tensor_scalar_mul(am_sb, am_sb, 1.0 / D))
    ins = ve.memset(zrow, 0.0)
    m_u = cx.dv.inc(ins, 1)
    cx.ve_edge(ins)
    gp.wait_ge(cx.dv.h, m_u)
    ins = gp.dma_start(out=sp["u_sp"][:, 0:127], in_=zrow[0:S, 0:127])
    cx.spo.inc(ins, 16)
    ins = gp.dma_start(out=sp["u_sp"][:, 255:256], in_=zrow[0:S, 0:1],
                       allow_slow_non_contiguous=True)
    cx.spv[0].inc(ins, 16)
    ins = gp.dma_start(
        out=sp["u_sp"][:, 127:255].unsqueeze(0),
        in_=am_sb[0:1, 0:S * 128].rearrange("o (s t) -> o s t", s=S))
    cx.spv[1].inc(ins, 16)
    cx.spo.wait(sy)
    cx.spv[0].wait(sy)
    cx.spv[1].wait(sy)
    tt_sems = [cx.ldc, cx.ldq, cx.ldw[0], cx.ldw[1]]
    for s in range(S):
        src_ap = bass.AP(tensor=sp["u_sp"].tensor, offset=s * 256,
                         ap=[[1, 128], [1, 128]])
        ins = sy.dma_start(out=ttbuf[:, s * 128:(s + 1) * 128], in_=src_ap)
        tt_sems[s % 4].inc(ins, 16)
    for sem in tt_sems:
        sem.wait(te)
    for s in range(S):
        ins = te.matmul(bank_i[:, s * S:(s + 1) * S],
                        ttbuf[:, s * 128:(s + 1) * 128],
                        bpr3[:, 16:32, s],
                        start=False, stop=(s == S - 1))
        m_ii = cx.pe.inc(ins, 1)
    ve.wait_ge(cx.pe.h, m_ii)
    ins = ve.tensor_copy(xmean_sb, bank_i[:, 0:256])
    m_xm = cx.dv.inc(ins, 1)
    cx.bank_consumed(3, cx.dv, m_xm)
    te.wait_ge(cx.dv.h, m_xm)
    xm2 = xmean_sb.rearrange("p (s k) -> p s k", s=S)
    for k in range(16):
        bank = cx.use_bank(6)
        ins = te.transpose(bank[0:S, 0:128], xm2[:, :, k], K["identc"])
        m_tk = cx.pe.inc(ins, 1)
        ve.wait_ge(cx.pe.h, m_tk)
        ins = ve.tensor_copy(xmt_sb[0:S, k * 128:(k + 1) * 128],
                             bank[0:S, 0:128])
        m_tkc = cx.dv.inc(ins, 1)
        cx.bank_consumed(6, cx.dv, m_tkc)
    cx.release()

    cx.mark()
    wout_sb = cx.al(C * D)
    wc_sb = cx.al(C * 128)
    xp_g2 = [cx.al(C * T), cx.al(C * T)]
    z_g2 = [cx.al(C * T), cx.al(C * T)]
    y_g = cx.al(C * T)
    xnp2 = [cx.al(C * T), cx.al(C * T)]
    ypair = [cx.al(C * T), cx.al(C * T)]
    xsq = cx.al(C * T)
    row2 = cx.al(3 * T)
    rep = cx.al(2 * T)
    scr = dict(xsq=xsq, row2=row2[0:1, :], rep=rep, ones=onesc, onesr=onesr,
               bi_stats=2, bi_rep=3)
    sy.wait_ge(cx.pe.h, cx.pe.v)
    sy.wait_ge(cx.dv.h, cx.dv.v)
    sy.wait_ge(cx.ac.h, cx.ac.v)
    ins = sy.dma_start(
        out=bf(wout_sb).rearrange("p (two dc e) -> p two dc e", two=2, dc=C),
        in_=sp["hl_woutT"].rearrange("(dc p) two e -> p two dc e", p=128))
    m_wo = cx.ldw[0].inc(ins, 16)
    ins = sy.dma_start(out=wc_sb[0:S, :],
                       in_=di["wcT"])
    m_wc = cx.ldc.inc(ins, 16)
    y3 = r3(y_g)
    p3st = {"stats": None, "yz": [None, None], "split": [None, None]}

    def p3_chain_steps(g, sl):
        xp3 = r3(xp_g2[sl])
        z3g = r3(z_g2[sl])
        xsrc3 = xs3[:, :, g * T:(g + 1) * T]
        ln = make_ln_stages(cx, T, xsrc3, r3(xnp2[sl]), scr)
        yh = r3(bf(ypair[sl][:, 0:C * T // 2]))
        yl = r3(bf(ypair[sl][:, C * T // 2:C * T]))
        h = {}

        def s_loads_sq():
            if p3st["yz"][sl] is not None:
                sy.wait_ge(cx.dv.h, p3st["yz"][sl])
            ins = sy.dma_start(
                out=xp3,
                in_=sp["xp_sp"].rearrange("(ec p) t -> p ec t", p=128)[
                    :, :, g * T:(g + 1) * T])
            h["mxp"] = cx.lds[0].inc(ins, 16)
            ins = sy.dma_start(
                out=z3g,
                in_=sp["z_sp"].rearrange("(ec p) t -> p ec t", p=128)[
                    :, :, g * T:(g + 1) * T])
            h["mz"] = cx.lds[1].inc(ins, 16)
            ln["sq"](p3st["stats"])

        def s_stats():
            ln["stats"]()
            p3st["stats"] = ln["st"]["stats"]

        def s_ymm(i):
            def go():
                for ec in (2 * i, 2 * i + 1):
                    bi = 4 if ec % 2 == 0 else 6
                    bank = cx.use_bank(bi)
                    for kt in range(T // 128):
                        k = g * (T // 128) + kt
                        ins = te.matmul(bank[:, kt * 128:(kt + 1) * 128],
                                        wc_sb[0:S, ec * 128:(ec + 1) * 128],
                                        xmt_sb[0:S, k * 128:(k + 1) * 128],
                                        start=(kt == 0),
                                        stop=(kt == T // 128 - 1))
                    h[("ymm", ec)] = cx.pe.inc(ins, 1)
            return go

        def s_yevac(i):
            def go():
                if i == 0:
                    ve.wait_ge(cx.lds[0].h, h["mxp"])
                ins = None
                for ec in (2 * i, 2 * i + 1):
                    bi = 4 if ec % 2 == 0 else 6
                    ve.wait_ge(cx.pe.h, h[("ymm", ec)])
                    ins = ve.scalar_tensor_tensor(
                        y3[:, ec, :], r3(xp_g2[sl])[:, ec, :],
                        K["dskip"][:, ec:ec + 1],
                        cx.pb[bi][:, 0:T], AL.mult, AL.add)
                    m_y = cx.dv.inc(ins, 1)
                    cx.bank_consumed(bi, cx.dv, m_y)
                if i == 3:
                    cx.ve_edge(ins)
            return go

        def s_yz():
            ve.wait_ge(cx.lds[1].h, h["mz"])
            ins = ve.tensor_tensor(y3, y3, z3g, AL.mult)
            p3st["yz"][sl] = cx.dv.inc(ins, 1)
            cx.ve_edge(ins)

        def s_split():
            p3st["split"][sl] = emit_split(cx, y3, yh, yl)

        def w2():
            ln["row"]()
            s_yevac(0)()

        def w4():
            ln["E"][0]()
            s_yevac(1)()

        def w5():
            ln["E"][1]()
            s_yevac(2)()

        def w6():
            ln["E"][2]()
            s_yevac(3)()
            s_yz()

        te_steps = {1: s_stats, 2: s_ymm(0), 3: ln["repmm"],
                    4: s_ymm(1), 5: s_ymm(2), 6: s_ymm(3)}
        ve_steps = {0: s_loads_sq, 2: w2, 4: w4, 5: w5, 6: w6,
                    7: s_split}
        return te_steps, ve_steps, s_split

    ve.wait_ge(cx.pe.h, cx.pe.v)
    ve.wait_ge(cx.ac.h, cx.ac.v)
    se.wait_ge(cx.pe.h, cx.pe.v)
    se.wait_ge(cx.dv.h, cx.dv.v)
    te.wait_ge(cx.dv.h, cx.dv.v)
    cx.ldc.wait(te)
    p3_order = [7] + list(range(7))
    rs3own = cx.al(C)
    rs3msk = cx.al(8 * C)
    rs3scr = dict(bank=7, xsqc=cx.al(C), rowc=cx.al(4)[0:1, :],
                  repc=cx.al(2), ones=onesc, onesr=onesr)
    m_rs3cc = None
    te0, ve0, split0 = p3_chain_steps(p3_order[0], 0)
    ve0[0]()
    te0[1]()
    te0[2]()
    ve0[2]()
    te0[3]()
    te0[4]()
    ve0[4]()
    te0[5]()
    ve0[5]()
    te0[6]()
    ve0[6]()
    ve0[7]()

    wo4 = bf(wout_sb).rearrange("p (two dc e) -> p two dc e", two=2, dc=C)
    for i, g in enumerate(p3_order):
        sl = i % 2
        xsrc3 = xs3[:, :, g * T:(g + 1) * T]
        yh3 = r3(bf(ypair[sl][:, 0:C * T // 2]))
        yl3 = r3(bf(ypair[sl][:, C * T // 2:C * T]))
        xnp3 = r3(xnp2[sl])
        if i + 1 < NGT:
            nte, nve, _ = p3_chain_steps(p3_order[i + 1], (i + 1) % 2)
        else:
            nte, nve = {}, {}
        te.wait_ge(cx.dv.h, p3st["split"][sl])
        if i == 0:
            te.wait_ge(cx.ldw[0].h, m_wo)
        for ec in range(C):
            bank = cx.use_bank(5)
            first = True
            for dc in range(C):
                wh = wo4[:, 0, dc, 128 * ec:128 * (ec + 1)]
                wl = wo4[:, 1, dc, 128 * ec:128 * (ec + 1)]
                for pi, (wt, xt3) in enumerate(((wh, yh3), (wh, yl3),
                                               (wl, yh3))):
                    ins = te.matmul(bank[:, 0:T], wt, xt3[:, dc, :],
                                    start=first,
                                    stop=(dc == C - 1 and pi == 2))
                    first = False
            m_pe = cx.pe.inc(ins, 1)
            if ec in nte:
                nte[ec]()
            ve.wait_ge(cx.pe.h, m_pe)
            ins = ve.scalar_tensor_tensor(
                xsrc3[:, ec, :], bank[:, 0:T], K["bout"][:, ec:ec + 1],
                xsrc3[:, ec, :], AL.add, AL.add)
            m_x4 = cx.dv.inc(ins, 1)
            cx.bank_consumed(5, cx.dv, m_x4)
            cx.ve_edge(ins)
            ins = ve.tensor_tensor(xsrc3[:, ec, :], xsrc3[:, ec, :],
                                   xnp3[:, ec, :], AL.add)
            cx.dv.inc(ins, 1)
            if ec in nve:
                nve[ec]()
        if i == 0 and cc is True:
            se.wait_ge(cx.dv.h, cx.dv.v)
            te.wait_ge(cx.dv.h, cx.dv.v)
            emit_row_ln(cx, xs3[:, :, NT - 1], rs3own, rs3scr)
            m_rs3cc = emit_rs_send(cx, rs3own, C, sp["rs3_in"],
                                   sp["rs3_out"], K, rs3msk)
    cx.release()
    cx.release()

    if cc is True:
        emit_rs_recv(cx, sp["rs3_out"], ln4row, K, m_rs3cc)
    else:
        ve.wait_ge(cx.ac.h, cx.ac.v)
        ve.wait_ge(cx.pe.h, cx.pe.v)
        ins = ve.memset(ln4row, 0.0)
        cx.dv.inc(ins, 1)
    cx.release()


def build_program(stop_after=None, cc=True, detect=False):
    nc = bass.Bass("TRN2", target_bir_lowering=False)
    nc.detect_race_conditions = detect
    cx = Ctx(nc)

    di = {}

    def inp(name, shape):
        di[name] = nc.dram_tensor(name, shape, F32, kind="ExternalInput").ap()
        return di[name]

    inp("x_res", [D, NT])
    inp("x_kv", [D, NK])
    for w in ["wqT", "wkT", "wvT", "wrT", "wk2T", "wv2T", "woutT"]:
        inp(w, [D, D])
    inp("winT", [D, 2 * D])
    inp("wbT", [D, S])
    inp("wcT", [S, D])
    for b in ["bq", "bk", "bv", "bout", "mu_r", "mu_k", "lng", "lnb",
              "dskip", "logdt"]:
        inp(b, [128, C])
    inp("b_in", [128, 2 * C])
    inp("alog", [128, C * S])
    inp("bias_cols", [128, NSTEPS])
    inp("m2d", [128, 256])
    inp("gate1", [128, 1])
    inp("slot8", [128, 8])
    inp("bv_row", [1, D])
    inp("iota_t", [128, 128])
    inp("iota_r", [128, 128])
    inp("iota_p", [128, 128])
    inp("arev", [128, 128])
    inp("identc", [128, 128])
    out_d = nc.dram_tensor("out_fm", [C, 128, NT], F32,
                           kind="ExternalOutput").ap()

    sp = {}
    for wname in ["wqT", "wkT", "wvT", "wrT", "wk2T", "wv2T", "woutT",
                  "win0T", "win1T"]:
        sp["hl_" + wname] = nc.dram_tensor("hl_" + wname, [D, 2, D],
                                           BF16).ap()
    sp["k_sp"] = nc.dram_tensor("k_sp", [D, NK], F32).ap()
    sp["v_sp"] = nc.dram_tensor("v_sp", [NK, D], F32).ap()
    sp["q_sp"] = nc.dram_tensor("q_sp", [D, NT], F32).ap()
    sp["xp_sp"] = nc.dram_tensor("xp_sp", [D, NT], F32).ap()
    sp["z_sp"] = nc.dram_tensor("z_sp", [D, NT], F32).ap()
    sp["u_sp"] = nc.dram_tensor("u_sp", [S, 256], F32).ap()
    sp["dbg_sp"] = nc.dram_tensor("dbg_sp", [D, NT], F32).ap()
    sp["rs1_in"] = nc.dram_tensor("rs1_in", [8, D], F32).ap()
    sp["rs1_out"] = nc.dram_tensor("rs1_out", [1, D], F32).ap()
    sp["rs2_in"] = nc.dram_tensor("rs2_in", [8, NT * S], F32).ap()
    sp["rs2_out"] = nc.dram_tensor("rs2_out", [1, NT * S], F32).ap()
    sp["rs3_in"] = nc.dram_tensor("rs3_in", [8, D], F32).ap()
    sp["rs3_out"] = nc.dram_tensor("rs3_out", [1, D], F32).ap()

    for nm in ["y", "g", "s", "v", "t"]:
        cx.e[nm] = REng()
    emit_all(cx, di, out_d, sp, stop_after, cc)

    for eng in cx.e.values():
        totals = {}
        out = []
        for c in eng.cmds:
            if c.name == "dma_start" and c.incs:
                sh, n = c.incs[0]
                prev = totals.get(id(sh), 0)
                if prev > 0:
                    w = RCmd("wait_ge", (sh, prev), {})
                    out.append(w)
                totals[id(sh)] = prev + n
            out.append(c)
        eng.cmds = out

    with nc.Block() as block:
        @block.sync
        def _(sy):
            cx.e["y"].replay(sy)

        @block.gpsimd
        def _(g):
            cx.e["g"].replay(g)

        @block.scalar
        def _(s):
            cx.e["s"].replay(s)

        @block.vector
        def _(v):
            cx.e["v"].replay(v)

        @block.tensor
        def _(t):
            cx.e["t"].replay(t)

    return nc


def emit_all(cx, di, out_d, sp, stop_after=None, cc=True):
    te, ve, se, gp, sy = (cx.e["t"], cx.e["v"], cx.e["s"], cx.e["g"],
                          cx.e["y"])

    xs = cx.al(C * NT)
    xs3 = r3(xs)
    K = {}
    for name in ["bq", "bk", "bv", "bout", "mu_r", "mu_k", "lng", "lnb",
                 "dskip", "logdt"]:
        K[name] = cx.al(C)
    K["b_in"] = cx.al(2 * C)
    K["alog"] = cx.al(C * S)
    K["gate1"] = cx.al(1)
    K["slot8"] = cx.al(8)
    K["iota_t"] = cx.al(128)
    K["iota_r"] = cx.al(128)
    K["iota_p"] = cx.al(128)
    K["arev"] = cx.al(128)
    K["identc"] = cx.al(128)
    onesc = cx.al(1)
    onesr_t = cx.al(128)
    ln2row = cx.al(C)
    ln4row = cx.al(C)

    for name in ["bq", "bk", "bv", "bout", "mu_r", "mu_k", "lng", "lnb",
                 "dskip", "logdt", "b_in", "alog", "gate1", "slot8",
                 "iota_t", "iota_r", "iota_p", "arev", "identc"]:
        ins = sy.dma_start(out=K[name], in_=di[name])
        cx.ldc.inc(ins, 16)
    cx.ldc.wait(ve)
    cx.ldc.wait(te)
    cx.ldc.wait(se)
    ve.memset(onesc, 1.0)
    ins = ve.memset(onesr_t[0:1, :], 1.0)
    m_ones = cx.dv.inc(ins, 1)
    onesr = onesr_t[0:1, :]
    te.wait_ge(cx.dv.h, m_ones)

    cx.mark()
    bvrep = cx.al(C * 128)
    wbuf = cx.al(C * D)
    wbuf2 = cx.al(C * D)
    stg = [cx.al(C * TG), cx.al(C * TG)]
    xn = cx.al(C * TG)
    xnh = cx.al(C * TG // 2)
    xnl = cx.al(C * TG // 2)
    xsq = cx.al(C * TG)
    row2 = cx.al(3 * TG)
    rep = cx.al(2 * TG)
    pout = [cx.al(C * TG), cx.al(C * TG)]
    vout = [cx.al(C * TG)]

    scr = dict(xsq=xsq, row2=row2[0:1, :], rep=rep, ones=onesc, onesr=onesr,
               bi_stats=2, bi_rep=3)

    def load_w(dst, name, eng, sem):
        ins = eng.dma_start(
            out=dst.rearrange("p (dc e) -> p dc e", dc=C),
            in_=di[name].rearrange("(dc p) e -> p dc e", p=128))
        return sem.inc(ins, 16)

    def load_w_hl(dst_words, name, eng, sem):
        ins = eng.dma_start(
            out=bf(dst_words).rearrange("p (two dc e) -> p two dc e", two=2,
                                        dc=C),
            in_=sp["hl_" + name].rearrange("(dc p) two e -> p two dc e",
                                           p=128))
        return sem.inc(ins, 16)

    def prep_src_ap(wname, ec):
        if wname == "win0T":
            base = di["winT"].rearrange("(dc p) e -> p dc e", p=128)[
                :, :, 0:D]
        elif wname == "win1T":
            base = di["winT"].rearrange("(dc p) e -> p dc e", p=128)[
                :, :, D:2 * D]
        else:
            base = di[wname].rearrange("(dc p) e -> p dc e", p=128)
        return base[:, :, 128 * ec:128 * (ec + 1)]

    def pair_view(words):
        return bf(words).rearrange("p (two dc e) -> p two dc e", two=2, dc=C)

    prep = {"war": {}, "m_out": {}, "m_dir": {}, "m_in": {},
            "pair_out": None}

    def prep_load(wname, ec, srcb, war=None):
        if war is not None:
            sy.wait_ge(cx.dv.h, war)
        prep["m_in"][(wname, ec)] = cx.dma(sy, cx.pin, r3(srcb),
                                           prep_src_ap(wname, ec))

    def prep_split(wname, ec, srcb, direct=None, pairb=None):
        ve.wait_ge(cx.pin.h, prep["m_in"].pop((wname, ec)))
        if direct is not None:
            pv = pair_view(direct)[:, :, :, 128 * ec:128 * (ec + 1)]
        else:
            if prep["pair_out"] is not None:
                ve.wait_ge(cx.pot.h, prep["pair_out"])
            pv = pair_view(pairb)
        ins = ve.tensor_copy(pv[:, 0], r3(srcb))
        cx.ve_edge(ins)
        ins = ve.tensor_tensor(pv[:, 1], r3(srcb), pv[:, 0], AL.subtract)
        m_spl = cx.dv.inc(ins, 1)
        if direct is None:
            gp.wait_ge(cx.dv.h, m_spl)
            m_out = cx.dma(
                gp, cx.pot,
                sp["hl_" + wname].rearrange(
                    "(dc p) two e -> p two dc e",
                    p=128)[:, :, :, 128 * ec:128 * (ec + 1)],
                pv)
            prep["pair_out"] = m_out
            prep["m_out"][wname] = m_out
        else:
            prep["m_dir"][wname] = m_spl
        return m_spl

    war_q = [None, None]
    for ec in range(C):
        srcb = pout[ec % 2][:, 0:1024]
        prep_load("wqT", ec, srcb, war=war_q[ec % 2])
        war_q[ec % 2] = prep_split("wqT", ec, srcb, direct=wbuf)
    m_wq = prep["m_dir"]["wqT"]

    stage_guard = [None, None]

    def ln_group(g, src_name, T=TG):
        sl = g % 2
        st = stg[sl]
        if stage_guard[sl] is not None:
            sy.wait_ge(cx.dv.h, stage_guard[sl][0])
            sy.wait_ge(cx.ac.h, stage_guard[sl][1])
        ins = sy.dma_start(
            out=r3(st),
            in_=di[src_name].rearrange("(dc p) t -> p dc t", p=128)[
                :, :, g * T:(g + 1) * T])
        cx.lds[sl].inc(ins, 16)
        cx.lds[sl].wait(ve)
        cx.lds[sl].wait(se)
        cx.lds[sl].wait(te)
        ve.wait_ge(cx.pe.h, cx.pe.v)
        ve.wait_ge(cx.ac.h, cx.ac.v)
        se.wait_ge(cx.pe.h, cx.pe.v)
        se.wait_ge(cx.dv.h, cx.dv.v)
        te.wait_ge(cx.dv.h, cx.dv.v)
        m_xn = emit_ln(cx, T, r3(st), r3(xn), scr)
        m_xn = emit_split(cx, r3(xn), r3(bf(xnh)), r3(bf(xnl)))
        stage_guard[sl] = (m_xn, cx.ac.v)
        te.wait_ge(cx.dv.h, m_xn)
        return m_xn

    def s1_load(t, war):
        wname = "wkT" if t % 2 == 0 else "wvT"
        prep_load(wname, t // 2, vout[0][:, (t % 2) * 1024:(t % 2 + 1) * 1024],
                  war=war)

    def s1_split(t):
        wname = "wkT" if t % 2 == 0 else "wvT"
        srcb = vout[0][:, (t % 2) * 1024:(t % 2 + 1) * 1024]
        if wname == "wkT":
            return prep_split(wname, t // 2, srcb, pairb=bvrep)
        return prep_split(wname, t // 2, srcb, direct=wbuf2)

    s1_war = [None, None]
    for g in range(NGT):
        s1_load(2 * g, s1_war[0])
        s1_load(2 * g + 1, s1_war[1])
        ln_group(g, "x_res")
        sl = g % 2
        po = pout[sl]
        cx.sps[sl].wait(ve)
        te.wait_ge(cx.dv.h, m_wq)
        emit_proj3(cx, TG, bf(wbuf), (r3(bf(xnh)), r3(bf(xnl))),
                   lambda ec: r3(po)[:, ec, :], 4, bias=K["bq"])
        gp.wait_ge(cx.dv.h, cx.dv.v)
        ins = gp.dma_start(
            out=sp["q_sp"].rearrange("(ec p) t -> p ec t", p=128)[
                :, :, g * TG:(g + 1) * TG],
            in_=r3(po))
        cx.sps[sl].inc(ins, 16)
        s1_war[0] = s1_split(2 * g)
        s1_war[1] = s1_split(2 * g + 1)

    sy.wait_ge(cx.pe.h, cx.pe.v)
    sy.wait_ge(cx.pot.h, prep["m_out"]["wkT"])
    load_w_hl(wbuf, "wkT", sy, cx.ldw[0])
    ins = sy.dma_start(out=bvrep,
                       in_=di["bv_row"][0:1, :].broadcast_to([128, D]))
    cx.ldc.inc(ins, 16)
    cx.ldc.wait(ve)
    for g in range(NGK):
        ln_group(g, "x_kv")
        sl = g % 2
        po = pout[sl]
        vo = vout[0]
        cx.sps[sl].wait(ve)
        cx.ldw[0].wait(te)
        emit_proj3(cx, TG, bf(wbuf), (r3(bf(xnh)), r3(bf(xnl))),
                   lambda ec: r3(po)[:, ec, :], 4, bias=K["bk"])
        gp.wait_ge(cx.dv.h, cx.dv.v)
        ins = gp.dma_start(
            out=sp["k_sp"].rearrange("(ec p) t -> p ec t", p=128)[
                :, :, g * TG:(g + 1) * TG],
            in_=r3(po))
        cx.sps[sl].inc(ins, 16)
        te.wait_ge(cx.dv.h, prep["m_dir"]["wvT"])
        cx.spv[0].wait(ve)
        wv4 = bf(wbuf2).rearrange("p (two dc e) -> p two dc e", two=2, dc=C)
        xh3b = r3(bf(xnh))
        xl3b = r3(bf(xnl))
        vo3 = vo.rearrange("p (kt e) -> p kt e", kt=TG // 128)
        m_vv = None
        for kt in range(TG // 128):
            for half in range(2):
                bank = cx.use_bank(5)
                first = True
                for dc in range(C):
                    xh_c = xh3b[:, dc, kt * 128:(kt + 1) * 128]
                    xl_c = xl3b[:, dc, kt * 128:(kt + 1) * 128]
                    wh_c = wv4[:, 0, dc, half * 512:(half + 1) * 512]
                    wl_c = wv4[:, 1, dc, half * 512:(half + 1) * 512]
                    for (xt, wt) in ((xh_c, wh_c), (xh_c, wl_c),
                                     (xl_c, wh_c)):
                        last = (dc == C - 1) and (xt is xl_c)
                        ins = te.matmul(bank[:, 0:512], xt, wt,
                                        start=first, stop=last)
                        first = False
                m_pe = cx.pe.inc(ins, 1)
                ve.wait_ge(cx.pe.h, m_pe)
                ins = ve.tensor_tensor(
                    vo3[:, kt, half * 512:(half + 1) * 512], bank[:, 0:512],
                    bvrep[:, half * 512:(half + 1) * 512], AL.add)
                m_vv = cx.dv.inc(ins, 1)
                cx.bank_consumed(5, cx.dv, m_vv)
        gp.wait_ge(cx.dv.h, m_vv)
        ins = gp.dma_start(
            out=sp["v_sp"][g * TG:(g + 1) * TG, :].rearrange(
                "(kt p) e -> p kt e", p=128),
            in_=vo3)
        cx.spv[0].inc(ins, 16)
    cx.release()

    if stop_after in ("dump_q", "dump_k", "dump_v"):
        finish_stub(cx, out_d,
                    dump=sp[{"dump_q": "q_sp", "dump_k": "k_sp",
                             "dump_v": "v_sp"}[stop_after]])
        return
    if stop_after == "qkv":
        finish_stub(cx, out_d, zero=True)
        return

    for eng in (sy, ve, se, gp):
        for s in cx.sps + cx.spv:
            s.wait(eng)
    cx.mark()
    qblk = cx.al(C * 512)
    oblk = cx.al(QB * C * 128)
    ktb = [cx.al(C * 128), cx.al(C * 128)]
    vtb = [cx.al(C * 128), cx.al(C * 128)]
    p_sb = cx.al(128)
    pt_sb = cx.al(128)
    mstat = cx.al(QB * 2)
    lstat = cx.al(QB)
    colA = cx.al(1)
    colB = cx.al(1)
    colC = cx.al(1)
    colD = cx.al(1)
    bias_sb = cx.al(NSTEPS)
    m2d_sb = cx.al(256)
    p_src = cx.al(2 * 1024)
    p_pair = cx.al(2 * 1024)

    ins = sy.dma_start(out=bias_sb, in_=di["bias_cols"])
    cx.ldc.inc(ins, 16)
    ins = sy.dma_start(out=m2d_sb, in_=di["m2d"])
    cx.ldc.inc(ins, 16)

    s2_list = [(w, ec) for w in ["wrT", "wk2T", "wv2T", "win0T", "win1T",
                                 "woutT"] for ec in range(C)]
    s2 = {"load": 0, "split": 0, "war": [None, None]}

    def s2_pump():
        if s2["load"] < len(s2_list) and s2["load"] - s2["split"] < 2:
            t = s2["load"]
            w, ec = s2_list[t]
            prep_load(w, ec, p_src[:, (t % 2) * 1024:(t % 2 + 1) * 1024],
                      war=s2["war"][t % 2])
            s2["load"] += 1
        elif s2["split"] < s2["load"]:
            t = s2["split"]
            w, ec = s2_list[t]
            s2["war"][t % 2] = prep_split(
                w, ec, p_src[:, (t % 2) * 1024:(t % 2 + 1) * 1024],
                pairb=p_pair[:, (t % 2) * 1024:(t % 2 + 1) * 1024])
            s2["split"] += 1
    oblk3 = oblk.rearrange("p (i e) -> p i e", i=QB)
    cx.ldc.wait(ve)
    cx.ldc.wait(se)

    step_idx = 0
    pipe_state = {"mpp": None}
    nb_run = 1 if stop_after in ("attn1b", "attn1b_noret") else NB
    for bq in range(nb_run):
        nslot = 20 + 4 * bq
        sy.wait_ge(cx.pe.h, cx.pe.v)
        ins = sy.dma_start(
            out=r3(qblk),
            in_=sp["q_sp"].rearrange("(ec p) t -> p ec t", p=128)[
                :, :, bq * 512:(bq + 1) * 512])
        cx.ldq.inc(ins, 16)
        if bq == 0:
            ve.wait_ge(cx.pe.h, cx.pe.v)
            ve.wait_ge(cx.ac.h, cx.ac.v)
        ve.memset(oblk, 0.0)
        ve.memset(mstat, -1e30)
        ins = ve.memset(lstat, 0.0)
        m_init = cx.dv.inc(ins, 1)
        cx.ldq.wait(te)
        te.wait_ge(cx.dv.h, m_init)
        q3 = r3(qblk)
        kv_guard = {}
        tile_par = [0] * QB
        pending = []

        def emit_rest(st):
            (j2, ti2, special, bi_s, vta2) = st[:5]
            bank_s = cx.pb[bi_s]
            par = tile_par[ti2]
            mc_prev = mstat[:, ti2 * 2 + par:ti2 * 2 + par + 1]
            mc_new = mstat[:, ti2 * 2 + 1 - par:ti2 * 2 + 2 - par]
            tile_par[ti2] = 1 - par
            sidx = st[5]
            ve.wait_ge(cx.pe.h, st[6])
            if special:
                if pipe_state["mpp"] is not None:
                    ve.wait_ge(cx.pe.h, pipe_state["mpp"])
                msl = (m2d_sb[:, 0:128] if special == 1
                       else m2d_sb[:, 128:256])
                cx.ve_edge(ve.tensor_tensor(p_sb, bank_s[:, 0:128], msl,
                                            AL.add))
                srcb = p_sb
            else:
                srcb = bank_s[:, 0:128]
            cx.ve_edge(ve.reduce_max(colA, srcb, axis=AX.X))
            cx.ve_edge(ve.scalar_tensor_tensor(
                colA, colA, SCL, bias_sb[:, sidx:sidx + 1],
                AL.mult, AL.add))
            cx.ve_edge(ve.tensor_tensor(mc_new, mc_prev, colA, AL.max))
            ins = ve.tensor_sub(colB, bias_sb[:, sidx:sidx + 1], mc_new)
            m_mn = cx.dv.inc(ins, 1)
            cx.ve_edge(ins)
            se.wait_ge(cx.dv.h, m_mn)
            ins = se.activation(colC, mc_new, AF.Exp, bias=mc_prev,
                                scale=-1.0)
            if not special and pipe_state["mpp"] is not None:
                se.wait_ge(cx.pe.h, pipe_state["mpp"])
            ins = se.activation(p_sb, srcb, AF.Exp, bias=colB, scale=SCL)
            m_p = cx.ac.inc(ins, 1)
            cx.bank_consumed(bi_s, cx.ac, m_p)
            ve.wait_ge(cx.ac.h, m_p)
            cx.ve_edge(ve.reduce_sum(colD, p_sb, axis=AX.X))
            ins = ve.scalar_tensor_tensor(
                lstat[:, ti2:ti2 + 1], lstat[:, ti2:ti2 + 1], colC, colD,
                AL.mult, AL.add)
            m_l = cx.dv.inc(ins, 1)
            cx.ve_edge(ins)
            bank_t = cx.use_bank(2)
            te.wait_ge(cx.ac.h, m_p)
            ins = te.transpose(bank_t[:, 0:128], p_sb, K["identc"])
            m_t = cx.pe.inc(ins, 1)
            pipe_state["mpp"] = m_t
            ve.wait_ge(cx.pe.h, m_t)
            ins = ve.tensor_copy(pt_sb, bank_t[:, 0:128])
            m_ptc = cx.dv.inc(ins, 1)
            cx.bank_consumed(2, cx.dv, m_ptc)
            bank_a = cx.use_bank(3)
            bank_b = cx.use_bank(4)
            te.wait_ge(cx.dv.h, m_ptc)
            te.matmul(bank_a[:, 0:512], pt_sb, vta2[:, 0:512],
                      start=True, stop=True)
            ins = te.matmul(bank_b[:, 0:512], pt_sb, vta2[:, 512:1024],
                            start=True, stop=True)
            m_av = cx.pe.inc(ins, 1)
            kv_guard[j2] = m_av
            ve.wait_ge(cx.pe.h, m_av)
            ve.scalar_tensor_tensor(
                oblk3[:, ti2, 0:512], oblk3[:, ti2, 0:512], colC,
                bank_a[:, 0:512], AL.mult, AL.add)
            ins = ve.scalar_tensor_tensor(
                oblk3[:, ti2, 512:1024], oblk3[:, ti2, 512:1024], colC,
                bank_b[:, 0:512], AL.mult, AL.add)
            m_o = cx.dv.inc(ins, 1)
            cx.bank_consumed(3, cx.dv, m_o)
            cx.bank_consumed(4, cx.dv, m_o)

        for j in range(nslot):
            sl = j % 2
            kta, vta = ktb[sl], vtb[sl]
            while pending and pending[0][0] <= j - 2:
                emit_rest(pending.pop(0))
            if (j - 2) in kv_guard:
                sy.wait_ge(cx.pe.h, kv_guard[j - 2])
            cx.dma(sy, cx.ldk[sl], r3(kta),
                   sp["k_sp"].rearrange("(ec p) t -> p ec t", p=128)[
                       :, :, j * 128:(j + 1) * 128])
            m_kv = cx.dma(sy, cx.ldk[sl], vta,
                          sp["v_sp"][j * 128:(j + 1) * 128, :])
            s2_pump()
            cx.ldk[sl].wait(te)
            k3 = r3(kta)
            for ti in range(QB):
                i = bq * QB + ti
                if j > 16 + i:
                    continue
                special = 1 if j == i else (2 if j == 16 + i else 0)
                bi_s = step_idx % 2
                bank_s = cx.use_bank(bi_s)
                for dc in range(C):
                    ins = te.matmul(
                        bank_s[:, 0:128],
                        q3[:, dc, ti * 128:(ti + 1) * 128], k3[:, dc, :],
                        start=(dc == 0), stop=(dc == C - 1))
                m_s = cx.pe.inc(ins, 1)
                pending.append((j, ti, special, bi_s, vta, step_idx, m_s))
                step_idx += 1
                if len(pending) > 1:
                    emit_rest(pending.pop(0))
        while pending:
            emit_rest(pending.pop(0))
        if bq == nb_run - 1:
            while s2["split"] < len(s2_list):
                s2_pump()
        for ti in range(QB):
            i = bq * QB + ti
            cx.ve_edge(ve.reciprocal(colA, lstat[:, ti:ti + 1]))
            ins = ve.tensor_scalar_mul(oblk3[:, ti, :], oblk3[:, ti, :], colA)
            m_on = cx.dv.inc(ins, 1)
            cx.ve_edge(ins)
            te.wait_ge(cx.dv.h, m_on)
            for dc in range(C):
                bank = cx.use_bank(6)
                ins = te.transpose(bank[:, 0:128],
                                   oblk3[:, ti, dc * 128:(dc + 1) * 128],
                                   K["identc"])
                m_tt = cx.pe.inc(ins, 1)
                ve.wait_ge(cx.pe.h, m_tt)
                ins = ve.tensor_copy(xs3[:, dc, i * 128:(i + 1) * 128],
                                     bank[:, 0:128])
                m_xc = cx.dv.inc(ins, 1)
                cx.bank_consumed(6, cx.dv, m_xc)
    assert stop_after in ("attn1b", "attn1b_noret", "attn4b_noadd") or step_idx == NSTEPS

    if stop_after in ("attn1b", "attn1b_noret", "attn4b_noadd"):
        finish_stub(cx, out_d, zero=False)
        return

    m_rs1cc = None
    if cc is True:
        rs1x = cx.al(C)
        rs1t = cx.al(C)
        rs1own = cx.al(C)
        rs1msk = cx.al(8 * C)
        rs1scr = dict(bank=7, xsqc=cx.al(C), rowc=cx.al(4)[0:1, :],
                      repc=cx.al(2), ones=onesc, onesr=onesr)
        ins = sy.dma_start(
            out=rs1x.unsqueeze(2),
            in_=di["x_res"].rearrange("(dc p) t -> p dc t", p=128)[
                :, :, NT - 1:NT],
            allow_slow_non_contiguous=True)
        m_r1x = cx.ldc.inc(ins, 16)
        ve.wait_ge(cx.ldc.h, m_r1x)
        ve.wait_ge(cx.dv.h, cx.dv.v)
        ins = ve.tensor_tensor(rs1t, xs3[:, :, NT - 1], rs1x, AL.add)
        m_r1t = cx.dv.inc(ins, 1)
        cx.ve_edge(ins)
        se.wait_ge(cx.dv.h, m_r1t)
        te.wait_ge(cx.dv.h, m_r1t)
        emit_row_ln(cx, rs1t, rs1own, rs1scr)
        m_rs1cc = emit_rs_send(cx, rs1own, C, sp["rs1_in"], sp["rs1_out"],
                               K, rs1msk)

    res_war = [None, None]
    for g in range(NGT):
        sl = g % 2
        if res_war[sl] is not None:
            sy.wait_ge(cx.dv.h, res_war[sl])
        elif g == 0:
            sy.wait_ge(cx.dv.h, cx.dv.v)
        ins = sy.dma_start(
            out=r3(qblk)[:, :, sl * TG:(sl + 1) * TG],
            in_=di["x_res"].rearrange("(dc p) t -> p dc t", p=128)[
                :, :, g * TG:(g + 1) * TG])
        m_ld = cx.ldq.inc(ins, 16)
        ve.wait_ge(cx.ldq.h, m_ld)
        ins = ve.tensor_tensor(xs3[:, :, g * TG:(g + 1) * TG],
                               xs3[:, :, g * TG:(g + 1) * TG],
                               r3(qblk)[:, :, sl * TG:(sl + 1) * TG], AL.add)
        res_war[sl] = cx.dv.inc(ins, 1)
    cx.release()

    if stop_after == "attn":
        finish_stub(cx, out_d)
        return

    TF = 256
    cx.mark()
    wr_sb = cx.al(C * D)
    wk2_sb = cx.al(C * D)
    wv2_sb = cx.al(C * D)
    reg = {}
    for nm in ["xn", "xnsh", "diff", "xr", "xk"]:
        reg[nm] = cx.al(C * TF)
    reg["rb"] = reg["xn"]
    reg["kb"] = reg["xnsh"]
    reg["xsq"] = reg["xr"]
    f_row2 = cx.al(3 * TF)
    f_rep = reg["xnsh"][:, 0:2 * TF]
    reg["prevc"] = cx.al(C)
    mskbuf = cx.al(8 * C)
    rowc = cx.al(4)
    xsqc = cx.al(C)
    repc = cx.al(2)
    reg["scr"] = dict(xsq=reg["xsq"], row2=f_row2[0:1, :], rep=f_rep,
                      ones=onesc, onesr=onesr, bi_stats=2, bi_rep=3)
    scr2 = dict(bank=7, xsqc=xsqc, rowc=rowc[0:1, :], repc=repc,
                ones=onesc, onesr=onesr)
    sy.wait_ge(cx.pe.h, cx.pe.v)
    sy.wait_ge(cx.dv.h, cx.dv.v)
    sy.wait_ge(cx.ac.h, cx.ac.v)
    sy.wait_ge(cx.pot.h, cx.pot.v)
    m_fw0 = load_w_hl(wr_sb, "wrT", sy, cx.ldw[0])
    m_fw1 = load_w_hl(wk2_sb, "wk2T", sy, cx.ldw[1])
    load_w_hl(wv2_sb, "wv2T", sy, cx.ldw[0])
    ve.wait_ge(cx.ac.h, cx.ac.v)
    ve.wait_ge(cx.pe.h, cx.pe.v)
    se.wait_ge(cx.dv.h, cx.dv.v)
    se.wait_ge(cx.pe.h, cx.pe.v)
    if cc is True:
        emit_rs_recv(cx, sp["rs1_out"], ln2row, K, m_rs1cc)
    else:
        ins = ve.memset(ln2row, 0.0)
        cx.dv.inc(ins, 1)
    cx.ldw[0].wait(te)
    cx.ldw[1].wait(te)
    emit_ffn(cx, K, wr_sb, wk2_sb, wv2_sb, xs3, ln2row, reg, T=TF)
    cx.release()

    if stop_after == "ffn1":
        finish_stub(cx, out_d)
        return

    emit_attnb(cx, K, di, sp, xs3, onesc, onesr, ln4row,
               ("p1stop" if stop_after == "ab_p1" else
                ("p1nobp" if stop_after == "ab_p1nobp" else cc)))

    if stop_after in ("ab_p1", "ab_p1nobp"):
        finish_stub(cx, out_d, zero=True)
        return

    if stop_after == "attnb":
        finish_stub(cx, out_d)
        return

    TF = 256
    cx.mark()
    wrb = cx.al(C * D)
    wkb = cx.al(C * D)
    wvb = cx.al(C * D)
    reg = {}
    for nm in ["xn", "xnsh", "diff", "xr", "xk"]:
        reg[nm] = cx.al(C * TF)
    reg["rb"] = reg["xn"]
    reg["kb"] = reg["xnsh"]
    reg["xsq"] = reg["xr"]
    f_row2 = cx.al(3 * TF)
    f_rep = reg["xnsh"][:, 0:2 * TF]
    reg["prevc"] = cx.al(C)
    reg["scr"] = dict(xsq=reg["xsq"], row2=f_row2[0:1, :], rep=f_rep,
                      ones=onesc, onesr=onesr, bi_stats=2, bi_rep=3)
    sy.wait_ge(cx.pe.h, cx.pe.v)
    sy.wait_ge(cx.dv.h, cx.dv.v)
    sy.wait_ge(cx.ac.h, cx.ac.v)
    ve.wait_ge(cx.pe.h, cx.pe.v)
    ve.wait_ge(cx.ac.h, cx.ac.v)
    se.wait_ge(cx.pe.h, cx.pe.v)
    se.wait_ge(cx.dv.h, cx.dv.v)
    load_w_hl(wrb, "wrT", sy, cx.ldw[0])
    load_w_hl(wkb, "wk2T", sy, cx.ldw[1])
    load_w_hl(wvb, "wv2T", sy, cx.ldw[0])
    cx.ldw[0].wait(te)
    cx.ldw[1].wait(te)
    emit_ffn(cx, K, wrb, wkb, wvb, xs3, ln4row, reg, T=TF)
    cx.release()

    finish_stub(cx, out_d)


def finish_stub(cx, out_d, zero=False, dump=None):
    gp = cx.e["g"]
    ve = cx.e["v"]
    xs = cx.arena.ap()[:, 0:C * NT]
    if dump is not None:
        gp.wait_ge(cx.dv.h, cx.dv.v)
        gp.wait_ge(cx.ac.h, cx.ac.v)
        for s in (cx.sps + cx.spv + [cx.spo]):
            s.wait(gp)
        flat_out = out_d.rearrange("c p t -> (c p t)")
        ins = gp.dma_start(out=flat_out,
                           in_=dump.rearrange("a b -> (a b)")[0:C * 128 * NT])
        cx.spo.inc(ins, 16)
        cx.spo.wait(gp)
        return
    if zero:
        ins = ve.memset(xs, 0.0)
        cx.dv.inc(ins, 1)
    gp.wait_ge(cx.dv.h, cx.dv.v)
    gp.wait_ge(cx.ac.h, cx.ac.v)
    ins = gp.dma_start(out=out_d.rearrange("c p t -> p c t"), in_=r3(xs))
    cx.spo.inc(ins, 16)
    for s in ([cx.ldc, cx.ldq, cx.spo] + cx.lds + cx.ldw + cx.ldk +
              cx.sps + cx.spv):
        s.wait(gp)



def host_inputs(full, core):
    b, h = core // 2, core % 2
    X = np.asarray(full["L0_attn_a_X"])
    XF = np.ascontiguousarray(X[b].T)
    d = {}
    d["x_kv"] = XF
    d["x_res"] = np.ascontiguousarray(XF[:, 2048 * h:2048 * h + 2048])
    d["wqT"] = np.ascontiguousarray(np.asarray(full["L0_attn_a_Wq"]).T)
    d["wkT"] = np.ascontiguousarray(np.asarray(full["L0_attn_a_Wk"]).T)
    d["wvT"] = np.ascontiguousarray(np.asarray(full["L0_attn_a_Wv"]).T)
    d["wrT"] = np.ascontiguousarray(np.asarray(full["L0_ffn_W_r"]).T)
    d["wk2T"] = np.ascontiguousarray(np.asarray(full["L0_ffn_W_k"]).T)
    d["wv2T"] = np.ascontiguousarray(np.asarray(full["L0_ffn_W_v"]).T)
    d["woutT"] = np.ascontiguousarray(np.asarray(full["L0_attn_b_W_out"]).T)
    d["winT"] = np.ascontiguousarray(np.asarray(full["L0_attn_b_W_in"]).T)
    d["wbT"] = np.ascontiguousarray(np.asarray(full["L0_attn_b_W_B"]).T)
    d["wcT"] = np.ascontiguousarray(np.asarray(full["L0_attn_b_W_C"]).T)

    def fm(v):
        return np.ascontiguousarray(np.asarray(v).reshape(C, 128).T)

    d["bv_row"] = np.asarray(full["L0_attn_a_bv"]).reshape(1, D)
    d["bq"] = fm(full["L0_attn_a_bq"])
    d["bk"] = fm(full["L0_attn_a_bk"])
    d["bv"] = fm(full["L0_attn_a_bv"])
    d["bout"] = fm(full["L0_attn_b_b_out"])
    d["mu_r"] = fm(full["L0_ffn_mu_r"])
    d["mu_k"] = fm(full["L0_ffn_mu_k"])
    d["lng"] = fm(full["L0_attn_b_ln_g"])
    d["lnb"] = fm(full["L0_attn_b_ln_b"])
    d["dskip"] = fm(full["L0_attn_b_D_skip"])
    d["logdt"] = fm(full["L0_attn_b_log_dt"])
    d["b_in"] = np.ascontiguousarray(
        np.asarray(full["L0_attn_b_b_in"]).reshape(2 * C, 128).T)
    al = np.asarray(full["L0_attn_b_A_log"]).reshape(C, 128, S)
    d["alog"] = np.ascontiguousarray(al.transpose(1, 0, 2).reshape(128, C * S))

    bias_cols = np.zeros((128, NSTEPS), np.float32)
    for sidx, (bq_, i, j) in enumerate(ATT_STEPS):
        g = 16 * h + i
        if j == i or j == 16 + i:
            val = 0.0
        elif j > g:
            val = -BIGB
        else:
            val = 0.0
        bias_cols[:, sidx] = val
    d["bias_cols"] = bias_cols
    diag = np.where(np.arange(128)[:, None] >= np.arange(128)[None, :],
                    0.0, -BIG2).astype(np.float32)
    m2d = np.zeros((128, 256), np.float32)
    if h == 0:
        m2d[:, 0:128] = diag
        m2d[:, 128:256] = -BIG2
    else:
        m2d[:, 128:256] = diag
    d["m2d"] = m2d
    d["gate1"] = np.full((128, 1), float(h), np.float32)
    slot8 = np.zeros((128, 8), np.float32)
    slot8[:, core ^ 1] = 1.0
    d["slot8"] = slot8
    d["iota_t"] = np.broadcast_to(
        np.arange(128, dtype=np.float32)[None, :], (128, 128)).copy()
    d["iota_r"] = np.broadcast_to(
        (128.0 - np.arange(128, dtype=np.float32))[None, :], (128, 128)).copy()
    d["iota_p"] = np.broadcast_to(
        (1.0 + np.arange(128, dtype=np.float32))[None, :], (128, 128)).copy()
    d["arev"] = np.eye(128, dtype=np.float32)[::-1].copy()
    d["identc"] = np.eye(128, dtype=np.float32)
    return {k: np.ascontiguousarray(v, dtype=np.float32) for k, v in d.items()}


def kernel(**inputs):
    nc = build_program()
    in_maps = [host_inputs(inputs, c) for c in range(8)]
    res = run_bass_kernel_spmd(nc, in_maps, core_ids=list(range(8)))
    out = np.empty((4, 4096, 1024), np.float32)
    for c in range(8):
        b, h = c // 2, c % 2
        o = np.asarray(res.results[c]["out_fm"])
        out[b, 2048 * h:2048 * h + 2048, :] = o.reshape(1024, 2048).T
    return out


# version 23
# speedup vs baseline: 1.0800x; 1.0675x over previous
import numpy as np
import concourse.bass as bass
import concourse.mybir as mybir
from concourse.bass_utils import run_bass_kernel_spmd

F32 = mybir.dt.float32
BF16 = mybir.dt.bfloat16
AL = mybir.AluOpType
AF = mybir.ActivationFunctionType
AX = mybir.AxisListType

D = 1024
C = 8
NT = 2048
NK = 4096
TG = 256
NGK = NK // TG
NGT = NT // TG
NQT = NT // 128
QB = 4
NB = NQT // QB
S = 16
NCH = 32
BIGB = 30000.0
BIG2 = 3.0e6
SCL = 1.0 / 32.0


def attn_steps():
    steps = []
    for bq in range(NB):
        nslot = 20 + 4 * bq
        for j in range(nslot):
            for ti in range(QB):
                i = bq * QB + ti
                if j <= 16 + i:
                    steps.append((bq, i, j))
    return steps


ATT_STEPS = attn_steps()
NSTEPS = len(ATT_STEPS)


class RCmd:
    __slots__ = ("name", "a", "k", "incs")

    def __init__(self, name, a, k):
        self.name = name
        self.a = a
        self.k = k
        self.incs = []

    def then_inc(self, sem_h, n):
        self.incs.append((sem_h, n))
        return self


class REng:

    def __init__(self):
        self.cmds = []

    def __getattr__(self, name):
        def call(*a, **k):
            c = RCmd(name, a, k)
            self.cmds.append(c)
            return c
        return call

    def replay(self, eng):
        for c in self.cmds:
            ins = getattr(eng, c.name)(*c.a, **c.k)
            for (sh, n) in c.incs:
                ins.then_inc(sh, n)


class Sem:
    def __init__(self, nc, name):
        self.h = nc.alloc_semaphore(name)
        self.v = 0

    def inc(self, ins, k):
        ins.then_inc(self.h, k)
        self.v += k
        return self.v

    def wait(self, eng):
        eng.wait_ge(self.h, self.v)


class Ctx:
    def __init__(self, nc):
        self.nc = nc
        self.e = {}
        self.ldc = Sem(nc, "s_ldc")
        self.lds = [Sem(nc, f"s_lds{i}") for i in range(2)]
        self.ldw = [Sem(nc, f"s_ldw{i}") for i in range(2)]
        self.ldk = [Sem(nc, f"s_ldk{i}") for i in range(2)]
        self.ldq = Sem(nc, "s_ldq")
        self.sps = [Sem(nc, f"s_sps{i}") for i in range(2)]
        self.spv = [Sem(nc, f"s_spv{i}") for i in range(2)]
        self.spo = Sem(nc, "s_spo")
        self.pin = Sem(nc, "s_pin")
        self.pot = Sem(nc, "s_pot")
        self.pe = Sem(nc, "s_pe")
        self.dv = Sem(nc, "s_dv")
        self.ac = Sem(nc, "s_ac")
        self.cc = Sem(nc, "s_cc")
        self.dq = Sem(nc, "s_dq")
        self.sem_of = {}
        for s in [self.ldc, self.ldq, self.spo, self.pin, self.pot, self.pe,
                  self.dv, self.ac,
                  self.cc, self.dq] + self.lds + self.ldw + self.ldk +                  self.sps + self.spv:
            self.sem_of[id(s.h)] = s
        self.AW = 53200
        self.arena = nc.alloc_sbuf_tensor("arena", [128, self.AW], F32)
        self.off = 0
        self.marks = []
        self.pb = [nc.alloc_psum_tensor(f"pb{i}", [128, 512], F32)
                   for i in range(8)]
        self.bank_free = [None] * 8

    def al(self, words):
        assert self.off + words <= self.AW, (self.off, words)
        ap = self.arena.ap()[:, self.off:self.off + words]
        self.off += words
        return ap

    def mark(self):
        self.marks.append(self.off)

    def release(self):
        self.off = self.marks.pop()

    def use_bank(self, i):
        bf = self.bank_free[i]
        if bf is not None:
            self.e["t"].wait_ge(bf[0].h, bf[1])
            self.bank_free[i] = None
        return self.pb[i]

    def bank_consumed(self, i, sem, val):
        self.bank_free[i] = (sem, val)

    def dma(self, eng, sem, out, in_, **kw):
        eng.wait_ge(sem.h, sem.v)
        ins = eng.dma_start(out=out, in_=in_, **kw)
        return sem.inc(ins, 16)

    def ve_edge(self, ins):
        if getattr(ins, "incs", None):
            sh, _ = ins.incs[0]
            sem = self.sem_of.get(id(sh))
            assert sem is not None, "ve_edge: unknown sem on instruction"
            self.e["v"].wait_ge(sem.h, sem.v)
        else:
            v = self.dq.inc(ins, 1)
            self.e["v"].wait_ge(self.dq.h, v)



def r3(ap, c=C):
    return ap.rearrange("p (c t) -> p c t", c=c)


def bf(ap):
    return ap.bitcast(BF16)


def emit_split(cx, x3f, xh3, xl3):
    ve = cx.e["v"]
    ins = ve.tensor_copy(xh3, x3f)
    cx.ve_edge(ins)
    ins = ve.tensor_tensor(xl3, x3f, xh3, AL.subtract)
    m = cx.dv.inc(ins, 1)
    return m


def emit_proj3(cx, T, wp, x_pair, out_ec, bi, nec=C, bias=None, act=AF.Copy,
               e_tot=D, hooks_te=None, hooks_ve=None):
    te, se = cx.e["t"], cx.e["s"]
    hooks_te = hooks_te or {}
    hooks_ve = hooks_ve or {}
    xh3, xl3 = x_pair
    w4 = wp.rearrange("p (two dc e) -> p two dc e", two=2, dc=C)
    m_ev = None
    for ec in range(nec):
        bank = cx.use_bank(bi)
        first = True
        for dc in range(C):
            wh = w4[:, 0, dc, 128 * ec:128 * (ec + 1)]
            wl = w4[:, 1, dc, 128 * ec:128 * (ec + 1)]
            for (wt, xt) in ((wh, xh3), (wh, xl3), (wl, xh3)):
                last = (dc == C - 1) and (wt is wl)
                ins = te.matmul(bank[:, 0:T], wt, xt[:, dc, :],
                                start=first, stop=last)
                first = False
        m_pe = cx.pe.inc(ins, 1)
        if ec in hooks_te:
            hooks_te[ec]()
        b = bias[:, ec:ec + 1] if bias is not None else 0.0
        if act == AF.Copy:
            ve = cx.e["v"]
            ve.wait_ge(cx.pe.h, m_pe)
            if bias is not None:
                ins = ve.tensor_scalar_add(out_ec(ec), bank[:, 0:T], b)
            else:
                ins = ve.tensor_copy(out_ec(ec), bank[:, 0:T])
            m_ev = cx.dv.inc(ins, 1)
            cx.bank_consumed(bi, cx.dv, m_ev)
            if ec in hooks_ve:
                hooks_ve[ec]()
        else:
            se.wait_ge(cx.pe.h, m_pe)
            ins = se.activation(out_ec(ec), bank[:, 0:T], act, bias=b)
            m_ev = cx.ac.inc(ins, 1)
            cx.bank_consumed(bi, cx.ac, m_ev)
            if ec in hooks_ve:
                hooks_ve[ec]()
    return m_ev



def emit_ln(cx, T, x_src3, xn_out3, scr, gamma=None, beta=None):
    te, ve, se = cx.e["t"], cx.e["v"], cx.e["s"]
    xsq3 = r3(scr["xsq"][:, :C * T])
    ins = se.activation(xsq3, x_src3, AF.Square)
    m_sq = cx.ac.inc(ins, 1)

    pb = cx.use_bank(scr["bi_stats"])
    te.wait_ge(cx.ac.h, m_sq)
    for ci in range(C):
        te.matmul(pb[0:1, 0:T], scr["ones"], x_src3[:, ci, :],
                  start=(ci == 0), stop=False)
        ins = te.matmul(pb[0:1, T:2 * T], scr["ones"], xsq3[:, ci, :],
                        start=False, stop=(ci == C - 1))
    m_pe = cx.pe.inc(ins, 1)

    row = scr["row2"]
    mu = row[0:1, 0:T]
    var = row[0:1, T:2 * T]
    tmp = row[0:1, 2 * T:3 * T]
    ve.wait_ge(cx.pe.h, m_pe)
    ins = ve.tensor_scalar_mul(row[0:1, 0:2 * T], pb[0:1, 0:2 * T], 1.0 / D)
    m_dv = cx.dv.inc(ins, 1)
    cx.bank_consumed(scr["bi_stats"], cx.dv, m_dv)
    cx.ve_edge(ins)
    cx.ve_edge(ve.tensor_mul(tmp, mu, mu))
    cx.ve_edge(ve.tensor_sub(var, var, tmp))
    cx.ve_edge(ve.tensor_scalar_add(var, var, 1e-5))
    ins = ve.reciprocal(tmp, var)
    m_rc = cx.dv.inc(ins, 1)
    rstd = var
    se.wait_ge(cx.dv.h, m_rc)
    ins = se.sqrt(rstd, tmp)
    m_rs = cx.ac.inc(ins, 1)
    ve.wait_ge(cx.ac.h, m_rs)
    ins = ve.tensor_mul(mu, mu, rstd)
    m_mr = cx.dv.inc(ins, 1)
    cx.ve_edge(ins)

    pb2 = cx.use_bank(scr["bi_rep"])
    te.wait_ge(cx.dv.h, m_mr)
    te.matmul(pb2[:, 0:T], scr["onesr"], rstd, start=True, stop=False)
    ins = te.matmul(pb2[:, T:2 * T], scr["onesr"], mu, start=False, stop=True)
    m_rep = cx.pe.inc(ins, 1)
    ve.wait_ge(cx.pe.h, m_rep)
    rep = scr["rep"]
    ins = ve.tensor_copy(rep[:, 0:2 * T], pb2[:, 0:2 * T])
    m_repc = cx.dv.inc(ins, 1)
    cx.bank_consumed(scr["bi_rep"], cx.dv, m_repc)
    cx.ve_edge(ins)

    rrep = rep[:, 0:T].unsqueeze(1).broadcast_to([128, C, T])
    mrep = rep[:, T:2 * T].unsqueeze(1).broadcast_to([128, C, T])
    cx.ve_edge(ve.tensor_tensor(xn_out3, x_src3, rrep, AL.mult))
    ins = ve.tensor_tensor(xn_out3, xn_out3, mrep, AL.subtract)
    if gamma is not None:
        cx.ve_edge(ins)
        for ci in range(C):
            ins = ve.scalar_tensor_tensor(
                xn_out3[:, ci, :], xn_out3[:, ci, :], gamma[:, ci:ci + 1],
                beta[:, ci:ci + 1].broadcast_to([128, T]), AL.mult, AL.add)
    m_xn = cx.dv.inc(ins, 1)
    cx.ve_edge(ins)
    return m_xn


def make_ln_stages(cx, T, src3, out3, scr, gamma=None, beta=None):
    te, ve, se = cx.e["t"], cx.e["v"], cx.e["s"]
    xsq3 = r3(scr["xsq"][:, :C * T])
    row2 = scr["row2"]
    rep = scr["rep"]
    st = {}

    def sq(guard=None, dv_guard=None):
        if guard is not None:
            se.wait_ge(cx.pe.h, guard)
        if dv_guard is not None:
            se.wait_ge(cx.dv.h, dv_guard)
        ins = se.activation(xsq3, src3, AF.Square)
        st["sq"] = cx.ac.inc(ins, 1)
        return st["sq"]

    def stats():
        pb = cx.use_bank(scr["bi_stats"])
        te.wait_ge(cx.ac.h, st["sq"])
        for ci in range(C):
            te.matmul(pb[0:1, 0:T], scr["ones"], src3[:, ci, :],
                      start=(ci == 0), stop=False)
            ins = te.matmul(pb[0:1, T:2 * T], scr["ones"], xsq3[:, ci, :],
                            start=False, stop=(ci == C - 1))
        st["stats"] = cx.pe.inc(ins, 1)
        st["pb"] = pb
        return st["stats"]

    def row():
        mu = row2[0:1, 0:T]
        var = row2[0:1, T:2 * T]
        tmp = row2[0:1, 2 * T:3 * T]
        ve.wait_ge(cx.pe.h, st["stats"])
        ins = ve.tensor_scalar_mul(row2[0:1, 0:2 * T], st["pb"][0:1, 0:2 * T],
                                   1.0 / D)
        m_dv = cx.dv.inc(ins, 1)
        cx.bank_consumed(scr["bi_stats"], cx.dv, m_dv)
        cx.ve_edge(ins)
        cx.ve_edge(ve.tensor_mul(tmp, mu, mu))
        cx.ve_edge(ve.tensor_sub(var, var, tmp))
        cx.ve_edge(ve.tensor_scalar_add(var, var, 1e-5))
        ins = ve.reciprocal(tmp, var)
        m_rc = cx.dv.inc(ins, 1)
        rstd = var
        se.wait_ge(cx.dv.h, m_rc)
        ins = se.sqrt(rstd, tmp)
        m_rs = cx.ac.inc(ins, 1)
        ve.wait_ge(cx.ac.h, m_rs)
        ins = ve.tensor_mul(mu, mu, rstd)
        st["row"] = cx.dv.inc(ins, 1)
        cx.ve_edge(ins)

    def repmm():
        mu = row2[0:1, 0:T]
        rstd = row2[0:1, T:2 * T]
        pb2 = cx.use_bank(scr["bi_rep"])
        te.wait_ge(cx.dv.h, st["row"])
        te.matmul(pb2[:, 0:T], scr["onesr"], rstd, start=True, stop=False)
        ins = te.matmul(pb2[:, T:2 * T], scr["onesr"], mu, start=False,
                        stop=True)
        st["rep"] = cx.pe.inc(ins, 1)
        st["pb2"] = pb2

    def w_repcopy(guard=None):
        ve.wait_ge(cx.pe.h, st["rep"])
        if guard is not None:
            ve.wait_ge(cx.pe.h, guard)
        ins = ve.tensor_copy(rep[:, 0:2 * T], st["pb2"][:, 0:2 * T])
        m_rc = cx.dv.inc(ins, 1)
        cx.bank_consumed(scr["bi_rep"], cx.dv, m_rc)
        cx.ve_edge(ins)

    rrep = rep[:, 0:T].unsqueeze(1).broadcast_to([128, C, T])
    mrep = rep[:, T:2 * T].unsqueeze(1).broadcast_to([128, C, T])

    def w_mult1():
        cx.ve_edge(ve.tensor_tensor(out3, src3, rrep, AL.mult))

    def w_mult2():
        ins = ve.tensor_tensor(out3, out3, mrep, AL.subtract)
        st["xn"] = cx.dv.inc(ins, 1)
        cx.ve_edge(ins)

    def w_gb():
        ins = None
        for ci in range(C):
            ins = ve.scalar_tensor_tensor(
                out3[:, ci, :], out3[:, ci, :], gamma[:, ci:ci + 1],
                beta[:, ci:ci + 1].broadcast_to([128, T]), AL.mult, AL.add)
        st["xn"] = cx.dv.inc(ins, 1)
        cx.ve_edge(ins)

    E = [w_repcopy, w_mult1, w_mult2]
    if gamma is not None:
        E.append(w_gb)
    return dict(sq=sq, stats=stats, row=row, repmm=repmm, E=E, st=st)


def emit_proj(cx, T, w_sb, xn3, out_ec, bi, nec=C, bias=None, act=AF.Copy,
              pre_wait=None):
    te, se = cx.e["t"], cx.e["s"]
    w3 = w_sb.rearrange("p (dc e) -> p dc e", dc=C)
    m_ev = None
    for ec in range(nec):
        bank = cx.use_bank(bi)
        if pre_wait is not None:
            pre_wait()
            pre_wait = None
        for dc in range(C):
            ins = te.matmul(bank[:, 0:T], w3[:, dc, 128 * ec:128 * (ec + 1)],
                            xn3[:, dc, :], start=(dc == 0), stop=(dc == C - 1))
        m_pe = cx.pe.inc(ins, 1)
        b = bias[:, ec:ec + 1] if bias is not None else 0.0
        if act == AF.Copy:
            ve = cx.e["v"]
            ve.wait_ge(cx.pe.h, m_pe)
            if bias is not None:
                ins = ve.tensor_scalar_add(out_ec(ec), bank[:, 0:T], b)
            else:
                ins = ve.tensor_copy(out_ec(ec), bank[:, 0:T])
            m_ev = cx.dv.inc(ins, 1)
            cx.bank_consumed(bi, cx.dv, m_ev)
            if ec in hooks_ve:
                hooks_ve[ec]()
        else:
            se.wait_ge(cx.pe.h, m_pe)
            ins = se.activation(out_ec(ec), bank[:, 0:T], act, bias=b)
            m_ev = cx.ac.inc(ins, 1)
            cx.bank_consumed(bi, cx.ac, m_ev)
            if ec in hooks_ve:
                hooks_ve[ec]()
    return m_ev




def emit_row_ln(cx, xcol, out_col, scr2):
    te, ve, se = cx.e["t"], cx.e["v"], cx.e["s"]
    ins = se.activation(scr2["xsqc"], xcol, AF.Square)
    m_sq = cx.ac.inc(ins, 1)
    pb = cx.use_bank(scr2["bank"])
    te.wait_ge(cx.ac.h, m_sq)
    te.matmul(pb[0:1, 0:C], scr2["ones"], xcol, start=True, stop=False)
    ins = te.matmul(pb[0:1, C:2 * C], scr2["ones"], scr2["xsqc"],
                    start=False, stop=True)
    m_pe = cx.pe.inc(ins, 1)
    row = scr2["rowc"]
    ve.wait_ge(cx.pe.h, m_pe)
    cx.ve_edge(ve.reduce_sum(row[0:1, 0:1], pb[0:1, 0:C], axis=AX.X))
    ins = ve.reduce_sum(row[0:1, 1:2], pb[0:1, C:2 * C], axis=AX.X)
    m_dv = cx.dv.inc(ins, 1)
    cx.bank_consumed(scr2["bank"], cx.dv, m_dv)
    cx.ve_edge(ins)
    mu = row[0:1, 0:1]
    s2 = row[0:1, 1:2]
    tmp = row[0:1, 2:3]
    cx.ve_edge(ve.tensor_scalar_mul(mu, mu, 1.0 / D))
    cx.ve_edge(ve.tensor_scalar_mul(s2, s2, 1.0 / D))
    cx.ve_edge(ve.tensor_mul(tmp, mu, mu))
    cx.ve_edge(ve.tensor_sub(s2, s2, tmp))
    cx.ve_edge(ve.tensor_scalar_add(s2, s2, 1e-5))
    ins = ve.reciprocal(tmp, s2)
    m_rc = cx.dv.inc(ins, 1)
    se.wait_ge(cx.dv.h, m_rc)
    ins = se.sqrt(s2, tmp)
    m_rs = cx.ac.inc(ins, 1)
    pb2 = cx.use_bank(scr2["bank"])
    te.wait_ge(cx.ac.h, m_rs)
    ins = te.matmul(pb2[:, 0:2], scr2["onesr"], row[0:1, 0:2],
                    start=True, stop=True)
    m_rep = cx.pe.inc(ins, 1)
    ve.wait_ge(cx.pe.h, m_rep)
    ins = ve.tensor_copy(scr2["repc"], pb2[:, 0:2])
    m_rpc = cx.dv.inc(ins, 1)
    cx.bank_consumed(scr2["bank"], cx.dv, m_rpc)
    cx.ve_edge(ins)
    ins = ve.scalar_tensor_tensor(
        out_col, xcol, scr2["repc"][:, 0:1],
        scr2["repc"][:, 1:2].broadcast_to([128, C]), AL.subtract, AL.mult)
    m_o = cx.dv.inc(ins, 1)
    cx.ve_edge(ins)
    return m_o


def emit_rs_send(cx, payload, nslots_words, rs_in, rs_out, K, mskbuf):
    ve, gp = cx.e["v"], cx.e["g"]
    W = nslots_words
    mb3 = mskbuf.rearrange("p (s w) -> p s w", s=8)
    for k in range(8):
        ins = ve.tensor_scalar_mul(mb3[:, k, :], payload,
                                   K["slot8"][:, k:k + 1])
    m_msk = cx.dv.inc(ins, 1)
    cx.ve_edge(ins)
    gp.wait_ge(cx.dv.h, m_msk)
    ins = gp.dma_start(
        out=rs_in.rearrange("s (p w) -> p s w", p=128),
        in_=mb3)
    m_in = cx.spo.inc(ins, 16)
    gp.wait_ge(cx.spo.h, m_in)
    ins = gp.collective_compute(
        "ReduceScatter", AL.add, replica_groups=[list(range(8))],
        ins=[rs_in.opt()], outs=[rs_out.opt()])
    return cx.cc.inc(ins, 1)


def emit_rs_recv(cx, rs_out, recv_buf, K, m_cc):
    ve, sy = cx.e["v"], cx.e["y"]
    sy.wait_ge(cx.cc.h, m_cc)
    ins = sy.dma_start(out=recv_buf.unsqueeze(1),
                       in_=rs_out.rearrange("o (p w) -> p o w", p=128))
    m_rv = cx.ldc.inc(ins, 16)
    cx.ldc.wait(ve)
    ins = ve.tensor_scalar_mul(recv_buf, recv_buf, K["gate1"][:, 0:1])
    m_g = cx.dv.inc(ins, 1)
    cx.ve_edge(ins)
    return m_g


def emit_rs(cx, payload, nslots_words, rs_in, rs_out, recv_buf, K, mskbuf):
    m_cc = emit_rs_send(cx, payload, nslots_words, rs_in, rs_out, K, mskbuf)
    return emit_rs_recv(cx, rs_out, recv_buf, K, m_cc)


def emit_ffn(cx, K, wr_sb, wk2_sb, wv2_sb, xs3, bound_col, reg, T=128):
    te, ve, se, gp, sy = (cx.e["t"], cx.e["v"], cx.e["s"], cx.e["g"],
                          cx.e["y"])

    def rr(ap):
        return ap[:, 0:C * T].rearrange("p (c t) -> p c t", c=C)
    scr = reg["scr"]
    B_xn, B_sh, B_xr = reg["xn"], reg["xnsh"], reg["xr"]
    B_kp, B_r = reg["xk"], reg["diff"]
    xn3 = rr(B_xn)
    xnsh3 = rr(B_sh)
    xr3 = rr(B_xr)
    xsq3 = rr(B_xr)
    kb3 = rr(B_sh)
    rb3 = rr(B_r)
    xrh3 = r3(bf(B_xn[:, 0:C * T // 2]))
    xrl3 = r3(bf(B_xn[:, C * T // 2:C * T]))
    xkh3 = r3(bf(B_xr[:, 0:C * T // 2]))
    xkl3 = r3(bf(B_xr[:, C * T // 2:C * T]))
    kah3 = r3(bf(B_kp[:, 0:C * T // 2]))
    kal3 = r3(bf(B_kp[:, C * T // 2:C * T]))
    prevc = reg["prevc"]
    row2 = scr["row2"]
    rep = scr["rep"]
    onesv = scr["ones"]
    onesr_v = scr["onesr"]
    NG = NT // T
    st = {"kmm": None, "sq": None, "stats": None, "rep": None,
          "sxr": None, "sxk": None}

    def stage_sq(g):
        if st["kmm"] is not None:
            se.wait_ge(cx.pe.h, st["kmm"])
        ins = se.activation(xsq3, xs3[:, :, g * T:(g + 1) * T], AF.Square)
        st["sq"] = cx.ac.inc(ins, 1)

    def stage_stats(g):
        pb = cx.use_bank(scr["bi_stats"])
        te.wait_ge(cx.ac.h, st["sq"])
        xsrc3 = xs3[:, :, g * T:(g + 1) * T]
        for ci in range(C):
            te.matmul(pb[0:1, 0:T], onesv, xsrc3[:, ci, :],
                      start=(ci == 0), stop=False)
            ins = te.matmul(pb[0:1, T:2 * T], onesv, xsq3[:, ci, :],
                            start=False, stop=(ci == C - 1))
        st["stats"] = (cx.pe.inc(ins, 1), pb)

    def stage_row():
        m_pe, pb = st["stats"]
        mu = row2[0:1, 0:T]
        var = row2[0:1, T:2 * T]
        tmp = row2[0:1, 2 * T:3 * T]
        ve.wait_ge(cx.pe.h, m_pe)
        ins = ve.tensor_scalar_mul(row2[0:1, 0:2 * T], pb[0:1, 0:2 * T],
                                   1.0 / D)
        m_dv = cx.dv.inc(ins, 1)
        cx.bank_consumed(scr["bi_stats"], cx.dv, m_dv)
        cx.ve_edge(ins)
        cx.ve_edge(ve.tensor_mul(tmp, mu, mu))
        cx.ve_edge(ve.tensor_sub(var, var, tmp))
        cx.ve_edge(ve.tensor_scalar_add(var, var, 1e-5))
        ins = ve.reciprocal(tmp, var)
        m_rc = cx.dv.inc(ins, 1)
        rstd = var
        se.wait_ge(cx.dv.h, m_rc)
        ins = se.sqrt(rstd, tmp)
        m_rs = cx.ac.inc(ins, 1)
        ve.wait_ge(cx.ac.h, m_rs)
        ins = ve.tensor_mul(mu, mu, rstd)
        st["row"] = cx.dv.inc(ins, 1)
        cx.ve_edge(ins)

    def stage_repmm():
        mu = row2[0:1, 0:T]
        rstd = row2[0:1, T:2 * T]
        pb2 = cx.use_bank(scr["bi_rep"])
        te.wait_ge(cx.dv.h, st["row"])
        te.matmul(pb2[:, 0:T], onesr_v, rstd, start=True, stop=False)
        ins = te.matmul(pb2[:, T:2 * T], onesr_v, mu, start=False, stop=True)
        st["rep"] = (cx.pe.inc(ins, 1), pb2)

    def make_E(g):
        xsrc3 = xs3[:, :, g * T:(g + 1) * T]
        rrep = rep[:, 0:T].unsqueeze(1).broadcast_to([128, C, T])
        mrep = rep[:, T:2 * T].unsqueeze(1).broadcast_to([128, C, T])

        def w_repcopy():
            m_pe, pb2 = st["rep"]
            ve.wait_ge(cx.pe.h, m_pe)
            ins = ve.tensor_copy(rep[:, 0:2 * T], pb2[:, 0:2 * T])
            m_rc = cx.dv.inc(ins, 1)
            cx.bank_consumed(scr["bi_rep"], cx.dv, m_rc)
            cx.ve_edge(ins)

        def w_xn1():
            cx.ve_edge(ve.tensor_tensor(xn3, xsrc3, rrep, AL.mult))

        def w_xn2():
            cx.ve_edge(ve.tensor_tensor(xn3, xn3, mrep, AL.subtract))

        def w_shift():
            cx.ve_edge(ve.tensor_copy(xnsh3[:, :, 1:T], xn3[:, :, 0:T - 1]))
            if g == 0:
                ins = ve.tensor_copy(xnsh3[:, :, 0:1],
                                     bound_col.unsqueeze(2))
            else:
                ins = ve.tensor_copy(xnsh3[:, :, 0:1], prevc.unsqueeze(2))
            cx.ve_edge(ins)
            cx.ve_edge(ve.tensor_copy(prevc, xn3[:, :, T - 1]))

        def w_diff():
            cx.ve_edge(ve.tensor_tensor(xn3, xn3, xnsh3, AL.subtract))

        def w_mixr():
            ins = None
            for c in range(C):
                ins = ve.scalar_tensor_tensor(
                    xr3[:, c, :], xn3[:, c, :], K["mu_r"][:, c:c + 1],
                    xnsh3[:, c, :], AL.mult, AL.add)
            cx.ve_edge(ins)

        def w_mixk():
            ins = None
            for c in range(C):
                ins = ve.scalar_tensor_tensor(
                    xnsh3[:, c, :], xn3[:, c, :], K["mu_k"][:, c:c + 1],
                    xnsh3[:, c, :], AL.mult, AL.add)
            cx.ve_edge(ins)

        return [w_repcopy, w_xn1, w_xn2, w_shift, w_diff, w_mixr, w_mixk]

    def post_splits(g):
        st["sxr"] = emit_split(cx, xr3, xrh3, xrl3)
        st["sxk"] = emit_split(cx, rr(B_sh), xkh3, xkl3)

    stage_sq(0)
    stage_stats(0)
    stage_row()
    stage_repmm()
    for w in make_E(0):
        w()
    post_splits(0)

    for g in range(NG):
        xsrc3 = xs3[:, :, g * T:(g + 1) * T]
        te.wait_ge(cx.dv.h, st["sxr"])
        emit_proj3(cx, T, bf(wr_sb), (xrh3, xrl3), lambda ec: rb3[:, ec, :],
                   4, act=AF.Sigmoid)
        te.wait_ge(cx.dv.h, st["sxk"])
        emit_proj3(cx, T, bf(wk2_sb), (xkh3, xkl3),
                   lambda ec: kb3[:, ec, :], 5, act=AF.Relu)
        st["kmm"] = cx.pe.v
        ins = se.activation(kb3, kb3, AF.Square)
        m_kact = cx.ac.inc(ins, 1)
        if g + 1 < NG:
            stage_sq(g + 1)
        ve.wait_ge(cx.ac.h, m_kact)
        m_ska = emit_split(cx, kb3, kah3, kal3)
        te.wait_ge(cx.dv.h, m_ska)
        E = make_E(g + 1) if g + 1 < NG else []
        wv4 = bf(wv2_sb).rearrange("p (two dc e) -> p two dc e", two=2, dc=C)
        for ec in range(C):
            bank = cx.use_bank(6)
            first = True
            for dc in range(C):
                wh = wv4[:, 0, dc, 128 * ec:128 * (ec + 1)]
                wl = wv4[:, 1, dc, 128 * ec:128 * (ec + 1)]
                for pi, (wt, xt3) in enumerate(((wh, kah3), (wh, kal3),
                                               (wl, kah3))):
                    last = (dc == C - 1) and pi == 2
                    ins = te.matmul(bank[:, 0:T], wt, xt3[:, dc, :],
                                    start=first, stop=last)
                    first = False
            m_pe = cx.pe.inc(ins, 1)
            if g + 1 < NG and ec == 1:
                stage_stats(g + 1)
            if g + 1 < NG and ec == 3:
                stage_repmm()
            ve.wait_ge(cx.pe.h, m_pe)
            ins = ve.tensor_mul(bank[:, 0:T], bank[:, 0:T], rb3[:, ec, :])
            cx.ve_edge(ins)
            ins = ve.tensor_tensor(xsrc3[:, ec, :], xsrc3[:, ec, :],
                                   bank[:, 0:T], AL.add)
            m_ev = cx.dv.inc(ins, 1)
            cx.ve_edge(ins)
            cx.bank_consumed(6, cx.dv, m_ev)
            if g + 1 < NG:
                if ec == 2:
                    stage_row()
                elif ec >= 3 and ec - 3 < len(E):
                    E[ec - 3]()
        if g + 1 < NG:
            for w in E[5:]:
                w()
            st["sxr"] = emit_split(cx, xr3, xrh3, xrl3)
            st["sxk"] = emit_split(cx, rr(B_sh), xkh3, xkl3)


def emit_attnb(cx, K, di, sp, xs3, onesc, onesr, ln4row, cc):
    te, ve, se, gp, sy = (cx.e["t"], cx.e["v"], cx.e["s"], cx.e["g"],
                          cx.e["y"])
    T = TG
    cx.mark()
    bproj = cx.al(NCH * S)

    def load_w2(dst, name, necol):
        ins = sy.dma_start(
            out=dst.rearrange("p (dc e) -> p dc e", dc=C),
            in_=di[name].rearrange("(dc p) e -> p dc e", p=128))
        return cx.ldw[0].inc(ins, 16)

    cx.mark()
    win_sb = cx.al(C * 2 * D)
    xn = cx.al(C * T)
    xsq = cx.al(C * T)
    row2 = cx.al(3 * T)
    rep = cx.al(2 * T)
    xp_g = cx.al(C * T)
    z_g = cx.al(C * T)
    wb_sb = cx.al(C * S)
    dtv = cx.al(C)
    dtm = cx.al(1)
    mskbuf2 = cx.al(8 * NCH * S // 2)
    rowc = cx.al(4)
    xsqc = cx.al(C)
    repc = cx.al(2)
    scr = dict(xsq=xsq, row2=row2[0:1, :], rep=rep, ones=onesc, onesr=onesr,
               bi_stats=2, bi_rep=3)
    scr2 = dict(bank=7, xsqc=xsqc, rowc=rowc[0:1, :], repc=repc,
                ones=onesc, onesr=onesr)

    sy.wait_ge(cx.pe.h, cx.pe.v)
    sy.wait_ge(cx.dv.h, cx.dv.v)
    sy.wait_ge(cx.ac.h, cx.ac.v)
    se.wait_ge(cx.pe.h, cx.pe.v)
    se.wait_ge(cx.dv.h, cx.dv.v)
    ve.wait_ge(cx.pe.h, cx.pe.v)
    ve.wait_ge(cx.ac.h, cx.ac.v)
    te.wait_ge(cx.dv.h, cx.dv.v)
    te.wait_ge(cx.ac.h, cx.ac.v)
    ins = sy.dma_start(
        out=bf(win_sb[:, 0:C * D]).rearrange(
            "p (two dc e) -> p two dc e", two=2, dc=C),
        in_=sp["hl_win0T"].rearrange("(dc p) two e -> p two dc e", p=128))
    cx.ldw[0].inc(ins, 16)
    ins = sy.dma_start(
        out=bf(win_sb[:, C * D:2 * C * D]).rearrange(
            "p (two dc e) -> p two dc e", two=2, dc=C),
        in_=sp["hl_win1T"].rearrange("(dc p) two e -> p two dc e", p=128))
    m_win = cx.ldw[0].inc(ins, 16)
    ins = sy.dma_start(out=wb_sb.rearrange("p (dc s) -> p dc s", dc=C),
                       in_=di["wbT"].rearrange("(dc p) s -> p dc s", p=128))
    m_wb = cx.ldc.inc(ins, 16)
    ins = se.activation(dtv, K["logdt"], AF.Exp)
    m_dt = cx.ac.inc(ins, 1)
    ve.wait_ge(cx.ac.h, m_dt)
    cx.ve_edge(ve.tensor_scalar_max(dtv, dtv, 1e-4))
    ins = ve.tensor_scalar_min(dtv, dtv, 1.0)
    m_dtc = cx.dv.inc(ins, 1)
    cx.ve_edge(ins)
    bank = cx.use_bank(7)
    te.wait_ge(cx.dv.h, m_dtc)
    ins = te.matmul(bank[0:1, 0:C], onesc, dtv, start=True, stop=True)
    m_pe = cx.pe.inc(ins, 1)
    ve.wait_ge(cx.pe.h, m_pe)
    ins = ve.reduce_sum(dtm[0:1, 0:1], bank[0:1, 0:C], axis=AX.X)
    m_d1 = cx.dv.inc(ins, 1)
    cx.bank_consumed(7, cx.dv, m_d1)
    cx.ve_edge(ins)
    ins = ve.tensor_scalar_mul(dtm[0:1, 0:1], dtm[0:1, 0:1], 1.0 / D)
    m_d2 = cx.dv.inc(ins, 1)
    cx.ve_edge(ins)
    bank = cx.use_bank(7)
    te.wait_ge(cx.dv.h, m_d2)
    ins = te.matmul(bank[:, 0:1], onesr, dtm[0:1, 0:1], start=True, stop=True)
    m_pe = cx.pe.inc(ins, 1)
    dtm_rep = cx.al(1)
    ve.wait_ge(cx.pe.h, m_pe)
    ins = ve.tensor_copy(dtm_rep, bank[:, 0:1])
    m_d3 = cx.dv.inc(ins, 1)
    cx.bank_consumed(7, cx.dv, m_d3)
    cx.ve_edge(ins)
    cx.ldc.wait(ve)
    ins = ve.tensor_scalar_mul(wb_sb, wb_sb, dtm_rep[:, 0:1])
    m_wbs = cx.dv.inc(ins, 1)
    cx.ve_edge(ins)

    xp3 = r3(xp_g)
    z3 = r3(z_g)
    wb3 = wb_sb.rearrange("p (dc s) -> p dc s", dc=C)
    bp3 = bproj.rearrange("p (k s) -> p k s", k=NCH)
    xnp = cx.al(C * T)
    pairb = [cx.al(C * T), cx.al(C * T)]
    win0p = bf(win_sb[:, 0:C * D]).rearrange(
        "p (two dc e) -> p two dc e", two=2, dc=C)
    win1p = bf(win_sb[:, C * D:2 * C * D]).rearrange(
        "p (two dc e) -> p two dc e", two=2, dc=C)
    p1st = {"stats2": None, "sq2": None, "split": [None, None]}

    def p1_chain_steps(g):
        xsrc3 = xs3[:, :, g * T:(g + 1) * T]
        ln1 = make_ln_stages(cx, T, xsrc3, r3(xnp), scr)
        ln2 = make_ln_stages(cx, T, r3(xnp), r3(xn), scr,
                             gamma=K["lng"], beta=K["lnb"])
        sl = g % 2
        xh = r3(bf(pairb[sl][:, 0:C * T // 2]))
        xl = r3(bf(pairb[sl][:, C * T // 2:C * T]))

        def s_sq1():
            ln1["sq"](p1st["stats2"])

        def s_mult1():
            if p1st["stats2"] is not None:
                ve.wait_ge(cx.pe.h, p1st["stats2"])
            if p1st["sq2"] is not None:
                ve.wait_ge(cx.ac.h, p1st["sq2"])
            ln1["E"][1]()

        def s_mult2_sq2():
            ln1["E"][2]()
            p1st["sq2"] = ln2["sq"](ln1["st"]["stats"],
                                    dv_guard=ln1["st"]["xn"])

        def s_stats2():
            ln2["stats"]()
            p1st["stats2"] = ln2["st"]["stats"]

        def s_split():
            p1st["split"][sl] = emit_split(cx, r3(xn), xh, xl)

        te_steps = {1: ln1["stats"], 3: ln1["repmm"], 7: s_stats2,
                    9: ln2["repmm"]}
        ve_steps = {0: s_sq1, 2: ln1["row"], 4: ln1["E"][0],
                    5: s_mult1, 6: s_mult2_sq2, 8: ln2["row"],
                    10: ln2["E"][0], 11: ln2["E"][1], 12: ln2["E"][2],
                    13: ln2["E"][3], 14: s_split}
        return te_steps, ve_steps

    te0, ve0 = p1_chain_steps(0)
    for i in range(16):
        if i in ve0 and i != 0:
            pass
    ve0[0]()
    te0[1]()
    ve0[2]()
    te0[3]()
    ve0[4]()
    ve0[5]()
    ve0[6]()
    te0[7]()
    ve0[8]()
    te0[9]()
    for i in (10, 11, 12, 13, 14):
        ve0[i]()

    for g in range(NGT):
        sl = g % 2
        xnh3 = r3(bf(pairb[sl][:, 0:C * T // 2]))
        xnl3 = r3(bf(pairb[sl][:, C * T // 2:C * T]))
        xsrc3 = xs3[:, :, g * T:(g + 1) * T]
        if cc in ("p1stop", "p1nobp"):
            gp.wait_ge(cx.dv.h, cx.dv.v)
            ins = gp.dma_start(
                out=sp["dbg_sp"].rearrange("(ec p) t -> p ec t", p=128)[
                    :, :, g * T:(g + 1) * T], in_=xsrc3)
            cx.spo.inc(ins, 16)
            ins = gp.dma_start(
                out=sp["q_sp"].rearrange("(ec p) t -> p ec t", p=128)[
                    :, :, g * T:(g + 1) * T], in_=r3(xnp))
            cx.spo.inc(ins, 16)
            cx.spo.wait(ve)
            cx.spo.wait(se)
            cx.spo.wait(te)
        nte, nve = (p1_chain_steps(g + 1) if g + 1 < NGT else ({}, {}))
        te.wait_ge(cx.dv.h, p1st["split"][sl])
        if g == 0:
            te.wait_ge(cx.ldw[0].h, m_win)
        m_xp_last = None
        for ec in range(16):
            wp4 = win0p if ec < 8 else win1p
            el = (ec % 8) * 128
            bank = cx.use_bank(4)
            first = True
            for dc in range(C):
                wh = wp4[:, 0, dc, el:el + 128]
                wl = wp4[:, 1, dc, el:el + 128]
                for pi, (wt, xt3) in enumerate(((wh, xnh3), (wh, xnl3),
                                               (wl, xnh3))):
                    ins = te.matmul(bank[:, 0:T], wt, xt3[:, dc, :],
                                    start=first,
                                    stop=(dc == C - 1 and pi == 2))
                    first = False
            m_pe = cx.pe.inc(ins, 1)
            if ec in nte:
                nte[ec]()
            if ec < 8:
                ve.wait_ge(cx.pe.h, m_pe)
                ins = ve.tensor_scalar_add(xp3[:, ec, :], bank[:, 0:T],
                                           K["b_in"][:, ec:ec + 1])
                m_ev = cx.dv.inc(ins, 1)
                m_xp_last = m_ev
                cx.bank_consumed(4, cx.dv, m_ev)
            else:
                se.wait_ge(cx.pe.h, m_pe)
                ins = se.activation(z3[:, ec - 8, :], bank[:, 0:T], AF.Silu,
                                    bias=K["b_in"][:, ec:ec + 1])
                m_ev = cx.ac.inc(ins, 1)
                cx.bank_consumed(4, cx.ac, m_ev)
            if ec in nve:
                nve[ec]()
        gp.wait_ge(cx.dv.h, cx.dv.v)
        gp.wait_ge(cx.ac.h, cx.ac.v)
        ins = gp.dma_start(
            out=sp["xp_sp"].rearrange("(ec p) t -> p ec t", p=128)[
                :, :, g * T:(g + 1) * T], in_=xp3)
        cx.sps[0].inc(ins, 16)
        ins = gp.dma_start(
            out=sp["z_sp"].rearrange("(ec p) t -> p ec t", p=128)[
                :, :, g * T:(g + 1) * T], in_=z3)
        cx.sps[1].inc(ins, 16)
        if cc == "p1nobp":
            continue
        if g == 0:
            ve.wait_ge(cx.dv.h, m_wbs)
            te.wait_ge(cx.dv.h, m_wbs)
        te.wait_ge(cx.dv.h, m_xp_last)
        for kt in range(T // 128):
            bank = cx.use_bank(5)
            for dc in range(C):
                ins = te.matmul(bank[:, 0:S],
                                xp3[:, dc, kt * 128:(kt + 1) * 128],
                                wb3[:, dc, :], start=(dc == 0),
                                stop=(dc == C - 1))
            m_pe = cx.pe.inc(ins, 1)
            ve.wait_ge(cx.pe.h, m_pe)
            ins = ve.tensor_copy(bp3[:, 16 + g * (T // 128) + kt, :],
                                 bank[:, 0:S])
            m_bp = cx.dv.inc(ins, 1)
            cx.bank_consumed(5, cx.dv, m_bp)
        cx.sps[0].wait(ve)
        cx.sps[1].wait(ve)
        cx.sps[0].wait(se)
        cx.sps[1].wait(se)
    cx.release()

    if cc in ("p1stop", "p1nobp"):
        return

    if cc:
        cx.mark()
        mskbuf2 = cx.al(8 * 256)
        emit_rs(cx, bproj[:, 256:512], 256, sp["rs2_in"], sp["rs2_out"],
                bproj[:, 0:256], K, mskbuf2)
        cx.release()
    else:
        ins = ve.memset(bproj[:, 0:256], 0.0)
        cx.dv.inc(ins, 1)

    cx.mark()
    xmt_sb = cx.al(S * 128)
    xmean_sb = cx.al(NCH // 2 * S)
    bprev = cx.al(NCH * S)
    cx.mark()
    abar = cx.al(C * S)
    labar = cx.al(C * S)
    a128 = cx.al(C * S)
    lnD = cx.al(1)
    ins = cx.e["v"].memset(lnD, -6.931471805599453)
    cx.dv.inc(ins, 1)
    p2buf = cx.al(S * 128)
    k1buf = cx.al(S * 128)
    tmp128b = [cx.al(128), cx.al(128)]
    hstate = cx.al(C * S)
    hbuf = cx.al(C * S * 16)
    cbuf = cx.al(C * S * NCH)
    ttbuf = cx.al(S * 128)
    am_sb = cx.al(S * 128)
    zrow = cx.al(128)

    ve.wait_ge(cx.dv.h, cx.dv.v)
    ve.wait_ge(cx.pe.h, cx.pe.v)
    ve.wait_ge(cx.ac.h, cx.ac.v)
    se.wait_ge(cx.pe.h, cx.pe.v)
    se.wait_ge(cx.dv.h, cx.dv.v)
    te.wait_ge(cx.dv.h, cx.dv.v)
    for half in range(1):
        bank = cx.use_bank(4)
        ins = te.matmul(bank[:, 0:512], K["arev"], bproj, start=True,
                        stop=True)
        m_rv = cx.pe.inc(ins, 1)
        ve.wait_ge(cx.pe.h, m_rv)
        ins = ve.tensor_copy(bprev, bank[:, 0:512])
        m_rvc = cx.dv.inc(ins, 1)
        cx.bank_consumed(4, cx.dv, m_rvc)
        cx.ve_edge(ins)
    bpr3 = bprev.rearrange("p (k s) -> p k s", k=NCH)
    al3 = K["alog"].rearrange("p (dc s) -> p dc s", dc=C)
    ab3 = abar.rearrange("p (dc s) -> p dc s", dc=C)
    ins = se.activation(abar, K["alog"], AF.Exp)
    m1 = cx.ac.inc(ins, 1)
    ve.wait_ge(cx.ac.h, m1)
    for dc in range(C):
        ins = ve.tensor_scalar_mul(ab3[:, dc, :], ab3[:, dc, :],
                                   dtv[:, dc:dc + 1])
    cx.ve_edge(ins)
    ins = ve.tensor_scalar_mul(abar, abar, -1.0)
    m2 = cx.dv.inc(ins, 1)
    se.wait_ge(cx.dv.h, m2)
    ins = se.activation(abar, abar, AF.Exp)
    m3 = cx.ac.inc(ins, 1)
    ve.wait_ge(cx.ac.h, m3)
    cx.ve_edge(ve.tensor_scalar_max(abar, abar, 1e-8))
    ins = ve.tensor_scalar_min(abar, abar, 1.0 - 1e-8)
    m4 = cx.dv.inc(ins, 1)
    se.wait_ge(cx.dv.h, m4)
    ins = se.activation(labar, abar, AF.Ln)
    m5 = cx.ac.inc(ins, 1)
    ins = se.activation(a128, labar, AF.Exp, scale=128.0)
    m6 = cx.ac.inc(ins, 1)
    ve.wait_ge(cx.ac.h, m6)

    lb3 = labar.rearrange("p (dc s) -> p dc s", dc=C)
    p23 = p2buf.rearrange("p (s t) -> p s t", s=S)
    k13 = k1buf.rearrange("p (s t) -> p s t", s=S)
    cb3 = cbuf.rearrange("p (dc s k) -> p dc s k", dc=C, s=S)
    hb3 = hbuf.rearrange("p (dc s k) -> p dc s k", dc=C, s=S)

    exp_hist = [None, None]
    for dc in range(C):
        ve.wait_ge(cx.pe.h, cx.pe.v)
        se.wait_ge(cx.pe.h, cx.pe.v)
        for s in range(S):
            tmp = tmp128b[s % 2]
            if exp_hist[s % 2] is not None:
                ve.wait_ge(cx.ac.h, exp_hist[s % 2])
            ins = ve.tensor_scalar_mul(tmp, K["iota_p"],
                                       lb3[:, dc, s:s + 1])
            mm1 = cx.dv.inc(ins, 1)
            se.wait_ge(cx.dv.h, mm1)
            ins = se.activation(p23[:, s, :], tmp, AF.Exp,
                                bias=lnD[:, 0:1])
            mm2 = cx.ac.inc(ins, 1)
            exp_hist[s % 2] = mm2
            bank_id = 6 if s % 2 == 0 else 4
            bank = cx.use_bank(bank_id)
            te.wait_ge(cx.ac.h, mm2)
            ins = te.transpose(bank[:, 0:128], p23[:, s, :], K["identc"])
            mm3 = cx.pe.inc(ins, 1)
            ve.wait_ge(cx.pe.h, mm3)
            ins = ve.tensor_copy(k13[:, s, :], bank[:, 0:128])
            mm4 = cx.dv.inc(ins, 1)
            cx.bank_consumed(bank_id, cx.dv, mm4)
        te.wait_ge(cx.dv.h, cx.dv.v)
        for s in range(S):
            bank_id = 7 if s % 2 == 0 else 5
            bank = cx.use_bank(bank_id)
            ins = te.matmul(bank[:, 0:NCH], k13[:, s, :], bpr3[:, :, s],
                            start=True, stop=True)
            mm5 = cx.pe.inc(ins, 1)
            ve.wait_ge(cx.pe.h, mm5)
            ins = ve.tensor_copy(cb3[:, dc, s, :], bank[:, 0:NCH])
            mm6 = cx.dv.inc(ins, 1)
            cx.bank_consumed(bank_id, cx.dv, mm6)

    ins = ve.memset(hstate, 0.0)
    m_h = cx.dv.inc(ins, 1)
    cx.ve_edge(ins)
    for k in range(1, NCH):
        cx.ve_edge(ve.tensor_mul(hstate, hstate, a128))
        ins = ve.tensor_tensor(
            hstate, hstate,
            cbuf.rearrange("p (g k) -> p g k", g=C * S)[:, :, k - 1], AL.add)
        cx.ve_edge(ins)
        if k >= 16:
            ins = ve.tensor_copy(
                hbuf.rearrange("p (g k) -> p g k", g=C * S)[:, :, k - 16],
                hstate)
    m_hc = cx.dv.inc(ins, 1)

    ins = ve.memset(am_sb, 0.0)
    cx.dv.inc(ins, 1)
    cx.ve_edge(ins)
    bank_i = cx.use_bank(3)
    exp_hist = [None, None]
    for dc in range(C):
        ve.wait_ge(cx.pe.h, cx.pe.v)
        se.wait_ge(cx.pe.h, cx.pe.v)
        for s in range(S):
            tmp = tmp128b[s % 2]
            if exp_hist[s % 2] is not None:
                ve.wait_ge(cx.ac.h, exp_hist[s % 2])
            ins = ve.tensor_scalar_mul(tmp, K["iota_t"],
                                       lb3[:, dc, s:s + 1])
            mm1 = cx.dv.inc(ins, 1)
            se.wait_ge(cx.dv.h, mm1)
            ins = se.activation(p23[:, s, :], tmp, AF.Exp)
            mm2 = cx.ac.inc(ins, 1)
            exp_hist[s % 2] = mm2
        te.wait_ge(cx.ac.h, cx.ac.v)
        if dc == 0:
            te.wait_ge(cx.dv.h, m_hc)
        for s in range(S):
            ins = te.matmul(bank_i[:, s * S:(s + 1) * S], p23[:, s, :],
                            hb3[:, dc, s, :],
                            start=(dc == 0 and s == 0), stop=False)
            cx.pe.inc(ins, 1)
        for s in range(S):
            bank_id = 6 if s % 2 == 0 else 7
            bank = cx.use_bank(bank_id)
            ins = te.matmul(bank[0:1, 0:128], onesc, p23[:, s, :],
                            start=True, stop=True)
            mm7 = cx.pe.inc(ins, 1)
            ve.wait_ge(cx.pe.h, mm7)
            ins = ve.tensor_tensor(am_sb[0:1, s * 128:(s + 1) * 128],
                                   am_sb[0:1, s * 128:(s + 1) * 128],
                                   bank[0:1, 0:128], AL.add)
            mm8 = cx.dv.inc(ins, 1)
            cx.bank_consumed(bank_id, cx.dv, mm8)

    cx.ve_edge(ve.tensor_scalar_mul(am_sb, am_sb, 1.0 / D))
    ins = ve.memset(zrow, 0.0)
    m_u = cx.dv.inc(ins, 1)
    cx.ve_edge(ins)
    gp.wait_ge(cx.dv.h, m_u)
    ins = gp.dma_start(out=sp["u_sp"][:, 0:127], in_=zrow[0:S, 0:127])
    cx.spo.inc(ins, 16)
    ins = gp.dma_start(out=sp["u_sp"][:, 255:256], in_=zrow[0:S, 0:1],
                       allow_slow_non_contiguous=True)
    cx.spv[0].inc(ins, 16)
    ins = gp.dma_start(
        out=sp["u_sp"][:, 127:255].unsqueeze(0),
        in_=am_sb[0:1, 0:S * 128].rearrange("o (s t) -> o s t", s=S))
    cx.spv[1].inc(ins, 16)
    cx.spo.wait(sy)
    cx.spv[0].wait(sy)
    cx.spv[1].wait(sy)
    tt_sems = [cx.ldc, cx.ldq, cx.ldw[0], cx.ldw[1]]
    for s in range(S):
        src_ap = bass.AP(tensor=sp["u_sp"].tensor, offset=s * 256,
                         ap=[[1, 128], [1, 128]])
        ins = sy.dma_start(out=ttbuf[:, s * 128:(s + 1) * 128], in_=src_ap)
        tt_sems[s % 4].inc(ins, 16)
    for sem in tt_sems:
        sem.wait(te)
    for s in range(S):
        ins = te.matmul(bank_i[:, s * S:(s + 1) * S],
                        ttbuf[:, s * 128:(s + 1) * 128],
                        bpr3[:, 16:32, s],
                        start=False, stop=(s == S - 1))
        m_ii = cx.pe.inc(ins, 1)
    ve.wait_ge(cx.pe.h, m_ii)
    ins = ve.tensor_copy(xmean_sb, bank_i[:, 0:256])
    m_xm = cx.dv.inc(ins, 1)
    cx.bank_consumed(3, cx.dv, m_xm)
    te.wait_ge(cx.dv.h, m_xm)
    xm2 = xmean_sb.rearrange("p (s k) -> p s k", s=S)
    for k in range(16):
        bank = cx.use_bank(6)
        ins = te.transpose(bank[0:S, 0:128], xm2[:, :, k], K["identc"])
        m_tk = cx.pe.inc(ins, 1)
        ve.wait_ge(cx.pe.h, m_tk)
        ins = ve.tensor_copy(xmt_sb[0:S, k * 128:(k + 1) * 128],
                             bank[0:S, 0:128])
        m_tkc = cx.dv.inc(ins, 1)
        cx.bank_consumed(6, cx.dv, m_tkc)
    cx.release()

    cx.mark()
    wout_sb = cx.al(C * D)
    wc_sb = cx.al(C * 128)
    xp_g2 = [cx.al(C * T), cx.al(C * T)]
    z_g2 = [cx.al(C * T), cx.al(C * T)]
    y_g = cx.al(C * T)
    xnp2 = [cx.al(C * T), cx.al(C * T)]
    ypair = [cx.al(C * T), cx.al(C * T)]
    xsq = cx.al(C * T)
    row2 = cx.al(3 * T)
    rep = cx.al(2 * T)
    scr = dict(xsq=xsq, row2=row2[0:1, :], rep=rep, ones=onesc, onesr=onesr,
               bi_stats=2, bi_rep=3)
    sy.wait_ge(cx.pe.h, cx.pe.v)
    sy.wait_ge(cx.dv.h, cx.dv.v)
    sy.wait_ge(cx.ac.h, cx.ac.v)
    ins = sy.dma_start(
        out=bf(wout_sb).rearrange("p (two dc e) -> p two dc e", two=2, dc=C),
        in_=sp["hl_woutT"].rearrange("(dc p) two e -> p two dc e", p=128))
    m_wo = cx.ldw[0].inc(ins, 16)
    ins = sy.dma_start(out=wc_sb[0:S, :],
                       in_=di["wcT"])
    m_wc = cx.ldc.inc(ins, 16)
    y3 = r3(y_g)
    p3st = {"stats": None, "yz": [None, None], "split": [None, None]}

    def p3_chain_steps(g, sl):
        xp3 = r3(xp_g2[sl])
        z3g = r3(z_g2[sl])
        xsrc3 = xs3[:, :, g * T:(g + 1) * T]
        ln = make_ln_stages(cx, T, xsrc3, r3(xnp2[sl]), scr)
        yh = r3(bf(ypair[sl][:, 0:C * T // 2]))
        yl = r3(bf(ypair[sl][:, C * T // 2:C * T]))
        h = {}

        def s_loads_sq():
            if p3st["yz"][sl] is not None:
                sy.wait_ge(cx.dv.h, p3st["yz"][sl])
            ins = sy.dma_start(
                out=xp3,
                in_=sp["xp_sp"].rearrange("(ec p) t -> p ec t", p=128)[
                    :, :, g * T:(g + 1) * T])
            h["mxp"] = cx.lds[0].inc(ins, 16)
            ins = sy.dma_start(
                out=z3g,
                in_=sp["z_sp"].rearrange("(ec p) t -> p ec t", p=128)[
                    :, :, g * T:(g + 1) * T])
            h["mz"] = cx.lds[1].inc(ins, 16)
            ln["sq"](p3st["stats"])

        def s_stats():
            ln["stats"]()
            p3st["stats"] = ln["st"]["stats"]

        def s_ymm(i):
            def go():
                for ec in (2 * i, 2 * i + 1):
                    bi = 4 if ec % 2 == 0 else 6
                    bank = cx.use_bank(bi)
                    for kt in range(T // 128):
                        k = g * (T // 128) + kt
                        ins = te.matmul(bank[:, kt * 128:(kt + 1) * 128],
                                        wc_sb[0:S, ec * 128:(ec + 1) * 128],
                                        xmt_sb[0:S, k * 128:(k + 1) * 128],
                                        start=(kt == 0),
                                        stop=(kt == T // 128 - 1))
                    h[("ymm", ec)] = cx.pe.inc(ins, 1)
            return go

        def s_yevac(i):
            def go():
                if i == 0:
                    ve.wait_ge(cx.lds[0].h, h["mxp"])
                ins = None
                for ec in (2 * i, 2 * i + 1):
                    bi = 4 if ec % 2 == 0 else 6
                    ve.wait_ge(cx.pe.h, h[("ymm", ec)])
                    ins = ve.scalar_tensor_tensor(
                        y3[:, ec, :], r3(xp_g2[sl])[:, ec, :],
                        K["dskip"][:, ec:ec + 1],
                        cx.pb[bi][:, 0:T], AL.mult, AL.add)
                    m_y = cx.dv.inc(ins, 1)
                    cx.bank_consumed(bi, cx.dv, m_y)
                if i == 3:
                    cx.ve_edge(ins)
            return go

        def s_yz():
            ve.wait_ge(cx.lds[1].h, h["mz"])
            ins = ve.tensor_tensor(y3, y3, z3g, AL.mult)
            p3st["yz"][sl] = cx.dv.inc(ins, 1)
            cx.ve_edge(ins)

        def s_split():
            p3st["split"][sl] = emit_split(cx, y3, yh, yl)

        def w2():
            ln["row"]()
            s_yevac(0)()

        def w4():
            ln["E"][0]()
            s_yevac(1)()

        def w5():
            ln["E"][1]()
            s_yevac(2)()

        def w6():
            ln["E"][2]()
            s_yevac(3)()
            s_yz()

        te_steps = {1: s_stats, 2: s_ymm(0), 3: ln["repmm"],
                    4: s_ymm(1), 5: s_ymm(2), 6: s_ymm(3)}
        ve_steps = {0: s_loads_sq, 2: w2, 4: w4, 5: w5, 6: w6,
                    7: s_split}
        return te_steps, ve_steps, s_split

    ve.wait_ge(cx.pe.h, cx.pe.v)
    ve.wait_ge(cx.ac.h, cx.ac.v)
    se.wait_ge(cx.pe.h, cx.pe.v)
    se.wait_ge(cx.dv.h, cx.dv.v)
    te.wait_ge(cx.dv.h, cx.dv.v)
    cx.ldc.wait(te)
    p3_order = [7] + list(range(7))
    rs3own = cx.al(C)
    rs3msk = cx.al(8 * C)
    rs3scr = dict(bank=7, xsqc=cx.al(C), rowc=cx.al(4)[0:1, :],
                  repc=cx.al(2), ones=onesc, onesr=onesr)
    m_rs3cc = None
    te0, ve0, split0 = p3_chain_steps(p3_order[0], 0)
    ve0[0]()
    te0[1]()
    te0[2]()
    ve0[2]()
    te0[3]()
    te0[4]()
    ve0[4]()
    te0[5]()
    ve0[5]()
    te0[6]()
    ve0[6]()
    ve0[7]()

    wo4 = bf(wout_sb).rearrange("p (two dc e) -> p two dc e", two=2, dc=C)
    for i, g in enumerate(p3_order):
        sl = i % 2
        xsrc3 = xs3[:, :, g * T:(g + 1) * T]
        yh3 = r3(bf(ypair[sl][:, 0:C * T // 2]))
        yl3 = r3(bf(ypair[sl][:, C * T // 2:C * T]))
        xnp3 = r3(xnp2[sl])
        if i + 1 < NGT:
            nte, nve, _ = p3_chain_steps(p3_order[i + 1], (i + 1) % 2)
        else:
            nte, nve = {}, {}
        te.wait_ge(cx.dv.h, p3st["split"][sl])
        if i == 0:
            te.wait_ge(cx.ldw[0].h, m_wo)
        for ec in range(C):
            bank = cx.use_bank(5)
            first = True
            for dc in range(C):
                wh = wo4[:, 0, dc, 128 * ec:128 * (ec + 1)]
                wl = wo4[:, 1, dc, 128 * ec:128 * (ec + 1)]
                for pi, (wt, xt3) in enumerate(((wh, yh3), (wh, yl3),
                                               (wl, yh3))):
                    ins = te.matmul(bank[:, 0:T], wt, xt3[:, dc, :],
                                    start=first,
                                    stop=(dc == C - 1 and pi == 2))
                    first = False
            m_pe = cx.pe.inc(ins, 1)
            if ec in nte:
                nte[ec]()
            ve.wait_ge(cx.pe.h, m_pe)
            ins = ve.scalar_tensor_tensor(
                xsrc3[:, ec, :], bank[:, 0:T], K["bout"][:, ec:ec + 1],
                xsrc3[:, ec, :], AL.add, AL.add)
            m_x4 = cx.dv.inc(ins, 1)
            cx.bank_consumed(5, cx.dv, m_x4)
            cx.ve_edge(ins)
            ins = ve.tensor_tensor(xsrc3[:, ec, :], xsrc3[:, ec, :],
                                   xnp3[:, ec, :], AL.add)
            cx.dv.inc(ins, 1)
            if ec in nve:
                nve[ec]()
        if i == 0 and cc is True:
            se.wait_ge(cx.dv.h, cx.dv.v)
            te.wait_ge(cx.dv.h, cx.dv.v)
            emit_row_ln(cx, xs3[:, :, NT - 1], rs3own, rs3scr)
            m_rs3cc = emit_rs_send(cx, rs3own, C, sp["rs3_in"],
                                   sp["rs3_out"], K, rs3msk)
    cx.release()
    cx.release()

    if cc is True:
        emit_rs_recv(cx, sp["rs3_out"], ln4row, K, m_rs3cc)
    else:
        ve.wait_ge(cx.ac.h, cx.ac.v)
        ve.wait_ge(cx.pe.h, cx.pe.v)
        ins = ve.memset(ln4row, 0.0)
        cx.dv.inc(ins, 1)
    cx.release()


def build_program(stop_after=None, cc=True, detect=False):
    nc = bass.Bass("TRN2", target_bir_lowering=False)
    nc.detect_race_conditions = detect
    cx = Ctx(nc)

    di = {}

    def inp(name, shape):
        di[name] = nc.dram_tensor(name, shape, F32, kind="ExternalInput").ap()
        return di[name]

    inp("x_res", [D, NT])
    inp("x_kv", [D, NK])
    for w in ["wqT", "wkT", "wvT", "wrT", "wk2T", "wv2T", "woutT"]:
        inp(w, [D, D])
    inp("winT", [D, 2 * D])
    inp("wbT", [D, S])
    inp("wcT", [S, D])
    for b in ["bq", "bk", "bv", "bout", "mu_r", "mu_k", "lng", "lnb",
              "dskip", "logdt"]:
        inp(b, [128, C])
    inp("b_in", [128, 2 * C])
    inp("alog", [128, C * S])
    inp("bias_cols", [128, NSTEPS])
    inp("m2d", [128, 256])
    inp("gate1", [128, 1])
    inp("slot8", [128, 8])
    inp("bv_row", [1, D])
    inp("iota_t", [128, 128])
    inp("iota_r", [128, 128])
    inp("iota_p", [128, 128])
    inp("arev", [128, 128])
    inp("identc", [128, 128])
    out_d = nc.dram_tensor("out_fm", [C, 128, NT], F32,
                           kind="ExternalOutput").ap()

    sp = {}
    for wname in ["wqT", "wkT", "wvT", "wrT", "wk2T", "wv2T", "woutT",
                  "win0T", "win1T"]:
        sp["hl_" + wname] = nc.dram_tensor("hl_" + wname, [D, 2, D],
                                           BF16).ap()
    sp["k_sp"] = nc.dram_tensor("k_sp", [D, NK], F32).ap()
    sp["v_sp"] = nc.dram_tensor("v_sp", [NK, D], F32).ap()
    sp["q_sp"] = nc.dram_tensor("q_sp", [D, NT], F32).ap()
    sp["xp_sp"] = nc.dram_tensor("xp_sp", [D, NT], F32).ap()
    sp["z_sp"] = nc.dram_tensor("z_sp", [D, NT], F32).ap()
    sp["u_sp"] = nc.dram_tensor("u_sp", [S, 256], F32).ap()
    sp["dbg_sp"] = nc.dram_tensor("dbg_sp", [D, NT], F32).ap()
    sp["rs1_in"] = nc.dram_tensor("rs1_in", [8, D], F32).ap()
    sp["rs1_out"] = nc.dram_tensor("rs1_out", [1, D], F32).ap()
    sp["rs2_in"] = nc.dram_tensor("rs2_in", [8, NT * S], F32).ap()
    sp["rs2_out"] = nc.dram_tensor("rs2_out", [1, NT * S], F32).ap()
    sp["rs3_in"] = nc.dram_tensor("rs3_in", [8, D], F32).ap()
    sp["rs3_out"] = nc.dram_tensor("rs3_out", [1, D], F32).ap()

    for nm in ["y", "g", "s", "v", "t"]:
        cx.e[nm] = REng()
    emit_all(cx, di, out_d, sp, stop_after, cc)

    for eng in cx.e.values():
        totals = {}
        out = []
        for c in eng.cmds:
            if c.name == "dma_start" and c.incs:
                sh, n = c.incs[0]
                prev = totals.get(id(sh), 0)
                if prev > 0:
                    w = RCmd("wait_ge", (sh, prev), {})
                    out.append(w)
                totals[id(sh)] = prev + n
            out.append(c)
        eng.cmds = out

    with nc.Block() as block:
        @block.sync
        def _(sy):
            cx.e["y"].replay(sy)

        @block.gpsimd
        def _(g):
            cx.e["g"].replay(g)

        @block.scalar
        def _(s):
            cx.e["s"].replay(s)

        @block.vector
        def _(v):
            cx.e["v"].replay(v)

        @block.tensor
        def _(t):
            cx.e["t"].replay(t)

    return nc


def emit_all(cx, di, out_d, sp, stop_after=None, cc=True):
    te, ve, se, gp, sy = (cx.e["t"], cx.e["v"], cx.e["s"], cx.e["g"],
                          cx.e["y"])

    xs = cx.al(C * NT)
    xs3 = r3(xs)
    K = {}
    for name in ["bq", "bk", "bv", "bout", "mu_r", "mu_k", "lng", "lnb",
                 "dskip", "logdt"]:
        K[name] = cx.al(C)
    K["b_in"] = cx.al(2 * C)
    K["alog"] = cx.al(C * S)
    K["gate1"] = cx.al(1)
    K["slot8"] = cx.al(8)
    K["iota_t"] = cx.al(128)
    K["iota_r"] = cx.al(128)
    K["iota_p"] = cx.al(128)
    K["arev"] = cx.al(128)
    K["identc"] = cx.al(128)
    onesc = cx.al(1)
    onesr_t = cx.al(128)
    ln2row = cx.al(C)
    ln4row = cx.al(C)

    for name in ["bq", "bk", "bv", "bout", "mu_r", "mu_k", "lng", "lnb",
                 "dskip", "logdt", "b_in", "alog", "gate1", "slot8",
                 "iota_t", "iota_r", "iota_p", "arev", "identc"]:
        ins = sy.dma_start(out=K[name], in_=di[name])
        cx.ldc.inc(ins, 16)
    cx.ldc.wait(ve)
    cx.ldc.wait(te)
    cx.ldc.wait(se)
    ve.memset(onesc, 1.0)
    ins = ve.memset(onesr_t[0:1, :], 1.0)
    m_ones = cx.dv.inc(ins, 1)
    onesr = onesr_t[0:1, :]
    te.wait_ge(cx.dv.h, m_ones)

    cx.mark()
    bvrep = cx.al(C * 128)
    wbuf = cx.al(C * D)
    wbuf2 = cx.al(C * D)
    stg = [cx.al(C * TG), cx.al(C * TG)]
    xn = cx.al(C * TG)
    pairq = [cx.al(C * TG), cx.al(C * TG)]
    row2 = cx.al(3 * TG)
    rep = cx.al(2 * TG)
    pout = [cx.al(C * TG), cx.al(C * TG)]
    vout = [cx.al(C * TG)]

    scrq = dict(xsq=xn, row2=row2[0:1, :], rep=rep, ones=onesc, onesr=onesr,
                bi_stats=2, bi_rep=3)

    def load_w(dst, name, eng, sem):
        ins = eng.dma_start(
            out=dst.rearrange("p (dc e) -> p dc e", dc=C),
            in_=di[name].rearrange("(dc p) e -> p dc e", p=128))
        return sem.inc(ins, 16)

    def load_w_hl(dst_words, name, eng, sem):
        ins = eng.dma_start(
            out=bf(dst_words).rearrange("p (two dc e) -> p two dc e", two=2,
                                        dc=C),
            in_=sp["hl_" + name].rearrange("(dc p) two e -> p two dc e",
                                           p=128))
        return sem.inc(ins, 16)

    def prep_src_ap(wname, ec):
        if wname == "win0T":
            base = di["winT"].rearrange("(dc p) e -> p dc e", p=128)[
                :, :, 0:D]
        elif wname == "win1T":
            base = di["winT"].rearrange("(dc p) e -> p dc e", p=128)[
                :, :, D:2 * D]
        else:
            base = di[wname].rearrange("(dc p) e -> p dc e", p=128)
        return base[:, :, 128 * ec:128 * (ec + 1)]

    def pair_view(words):
        return bf(words).rearrange("p (two dc e) -> p two dc e", two=2, dc=C)

    prep = {"war": {}, "m_out": {}, "m_dir": {}, "m_in": {},
            "pair_out": None}

    def prep_load(wname, ec, srcb, war=None):
        if war is not None:
            sy.wait_ge(cx.dv.h, war)
        prep["m_in"][(wname, ec)] = cx.dma(sy, cx.pin, r3(srcb),
                                           prep_src_ap(wname, ec))

    def prep_split(wname, ec, srcb, direct=None, pairb=None):
        ve.wait_ge(cx.pin.h, prep["m_in"].pop((wname, ec)))
        if direct is not None:
            pv = pair_view(direct)[:, :, :, 128 * ec:128 * (ec + 1)]
        else:
            if prep["pair_out"] is not None:
                ve.wait_ge(cx.pot.h, prep["pair_out"])
            pv = pair_view(pairb)
        ins = ve.tensor_copy(pv[:, 0], r3(srcb))
        cx.ve_edge(ins)
        ins = ve.tensor_tensor(pv[:, 1], r3(srcb), pv[:, 0], AL.subtract)
        m_spl = cx.dv.inc(ins, 1)
        if direct is None:
            gp.wait_ge(cx.dv.h, m_spl)
            m_out = cx.dma(
                gp, cx.pot,
                sp["hl_" + wname].rearrange(
                    "(dc p) two e -> p two dc e",
                    p=128)[:, :, :, 128 * ec:128 * (ec + 1)],
                pv)
            prep["pair_out"] = m_out
            prep["m_out"][wname] = m_out
        else:
            prep["m_dir"][wname] = m_spl
        return m_spl

    war_q = [None, None]
    for ec in range(C):
        srcb = pout[ec % 2][:, 0:1024]
        prep_load("wqT", ec, srcb, war=war_q[ec % 2])
        war_q[ec % 2] = prep_split("wqT", ec, srcb, direct=wbuf)
    m_wq = prep["m_dir"]["wqT"]

    qst = {"split": [None, None], "stg_free": [None, None],
           "m_ld": [None, None], "xn_free": None}

    def qkv_chain(g, src_name, sl, psl):
        src3 = r3(stg[sl])
        ln = make_ln_stages(cx, TG, src3, r3(xn), scrq)
        xh = r3(bf(pairq[psl][:, 0:C * TG // 2]))
        xl = r3(bf(pairq[psl][:, C * TG // 2:C * TG]))

        def s_load():
            if qst["stg_free"][sl] is not None:
                sy.wait_ge(cx.dv.h, qst["stg_free"][sl])
            ins = sy.dma_start(
                out=src3,
                in_=di[src_name].rearrange("(dc p) t -> p dc t", p=128)[
                    :, :, g * TG:(g + 1) * TG])
            qst["m_ld"][sl] = cx.lds[sl].inc(ins, 16)

        def s_sq():
            se.wait_ge(cx.lds[sl].h, qst["m_ld"][sl])
            if qst["xn_free"] is not None:
                se.wait_ge(cx.dv.h, qst["xn_free"])
            ln["sq"]()

        def s_stats():
            te.wait_ge(cx.lds[sl].h, qst["m_ld"][sl])
            ln["stats"]()

        def s_mult1():
            ve.wait_ge(cx.lds[sl].h, qst["m_ld"][sl])
            ln["E"][1]()

        def s_mult2():
            ln["E"][2]()
            qst["stg_free"][sl] = ln["st"]["xn"]

        def s_split():
            m = emit_split(cx, r3(xn), xh, xl)
            qst["split"][psl] = m
            qst["xn_free"] = m

        te_steps = {1: s_stats, 3: ln["repmm"]}
        ve_steps = {0: s_sq, 2: ln["row"], 4: ln["E"][0], 5: s_mult1,
                    6: s_mult2, 7: s_split}
        return s_load, te_steps, ve_steps

    def qkv_prologue(g, src_name, sl, psl):
        s_load, tec, vec = qkv_chain(g, src_name, sl, psl)
        s_load()
        vec[0]()
        tec[1]()
        vec[2]()
        tec[3]()
        for i in (4, 5, 6, 7):
            vec[i]()

    def s1_load(t, war):
        wname = "wkT" if t % 2 == 0 else "wvT"
        prep_load(wname, t // 2, vout[0][:, (t % 2) * 1024:(t % 2 + 1) * 1024],
                  war=war)

    def s1_split(t):
        wname = "wkT" if t % 2 == 0 else "wvT"
        srcb = vout[0][:, (t % 2) * 1024:(t % 2 + 1) * 1024]
        if wname == "wkT":
            return prep_split(wname, t // 2, srcb, pairb=bvrep)
        return prep_split(wname, t // 2, srcb, direct=wbuf2)

    s1_war = [None, None]
    qkv_prologue(0, "x_res", 0, 0)
    for g in range(NGT):
        psl = g % 2
        xh3 = r3(bf(pairq[psl][:, 0:C * TG // 2]))
        xl3 = r3(bf(pairq[psl][:, C * TG // 2:C * TG]))
        s1_load(2 * g, s1_war[0])
        s1_load(2 * g + 1, s1_war[1])
        if g + 1 < NGT:
            s_loadn, nte, nve = qkv_chain(g + 1, "x_res", (g + 1) % 2,
                                          (g + 1) % 2)
            s_loadn()
        else:
            nte, nve = {}, {}
        sl = g % 2
        po = pout[sl]
        cx.sps[sl].wait(ve)
        te.wait_ge(cx.dv.h, m_wq)
        te.wait_ge(cx.dv.h, qst["split"][psl])
        emit_proj3(cx, TG, bf(wbuf), (xh3, xl3),
                   lambda ec: r3(po)[:, ec, :], 4, bias=K["bq"],
                   hooks_te=nte, hooks_ve=nve)
        gp.wait_ge(cx.dv.h, cx.dv.v)
        ins = gp.dma_start(
            out=sp["q_sp"].rearrange("(ec p) t -> p ec t", p=128)[
                :, :, g * TG:(g + 1) * TG],
            in_=r3(po))
        cx.sps[sl].inc(ins, 16)
        s1_war[0] = s1_split(2 * g)
        s1_war[1] = s1_split(2 * g + 1)

    sy.wait_ge(cx.pe.h, cx.pe.v)
    sy.wait_ge(cx.pot.h, prep["m_out"]["wkT"])
    load_w_hl(wbuf, "wkT", sy, cx.ldw[0])
    ins = sy.dma_start(out=bvrep,
                       in_=di["bv_row"][0:1, :].broadcast_to([128, D]))
    cx.ldc.inc(ins, 16)
    cx.ldc.wait(ve)
    qkv_prologue(0, "x_kv", 0, 0)
    for g in range(NGK):
        psl = g % 2
        xh3 = r3(bf(pairq[psl][:, 0:C * TG // 2]))
        xl3 = r3(bf(pairq[psl][:, C * TG // 2:C * TG]))
        if g + 1 < NGK:
            s_loadn, nte, nve = qkv_chain(g + 1, "x_kv", (g + 1) % 2,
                                          (g + 1) % 2)
            s_loadn()
        else:
            nte, nve = {}, {}
        sl = g % 2
        po = pout[sl]
        vo = vout[0]
        cx.sps[sl].wait(ve)
        cx.ldw[0].wait(te)
        te.wait_ge(cx.dv.h, qst["split"][psl])
        emit_proj3(cx, TG, bf(wbuf), (xh3, xl3),
                   lambda ec: r3(po)[:, ec, :], 4, bias=K["bk"],
                   hooks_te=nte, hooks_ve=nve)
        gp.wait_ge(cx.dv.h, cx.dv.v)
        ins = gp.dma_start(
            out=sp["k_sp"].rearrange("(ec p) t -> p ec t", p=128)[
                :, :, g * TG:(g + 1) * TG],
            in_=r3(po))
        cx.sps[sl].inc(ins, 16)
        te.wait_ge(cx.dv.h, prep["m_dir"]["wvT"])
        cx.spv[0].wait(ve)
        wv4 = bf(wbuf2).rearrange("p (two dc e) -> p two dc e", two=2, dc=C)
        xh3b = xh3
        xl3b = xl3
        vo3 = vo.rearrange("p (kt e) -> p kt e", kt=TG // 128)
        m_vv = None
        for kt in range(TG // 128):
            for half in range(2):
                bank = cx.use_bank(5)
                first = True
                for dc in range(C):
                    xh_c = xh3b[:, dc, kt * 128:(kt + 1) * 128]
                    xl_c = xl3b[:, dc, kt * 128:(kt + 1) * 128]
                    wh_c = wv4[:, 0, dc, half * 512:(half + 1) * 512]
                    wl_c = wv4[:, 1, dc, half * 512:(half + 1) * 512]
                    for (xt, wt) in ((xh_c, wh_c), (xh_c, wl_c),
                                     (xl_c, wh_c)):
                        last = (dc == C - 1) and (xt is xl_c)
                        ins = te.matmul(bank[:, 0:512], xt, wt,
                                        start=first, stop=last)
                        first = False
                m_pe = cx.pe.inc(ins, 1)
                ve.wait_ge(cx.pe.h, m_pe)
                ins = ve.tensor_tensor(
                    vo3[:, kt, half * 512:(half + 1) * 512], bank[:, 0:512],
                    bvrep[:, half * 512:(half + 1) * 512], AL.add)
                m_vv = cx.dv.inc(ins, 1)
                cx.bank_consumed(5, cx.dv, m_vv)
        gp.wait_ge(cx.dv.h, m_vv)
        ins = gp.dma_start(
            out=sp["v_sp"][g * TG:(g + 1) * TG, :].rearrange(
                "(kt p) e -> p kt e", p=128),
            in_=vo3)
        cx.spv[0].inc(ins, 16)
    cx.release()

    if stop_after in ("dump_q", "dump_k", "dump_v"):
        finish_stub(cx, out_d,
                    dump=sp[{"dump_q": "q_sp", "dump_k": "k_sp",
                             "dump_v": "v_sp"}[stop_after]])
        return
    if stop_after == "qkv":
        finish_stub(cx, out_d, zero=True)
        return

    for eng in (sy, ve, se, gp):
        for s in cx.sps + cx.spv:
            s.wait(eng)
    cx.mark()
    qblk = cx.al(C * 512)
    oblk = cx.al(QB * C * 128)
    ktb = [cx.al(C * 128), cx.al(C * 128)]
    vtb = [cx.al(C * 128), cx.al(C * 128)]
    p_sb = cx.al(128)
    pt_sb = cx.al(128)
    mstat = cx.al(QB * 2)
    lstat = cx.al(QB)
    colA = cx.al(1)
    colB = cx.al(1)
    colC = cx.al(1)
    colD = cx.al(1)
    bias_sb = cx.al(NSTEPS)
    m2d_sb = cx.al(256)
    p_src = cx.al(2 * 1024)
    p_pair = cx.al(2 * 1024)

    ins = sy.dma_start(out=bias_sb, in_=di["bias_cols"])
    cx.ldc.inc(ins, 16)
    ins = sy.dma_start(out=m2d_sb, in_=di["m2d"])
    cx.ldc.inc(ins, 16)

    s2_list = [(w, ec) for w in ["wrT", "wk2T", "wv2T", "win0T", "win1T",
                                 "woutT"] for ec in range(C)]
    s2 = {"load": 0, "split": 0, "war": [None, None]}

    def s2_pump():
        if s2["load"] < len(s2_list) and s2["load"] - s2["split"] < 2:
            t = s2["load"]
            w, ec = s2_list[t]
            prep_load(w, ec, p_src[:, (t % 2) * 1024:(t % 2 + 1) * 1024],
                      war=s2["war"][t % 2])
            s2["load"] += 1
        elif s2["split"] < s2["load"]:
            t = s2["split"]
            w, ec = s2_list[t]
            s2["war"][t % 2] = prep_split(
                w, ec, p_src[:, (t % 2) * 1024:(t % 2 + 1) * 1024],
                pairb=p_pair[:, (t % 2) * 1024:(t % 2 + 1) * 1024])
            s2["split"] += 1
    oblk3 = oblk.rearrange("p (i e) -> p i e", i=QB)
    cx.ldc.wait(ve)
    cx.ldc.wait(se)

    step_idx = 0
    pipe_state = {"mpp": None}
    nb_run = 1 if stop_after in ("attn1b", "attn1b_noret") else NB
    for bq in range(nb_run):
        nslot = 20 + 4 * bq
        sy.wait_ge(cx.pe.h, cx.pe.v)
        ins = sy.dma_start(
            out=r3(qblk),
            in_=sp["q_sp"].rearrange("(ec p) t -> p ec t", p=128)[
                :, :, bq * 512:(bq + 1) * 512])
        cx.ldq.inc(ins, 16)
        if bq == 0:
            ve.wait_ge(cx.pe.h, cx.pe.v)
            ve.wait_ge(cx.ac.h, cx.ac.v)
        ve.memset(oblk, 0.0)
        ve.memset(mstat, -1e30)
        ins = ve.memset(lstat, 0.0)
        m_init = cx.dv.inc(ins, 1)
        cx.ldq.wait(te)
        te.wait_ge(cx.dv.h, m_init)
        q3 = r3(qblk)
        kv_guard = {}
        tile_par = [0] * QB
        pending = []

        def emit_rest(st):
            (j2, ti2, special, bi_s, vta2) = st[:5]
            bank_s = cx.pb[bi_s]
            par = tile_par[ti2]
            mc_prev = mstat[:, ti2 * 2 + par:ti2 * 2 + par + 1]
            mc_new = mstat[:, ti2 * 2 + 1 - par:ti2 * 2 + 2 - par]
            tile_par[ti2] = 1 - par
            sidx = st[5]
            ve.wait_ge(cx.pe.h, st[6])
            if special:
                if pipe_state["mpp"] is not None:
                    ve.wait_ge(cx.pe.h, pipe_state["mpp"])
                msl = (m2d_sb[:, 0:128] if special == 1
                       else m2d_sb[:, 128:256])
                cx.ve_edge(ve.tensor_tensor(p_sb, bank_s[:, 0:128], msl,
                                            AL.add))
                srcb = p_sb
            else:
                srcb = bank_s[:, 0:128]
            cx.ve_edge(ve.reduce_max(colA, srcb, axis=AX.X))
            cx.ve_edge(ve.scalar_tensor_tensor(
                colA, colA, SCL, bias_sb[:, sidx:sidx + 1],
                AL.mult, AL.add))
            cx.ve_edge(ve.tensor_tensor(mc_new, mc_prev, colA, AL.max))
            ins = ve.tensor_sub(colB, bias_sb[:, sidx:sidx + 1], mc_new)
            m_mn = cx.dv.inc(ins, 1)
            cx.ve_edge(ins)
            se.wait_ge(cx.dv.h, m_mn)
            ins = se.activation(colC, mc_new, AF.Exp, bias=mc_prev,
                                scale=-1.0)
            if not special and pipe_state["mpp"] is not None:
                se.wait_ge(cx.pe.h, pipe_state["mpp"])
            ins = se.activation(p_sb, srcb, AF.Exp, bias=colB, scale=SCL)
            m_p = cx.ac.inc(ins, 1)
            cx.bank_consumed(bi_s, cx.ac, m_p)
            ve.wait_ge(cx.ac.h, m_p)
            cx.ve_edge(ve.reduce_sum(colD, p_sb, axis=AX.X))
            ins = ve.scalar_tensor_tensor(
                lstat[:, ti2:ti2 + 1], lstat[:, ti2:ti2 + 1], colC, colD,
                AL.mult, AL.add)
            m_l = cx.dv.inc(ins, 1)
            cx.ve_edge(ins)
            bank_t = cx.use_bank(2)
            te.wait_ge(cx.ac.h, m_p)
            ins = te.transpose(bank_t[:, 0:128], p_sb, K["identc"])
            m_t = cx.pe.inc(ins, 1)
            pipe_state["mpp"] = m_t
            ve.wait_ge(cx.pe.h, m_t)
            ins = ve.tensor_copy(pt_sb, bank_t[:, 0:128])
            m_ptc = cx.dv.inc(ins, 1)
            cx.bank_consumed(2, cx.dv, m_ptc)
            bank_a = cx.use_bank(3)
            bank_b = cx.use_bank(4)
            te.wait_ge(cx.dv.h, m_ptc)
            te.matmul(bank_a[:, 0:512], pt_sb, vta2[:, 0:512],
                      start=True, stop=True)
            ins = te.matmul(bank_b[:, 0:512], pt_sb, vta2[:, 512:1024],
                            start=True, stop=True)
            m_av = cx.pe.inc(ins, 1)
            kv_guard[j2] = m_av
            ve.wait_ge(cx.pe.h, m_av)
            ve.scalar_tensor_tensor(
                oblk3[:, ti2, 0:512], oblk3[:, ti2, 0:512], colC,
                bank_a[:, 0:512], AL.mult, AL.add)
            ins = ve.scalar_tensor_tensor(
                oblk3[:, ti2, 512:1024], oblk3[:, ti2, 512:1024], colC,
                bank_b[:, 0:512], AL.mult, AL.add)
            m_o = cx.dv.inc(ins, 1)
            cx.bank_consumed(3, cx.dv, m_o)
            cx.bank_consumed(4, cx.dv, m_o)

        for j in range(nslot):
            sl = j % 2
            kta, vta = ktb[sl], vtb[sl]
            while pending and pending[0][0] <= j - 2:
                emit_rest(pending.pop(0))
            if (j - 2) in kv_guard:
                sy.wait_ge(cx.pe.h, kv_guard[j - 2])
            cx.dma(sy, cx.ldk[sl], r3(kta),
                   sp["k_sp"].rearrange("(ec p) t -> p ec t", p=128)[
                       :, :, j * 128:(j + 1) * 128])
            m_kv = cx.dma(sy, cx.ldk[sl], vta,
                          sp["v_sp"][j * 128:(j + 1) * 128, :])
            s2_pump()
            cx.ldk[sl].wait(te)
            k3 = r3(kta)
            for ti in range(QB):
                i = bq * QB + ti
                if j > 16 + i:
                    continue
                special = 1 if j == i else (2 if j == 16 + i else 0)
                bi_s = step_idx % 2
                bank_s = cx.use_bank(bi_s)
                for dc in range(C):
                    ins = te.matmul(
                        bank_s[:, 0:128],
                        q3[:, dc, ti * 128:(ti + 1) * 128], k3[:, dc, :],
                        start=(dc == 0), stop=(dc == C - 1))
                m_s = cx.pe.inc(ins, 1)
                pending.append((j, ti, special, bi_s, vta, step_idx, m_s))
                step_idx += 1
                if len(pending) > 1:
                    emit_rest(pending.pop(0))
        while pending:
            emit_rest(pending.pop(0))
        if bq == nb_run - 1:
            while s2["split"] < len(s2_list):
                s2_pump()
        for ti in range(QB):
            i = bq * QB + ti
            cx.ve_edge(ve.reciprocal(colA, lstat[:, ti:ti + 1]))
            ins = ve.tensor_scalar_mul(oblk3[:, ti, :], oblk3[:, ti, :], colA)
            m_on = cx.dv.inc(ins, 1)
            cx.ve_edge(ins)
            te.wait_ge(cx.dv.h, m_on)
            for dc in range(C):
                bank = cx.use_bank(6)
                ins = te.transpose(bank[:, 0:128],
                                   oblk3[:, ti, dc * 128:(dc + 1) * 128],
                                   K["identc"])
                m_tt = cx.pe.inc(ins, 1)
                ve.wait_ge(cx.pe.h, m_tt)
                ins = ve.tensor_copy(xs3[:, dc, i * 128:(i + 1) * 128],
                                     bank[:, 0:128])
                m_xc = cx.dv.inc(ins, 1)
                cx.bank_consumed(6, cx.dv, m_xc)
    assert stop_after in ("attn1b", "attn1b_noret", "attn4b_noadd") or step_idx == NSTEPS

    if stop_after in ("attn1b", "attn1b_noret", "attn4b_noadd"):
        finish_stub(cx, out_d, zero=False)
        return

    m_rs1cc = None
    if cc is True:
        rs1x = cx.al(C)
        rs1t = cx.al(C)
        rs1own = cx.al(C)
        rs1msk = cx.al(8 * C)
        rs1scr = dict(bank=7, xsqc=cx.al(C), rowc=cx.al(4)[0:1, :],
                      repc=cx.al(2), ones=onesc, onesr=onesr)
        ins = sy.dma_start(
            out=rs1x.unsqueeze(2),
            in_=di["x_res"].rearrange("(dc p) t -> p dc t", p=128)[
                :, :, NT - 1:NT],
            allow_slow_non_contiguous=True)
        m_r1x = cx.ldc.inc(ins, 16)
        ve.wait_ge(cx.ldc.h, m_r1x)
        ve.wait_ge(cx.dv.h, cx.dv.v)
        ins = ve.tensor_tensor(rs1t, xs3[:, :, NT - 1], rs1x, AL.add)
        m_r1t = cx.dv.inc(ins, 1)
        cx.ve_edge(ins)
        se.wait_ge(cx.dv.h, m_r1t)
        te.wait_ge(cx.dv.h, m_r1t)
        emit_row_ln(cx, rs1t, rs1own, rs1scr)
        m_rs1cc = emit_rs_send(cx, rs1own, C, sp["rs1_in"], sp["rs1_out"],
                               K, rs1msk)

    res_war = [None, None]
    for g in range(NGT):
        sl = g % 2
        if res_war[sl] is not None:
            sy.wait_ge(cx.dv.h, res_war[sl])
        elif g == 0:
            sy.wait_ge(cx.dv.h, cx.dv.v)
        ins = sy.dma_start(
            out=r3(qblk)[:, :, sl * TG:(sl + 1) * TG],
            in_=di["x_res"].rearrange("(dc p) t -> p dc t", p=128)[
                :, :, g * TG:(g + 1) * TG])
        m_ld = cx.ldq.inc(ins, 16)
        ve.wait_ge(cx.ldq.h, m_ld)
        ins = ve.tensor_tensor(xs3[:, :, g * TG:(g + 1) * TG],
                               xs3[:, :, g * TG:(g + 1) * TG],
                               r3(qblk)[:, :, sl * TG:(sl + 1) * TG], AL.add)
        res_war[sl] = cx.dv.inc(ins, 1)
    cx.release()

    if stop_after == "attn":
        finish_stub(cx, out_d)
        return

    TF = 256
    cx.mark()
    wr_sb = cx.al(C * D)
    wk2_sb = cx.al(C * D)
    wv2_sb = cx.al(C * D)
    reg = {}
    for nm in ["xn", "xnsh", "diff", "xr", "xk"]:
        reg[nm] = cx.al(C * TF)
    reg["rb"] = reg["xn"]
    reg["kb"] = reg["xnsh"]
    reg["xsq"] = reg["xr"]
    f_row2 = cx.al(3 * TF)
    f_rep = reg["xnsh"][:, 0:2 * TF]
    reg["prevc"] = cx.al(C)
    mskbuf = cx.al(8 * C)
    rowc = cx.al(4)
    xsqc = cx.al(C)
    repc = cx.al(2)
    reg["scr"] = dict(xsq=reg["xsq"], row2=f_row2[0:1, :], rep=f_rep,
                      ones=onesc, onesr=onesr, bi_stats=2, bi_rep=3)
    scr2 = dict(bank=7, xsqc=xsqc, rowc=rowc[0:1, :], repc=repc,
                ones=onesc, onesr=onesr)
    sy.wait_ge(cx.pe.h, cx.pe.v)
    sy.wait_ge(cx.dv.h, cx.dv.v)
    sy.wait_ge(cx.ac.h, cx.ac.v)
    sy.wait_ge(cx.pot.h, cx.pot.v)
    m_fw0 = load_w_hl(wr_sb, "wrT", sy, cx.ldw[0])
    m_fw1 = load_w_hl(wk2_sb, "wk2T", sy, cx.ldw[1])
    load_w_hl(wv2_sb, "wv2T", sy, cx.ldw[0])
    ve.wait_ge(cx.ac.h, cx.ac.v)
    ve.wait_ge(cx.pe.h, cx.pe.v)
    se.wait_ge(cx.dv.h, cx.dv.v)
    se.wait_ge(cx.pe.h, cx.pe.v)
    if cc is True:
        emit_rs_recv(cx, sp["rs1_out"], ln2row, K, m_rs1cc)
    else:
        ins = ve.memset(ln2row, 0.0)
        cx.dv.inc(ins, 1)
    cx.ldw[0].wait(te)
    cx.ldw[1].wait(te)
    emit_ffn(cx, K, wr_sb, wk2_sb, wv2_sb, xs3, ln2row, reg, T=TF)
    cx.release()

    if stop_after == "ffn1":
        finish_stub(cx, out_d)
        return

    emit_attnb(cx, K, di, sp, xs3, onesc, onesr, ln4row,
               ("p1stop" if stop_after == "ab_p1" else
                ("p1nobp" if stop_after == "ab_p1nobp" else cc)))

    if stop_after in ("ab_p1", "ab_p1nobp"):
        finish_stub(cx, out_d, zero=True)
        return

    if stop_after == "attnb":
        finish_stub(cx, out_d)
        return

    TF = 256
    cx.mark()
    wrb = cx.al(C * D)
    wkb = cx.al(C * D)
    wvb = cx.al(C * D)
    reg = {}
    for nm in ["xn", "xnsh", "diff", "xr", "xk"]:
        reg[nm] = cx.al(C * TF)
    reg["rb"] = reg["xn"]
    reg["kb"] = reg["xnsh"]
    reg["xsq"] = reg["xr"]
    f_row2 = cx.al(3 * TF)
    f_rep = reg["xnsh"][:, 0:2 * TF]
    reg["prevc"] = cx.al(C)
    reg["scr"] = dict(xsq=reg["xsq"], row2=f_row2[0:1, :], rep=f_rep,
                      ones=onesc, onesr=onesr, bi_stats=2, bi_rep=3)
    sy.wait_ge(cx.pe.h, cx.pe.v)
    sy.wait_ge(cx.dv.h, cx.dv.v)
    sy.wait_ge(cx.ac.h, cx.ac.v)
    ve.wait_ge(cx.pe.h, cx.pe.v)
    ve.wait_ge(cx.ac.h, cx.ac.v)
    se.wait_ge(cx.pe.h, cx.pe.v)
    se.wait_ge(cx.dv.h, cx.dv.v)
    load_w_hl(wrb, "wrT", sy, cx.ldw[0])
    load_w_hl(wkb, "wk2T", sy, cx.ldw[1])
    load_w_hl(wvb, "wv2T", sy, cx.ldw[0])
    cx.ldw[0].wait(te)
    cx.ldw[1].wait(te)
    emit_ffn(cx, K, wrb, wkb, wvb, xs3, ln4row, reg, T=TF)
    cx.release()

    finish_stub(cx, out_d)


def finish_stub(cx, out_d, zero=False, dump=None):
    gp = cx.e["g"]
    ve = cx.e["v"]
    xs = cx.arena.ap()[:, 0:C * NT]
    if dump is not None:
        gp.wait_ge(cx.dv.h, cx.dv.v)
        gp.wait_ge(cx.ac.h, cx.ac.v)
        for s in (cx.sps + cx.spv + [cx.spo]):
            s.wait(gp)
        flat_out = out_d.rearrange("c p t -> (c p t)")
        ins = gp.dma_start(out=flat_out,
                           in_=dump.rearrange("a b -> (a b)")[0:C * 128 * NT])
        cx.spo.inc(ins, 16)
        cx.spo.wait(gp)
        return
    if zero:
        ins = ve.memset(xs, 0.0)
        cx.dv.inc(ins, 1)
    gp.wait_ge(cx.dv.h, cx.dv.v)
    gp.wait_ge(cx.ac.h, cx.ac.v)
    ins = gp.dma_start(out=out_d.rearrange("c p t -> p c t"), in_=r3(xs))
    cx.spo.inc(ins, 16)
    for s in ([cx.ldc, cx.ldq, cx.spo] + cx.lds + cx.ldw + cx.ldk +
              cx.sps + cx.spv):
        s.wait(gp)



def host_inputs(full, core):
    b, h = core // 2, core % 2
    X = np.asarray(full["L0_attn_a_X"])
    XF = np.ascontiguousarray(X[b].T)
    d = {}
    d["x_kv"] = XF
    d["x_res"] = np.ascontiguousarray(XF[:, 2048 * h:2048 * h + 2048])
    d["wqT"] = np.ascontiguousarray(np.asarray(full["L0_attn_a_Wq"]).T)
    d["wkT"] = np.ascontiguousarray(np.asarray(full["L0_attn_a_Wk"]).T)
    d["wvT"] = np.ascontiguousarray(np.asarray(full["L0_attn_a_Wv"]).T)
    d["wrT"] = np.ascontiguousarray(np.asarray(full["L0_ffn_W_r"]).T)
    d["wk2T"] = np.ascontiguousarray(np.asarray(full["L0_ffn_W_k"]).T)
    d["wv2T"] = np.ascontiguousarray(np.asarray(full["L0_ffn_W_v"]).T)
    d["woutT"] = np.ascontiguousarray(np.asarray(full["L0_attn_b_W_out"]).T)
    d["winT"] = np.ascontiguousarray(np.asarray(full["L0_attn_b_W_in"]).T)
    d["wbT"] = np.ascontiguousarray(np.asarray(full["L0_attn_b_W_B"]).T)
    d["wcT"] = np.ascontiguousarray(np.asarray(full["L0_attn_b_W_C"]).T)

    def fm(v):
        return np.ascontiguousarray(np.asarray(v).reshape(C, 128).T)

    d["bv_row"] = np.asarray(full["L0_attn_a_bv"]).reshape(1, D)
    d["bq"] = fm(full["L0_attn_a_bq"])
    d["bk"] = fm(full["L0_attn_a_bk"])
    d["bv"] = fm(full["L0_attn_a_bv"])
    d["bout"] = fm(full["L0_attn_b_b_out"])
    d["mu_r"] = fm(full["L0_ffn_mu_r"])
    d["mu_k"] = fm(full["L0_ffn_mu_k"])
    d["lng"] = fm(full["L0_attn_b_ln_g"])
    d["lnb"] = fm(full["L0_attn_b_ln_b"])
    d["dskip"] = fm(full["L0_attn_b_D_skip"])
    d["logdt"] = fm(full["L0_attn_b_log_dt"])
    d["b_in"] = np.ascontiguousarray(
        np.asarray(full["L0_attn_b_b_in"]).reshape(2 * C, 128).T)
    al = np.asarray(full["L0_attn_b_A_log"]).reshape(C, 128, S)
    d["alog"] = np.ascontiguousarray(al.transpose(1, 0, 2).reshape(128, C * S))

    bias_cols = np.zeros((128, NSTEPS), np.float32)
    for sidx, (bq_, i, j) in enumerate(ATT_STEPS):
        g = 16 * h + i
        if j == i or j == 16 + i:
            val = 0.0
        elif j > g:
            val = -BIGB
        else:
            val = 0.0
        bias_cols[:, sidx] = val
    d["bias_cols"] = bias_cols
    diag = np.where(np.arange(128)[:, None] >= np.arange(128)[None, :],
                    0.0, -BIG2).astype(np.float32)
    m2d = np.zeros((128, 256), np.float32)
    if h == 0:
        m2d[:, 0:128] = diag
        m2d[:, 128:256] = -BIG2
    else:
        m2d[:, 128:256] = diag
    d["m2d"] = m2d
    d["gate1"] = np.full((128, 1), float(h), np.float32)
    slot8 = np.zeros((128, 8), np.float32)
    slot8[:, core ^ 1] = 1.0
    d["slot8"] = slot8
    d["iota_t"] = np.broadcast_to(
        np.arange(128, dtype=np.float32)[None, :], (128, 128)).copy()
    d["iota_r"] = np.broadcast_to(
        (128.0 - np.arange(128, dtype=np.float32))[None, :], (128, 128)).copy()
    d["iota_p"] = np.broadcast_to(
        (1.0 + np.arange(128, dtype=np.float32))[None, :], (128, 128)).copy()
    d["arev"] = np.eye(128, dtype=np.float32)[::-1].copy()
    d["identc"] = np.eye(128, dtype=np.float32)
    return {k: np.ascontiguousarray(v, dtype=np.float32) for k, v in d.items()}


def kernel(**inputs):
    nc = build_program()
    in_maps = [host_inputs(inputs, c) for c in range(8)]
    res = run_bass_kernel_spmd(nc, in_maps, core_ids=list(range(8)))
    out = np.empty((4, 4096, 1024), np.float32)
    for c in range(8):
        b, h = c // 2, c % 2
        o = np.asarray(res.results[c]["out_fm"])
        out[b, 2048 * h:2048 * h + 2048, :] = o.reshape(1024, 2048).T
    return out
